# version 08
# speedup vs baseline: 1.3746x; 1.0060x over previous
import contextlib
import numpy as np
import ml_dtypes

import concourse.bass as bass
import concourse.mybir as mybir
from concourse.bass_utils import run_bass_kernel_spmd
from concourse.alu_op_type import AluOpType

F32 = mybir.dt.float32
BF16 = mybir.dt.bfloat16
U32 = mybir.dt.uint32
AF = mybir.ActivationFunctionType
AX = mybir.AxisListType.X
ALU = AluOpType

D = 1024
NSEQ = 8192
NB = 4
NCORE = 8
NQ = 4096
NBLK = 8
QB = 512
KC = 1024
EXP = 8
HID = 64
DT_STEP = 0.1
LN_EPS = 1e-5
SCL = 1.0 / 32.0

A1T = 512
F1T = 256
F2T = 512


class Sem:
    def __init__(self, h):
        self.h = h
        self.n = 0

    def inc(self, instr, d=1):
        instr.then_inc(self.h, d)
        self.n += d
        return self.n


def bcast2(ap, n):
    return bass.AP(ap.tensor, ap.offset,
                   [list(ap.ap[0]), [0, n], list(ap.ap[1])])


class Plan:

    def __init__(self, sems):
        self.sems = sems
        self.ev = {}
        self.emitting = False
        self.shadow = {k: s.n for k, s in sems.items()}

    def start_emit(self):
        self.emitting = True
        for k, s in self.sems.items():
            assert self.shadow[k] >= s.n

    def e(self, key, semname, instr, d=1):
        s = self.sems[semname]
        if not self.emitting:
            self.shadow[semname] += d
            self.ev[key] = (semname, self.shadow[semname])
            return None
        if d:
            s.inc(instr, d)
        assert (semname, s.n) == self.ev[key], (key, s.n, self.ev[key])
        return instr

    def w(self, eng, key):
        if not self.emitting:
            return
        semname, cnt = self.ev[key]
        eng.wait_ge(self.sems[semname].h, cnt)


class _Null:

    def __getattr__(self, k):
        return lambda *a, **kw: None



def emit_A1(nc, E):
    sLD, sWL, sST, sMM, sSC, sVE = E["sLD"], E["sWL"], E["sST"], E["sMM"], E["sSC"], E["sVE"]
    d_xT, d_xqT, d_wqkv = E["d_xT"], E["d_xqT"], E["d_wqkv"]
    d_kTs, d_Vs, d_qTs = E["d_kTs"], E["d_Vs"], E["d_qTs"]

    NT_KV = NSEQ // A1T
    NT_Q = NQ // A1T
    jobs = [("kv", i) for i in range(NT_KV)] + [("q", i) for i in range(NT_Q)]
    ngrp = [16 if k == "kv" else 8 for k, _ in jobs]
    gbase = np.cumsum([0] + ngrp).tolist()
    NG = gbase[-1]

    sP = [E["sP0"], E["sP1"]]
    sSK, sSV = E["sS0"], E["sS1"]
    bWL, bMM, bSC, bVE = sWL.n, sMM.n, sSC.n, sVE.n
    bP = [sP[0].n, sP[1].n]
    bSK, bSV = sSK.n, sSV.n
    ld_done = [(sP[j % 2], bP[j % 2] + 16 * (j // 2 + 1)) for j in range(len(jobs))]
    cast_done = [bVE + j + 1 for j in range(len(jobs))]
    mm_done = [bMM + q + 1 for q in range(NG)]
    ev_done = [bSC + q + 1 for q in range(NG)]
    stK_done = [bSK + 16 * (j + 1) for j in range(len(jobs))]
    stV_done = [bSV + 16 * (j + 1) for j in range(NT_KV)]

    with (
        nc.sbuf_tensor("a1_w", [128, 8, 3 * D], BF16) as w,
        nc.sbuf_tensor("a1_x", [128, 2, 8, A1T], F32) as xt,
        nc.sbuf_tensor("a1_xb", [128, 2, 8, A1T], BF16) as xtb,
        nc.sbuf_tensor("a1_evK", [128, 8, A1T], BF16) as evK,
        nc.sbuf_tensor("a1_evV", [128, 4, D], BF16) as evV,
        nc.psum_tensor("a1_p0", [128, A1T], F32) as p0,
        nc.psum_tensor("a1_p1", [128, A1T], F32) as p1,
        nc.psum_tensor("a1_p2", [128, A1T], F32) as p2,
        nc.psum_tensor("a1_p3", [128, A1T], F32) as p3,
        nc.Block("A1") as blk,
    ):
        ps = [p0, p1, p2, p3]

        @blk.sync
        def _(sync):
            for j, (kind, i) in enumerate(jobs):
                if j >= 2:
                    sync.wait_ge(sVE.h, cast_done[j - 2])
                src = d_xT if kind == "kv" else d_xqT
                t0 = i * A1T
                ap = src.rearrange("(c p) t -> p c t", p=128)[:, :, t0:t0 + A1T]
                n = ld_done[j][0].inc(sync.dma_start(xt[:, j % 2], ap), 16)
                assert n == ld_done[j][1]

        @blk.vector
        def _(v):
            for j in range(len(jobs)):
                v.wait_ge(ld_done[j][0].h, ld_done[j][1])
                if j >= 2:
                    v.wait_ge(sMM.h, mm_done[gbase[j - 1] - 1])
                n = sVE.inc(v.tensor_copy(xtb[:, j % 2], xt[:, j % 2]))
                assert n == cast_done[j]

        @blk.tensor
        def _(t):
            t.wait_ge(sWL.h, bWL + 16)
            for j, (kind, i) in enumerate(jobs):
                t.wait_ge(sVE.h, cast_done[j])
                xb = xtb[:, j % 2]
                for grp in range(ngrp[j]):
                    q = gbase[j] + grp
                    if q >= 4:
                        t.wait_ge(sSC.h, ev_done[q - 4])
                    pb = ps[q % 4]
                    last = None
                    if kind == "q" or grp < 8:
                        col0 = (0 if kind == "q" else D) + grp * 128
                        for c in range(8):
                            last = t.matmul(pb[:, :], w[:, c, col0:col0 + 128],
                                            xb[:, c, :], start=(c == 0), stop=(c == 7))
                    else:
                        gg = grp - 8
                        ts0 = (gg // 2) * 128
                        dv0 = 2 * D + (gg % 2) * 512
                        for c in range(8):
                            last = t.matmul(pb[:, :], xb[:, c, ts0:ts0 + 128],
                                            w[:, c, dv0:dv0 + 512],
                                            start=(c == 0), stop=(c == 7))
                    n = sMM.inc(last)
                    assert n == mm_done[q]

        @blk.scalar
        def _(s):
            for j, (kind, i) in enumerate(jobs):
                for grp in range(ngrp[j]):
                    q = gbase[j] + grp
                    s.wait_ge(sMM.h, mm_done[q])
                    if grp == 0 and j > 0:
                        s.wait_ge(sSK.h, stK_done[j - 1])
                    if grp == 8 and j > 0:
                        s.wait_ge(sSV.h, stV_done[j - 1])
                    if kind == "q" or grp < 8:
                        dst = evK[:, grp, :]
                    else:
                        gg = grp - 8
                        dst = evV[:, gg // 2, (gg % 2) * 512:(gg % 2) * 512 + 512]
                    n = sSC.inc(s.activation(dst, ps[q % 4][:, :], AF.Copy))
                    assert n == ev_done[q]

        @blk.gpsimd
        def _(gp):
            n = sWL.inc(gp.dma_start(w[:, :, :],
                                     d_wqkv.rearrange("(c p) n -> p c n", p=128)), 16)
            assert n == bWL + 16
            for j, (kind, i) in enumerate(jobs):
                t0 = i * A1T
                if kind == "kv":
                    gp.wait_ge(sSC.h, ev_done[gbase[j] + 7])
                    ap = d_kTs.rearrange("(m p) t -> p m t", p=128)[:, :, t0:t0 + A1T]
                    n = sSK.inc(gp.dma_start(ap, evK[:, :, :]), 16)
                    assert n == stK_done[j]
                    gp.wait_ge(sSC.h, ev_done[gbase[j] + 15])
                    ap = d_Vs.rearrange("(g p) d -> p g d", p=128)[:, 4 * i:4 * i + 4, :]
                    n = sSV.inc(gp.dma_start(ap, evV[:, :, :]), 16)
                    assert n == stV_done[j]
                else:
                    gp.wait_ge(sSC.h, ev_done[gbase[j] + 7])
                    ap = d_qTs.rearrange("(m p) t -> p m t", p=128)[:, :, t0:t0 + A1T]
                    n = sSK.inc(gp.dma_start(ap, evK[:, :, :]), 16)
                    assert n == stK_done[j]
            gp.wait_ge(sSK.h, stK_done[-1])
            gp.wait_ge(sSV.h, stV_done[-1])



def emit_A2(nc, E):
    sLD, sWL, sST, sMM, sSC, sVE = E["sLD"], E["sWL"], E["sST"], E["sMM"], E["sSC"], E["sVE"]
    d_qTs, d_kTs, d_Vs, d_xqT, d_x1Ts, d_cmask = (
        E["d_qTs"], E["d_kTs"], E["d_Vs"], E["d_xqT"], E["d_x1Ts"], E["d_cmask"])
    c_oc, c_or32 = E["c_oc"], E["c_or32"]

    slots = list(range(NBLK))
    chunk_list = [(u, j) for u in slots for j in range(u + 1)]
    cidx = {uj: i for i, uj in enumerate(chunk_list)}
    sub_list = [(u, j, s) for (u, j) in chunk_list for s in range(8)]
    tidx = {ujs: i for i, ujs in enumerate(sub_list)}

    sPq = [E["sP0"], E["sP1"]]
    sPx = [E["sP2"], E["sP3"]]
    sPk = [E["sP4"], E["sP5"]]
    sPv = [E["sP6"], E["sP7"]]
    sSx = [E["sS0"], E["sS1"]]
    ld_q, ld_x, ld_k, ld_v = {}, {}, {}, {}
    bq = [sPq[0].n, sPq[1].n]; bx = [sPx[0].n, sPx[1].n]
    bk = [sPk[0].n, sPk[1].n]; bv = [sPv[0].n, sPv[1].n]
    for u in slots:
        ld_q[u] = (sPq[u % 2], bq[u % 2] + 16 * (u // 2 + 1))
        ld_x[u] = (sPx[u % 2], bx[u % 2] + 16 * (u // 2 + 1))
    for jj, (u, j) in enumerate(chunk_list):
        ld_k[(u, j)] = (sPk[jj % 2], bk[jj % 2] + 16 * (jj // 2 + 1))
        ld_v[(u, j)] = (sPv[jj % 2], bv[jj % 2] + 16 * (jj // 2 + 1))

    mask_ld = sWL.n + 16

    exp_done, sq_done, ln_done, ex2_done = {}, {}, {}, {}
    c = sSC.n
    for u in slots:
        for j in range(u + 1):
            for s in range(8):
                c += 1; exp_done[(u, j, s)] = c
        c += 1; sq_done[u] = c
        c += 1; ln_done[u] = c
        c += 1; ex2_done[u] = c
    sc_end = c

    msk_done, av_ev_done = {}, {}
    rec_done, comb_done, cast_done, stats_done, fin_done = {}, {}, {}, {}, {}
    c = sVE.n
    c += 1
    ms_done = c
    for u in slots:
        for j in range(u + 1):
            if j == u:
                for s in range(8):
                    c += 1; msk_done[(u, s)] = c
            for w in range(4):
                c += 1; av_ev_done[(u, j, w)] = c
        c += 1; rec_done[u] = c
        c += 1; comb_done[u] = c
        c += 1; cast_done[u] = c
        c += 1; stats_done[u] = c
        c += 1; fin_done[u] = c
    ve_end = c

    sc_grp_done, av_w_done, dn_done = {}, {}, {}
    bc1_done, s1_done, s2_done, bc2_done, bc3_done = {}, {}, {}, {}, {}
    c = sMM.n
    for u in slots:
        for j in range(u + 1):
            for s in range(8):
                c += 1; sc_grp_done[(u, j, s)] = c
            if j == u:
                c += 1; dn_done[u] = c
            for w in range(4):
                c += 1; av_w_done[(u, j, w)] = c
        c += 1; bc1_done[u] = c
        c += 1; s1_done[u] = c
        c += 1; s2_done[u] = c
        c += 1; bc2_done[u] = c
        c += 1; bc3_done[u] = c
    mm_end = c

    st_done = {}
    bs = [sSx[0].n, sSx[1].n]
    for u in slots:
        st_done[u] = (sSx[u % 2], bs[u % 2] + 16 * (u // 2 + 1))

    with (
        nc.sbuf_tensor("a2_qT", [128, 2, 8, QB], BF16) as qT,
        nc.sbuf_tensor("a2_kc", [128, 2, 8, KC], BF16) as kc,
        nc.sbuf_tensor("a2_vc", [128, 2, 8, KC], BF16) as vc,
        nc.sbuf_tensor("a2_PT", [128, 8, QB], BF16) as PT,
        nc.sbuf_tensor("a2_oa", [128, 8, QB], F32) as oa,
        nc.sbuf_tensor("a2_msk", [128, 8, QB], BF16) as msk,
        nc.sbuf_tensor("a2_xr", [128, 2, 8, QB], F32) as xr,
        nc.sbuf_tensor("a2_rT", [128, 8, QB], F32) as rT,
        nc.sbuf_tensor("a2_rTb", [128, 8, QB], BF16) as rTb,
        nc.sbuf_tensor("a2_r2b", [128, 8, QB], BF16) as r2b,
        nc.sbuf_tensor("a2_st", [128, 4, QB], F32) as st,
        nc.sbuf_tensor("a2_tmp", [128, QB], F32) as tmp,
        nc.psum_tensor("a2_pS0", [128, QB], F32) as pS0,
        nc.psum_tensor("a2_pS1", [128, QB], F32) as pS1,
        nc.psum_tensor("a2_pDN", [128, QB], F32) as pDN,
        nc.psum_tensor("a2_pAVa", [128, 2, QB], F32) as pAVa,
        nc.psum_tensor("a2_pAVb", [128, 2, QB], F32) as pAVb,
        nc.psum_tensor("a2_pEP", [128, QB], F32) as pEP,
        nc.Block("A2") as blk,
    ):
        pS = [pS0, pS1]
        pAV = [pAVa, pAVb]

        def pt_final(u, j, s):
            if j == u:
                return (sVE.h, msk_done[(u, s)])
            return (sSC.h, exp_done[(u, j, s)])

        @blk.sync
        def _(sync):
            for u in slots:
                if u >= 2:
                    sync.wait_ge(sVE.h, fin_done[u - 2])
                    sync.wait_ge(st_done[u - 2][0].h, st_done[u - 2][1])
                ap = d_qTs.rearrange("(m p) t -> p m t", p=128)[:, :, u * QB:(u + 1) * QB]
                n = ld_q[u][0].inc(sync.dma_start(qT[:, u % 2], ap), 16)
                assert n == ld_q[u][1]
                ap = d_xqT.rearrange("(m p) t -> p m t", p=128)[:, :, u * QB:(u + 1) * QB]
                n = ld_x[u][0].inc(sync.dma_start(xr[:, u % 2], ap), 16)
                assert n == ld_x[u][1]
                for j in range(u + 1):
                    jj = cidx[(u, j)]
                    if jj >= 2:
                        sync.wait_ge(sMM.h, av_w_done[chunk_list[jj - 2] + (3,)])
                    ap = d_kTs.rearrange("(m p) t -> p m t", p=128)[:, :, j * KC:(j + 1) * KC]
                    n = ld_k[(u, j)][0].inc(sync.dma_start(kc[:, jj % 2], ap), 16)
                    assert n == ld_k[(u, j)][1]
                    ap = d_Vs.rearrange("(g p) d -> p g d", p=128)[:, 8 * j:8 * j + 8, :]
                    n = ld_v[(u, j)][0].inc(sync.dma_start(vc[:, jj % 2], ap), 16)
                    assert n == ld_v[(u, j)][1]

        @blk.tensor
        def _(t):
            t.wait_ge(sWL.h, mask_ld)
            t.wait_ge(E["sAux"].h, 80)
            for u in slots:
                t.wait_ge(ld_q[u][0].h, ld_q[u][1])
                for j in range(u + 1):
                    jj = cidx[(u, j)]
                    t.wait_ge(ld_k[(u, j)][0].h, ld_k[(u, j)][1])
                    t.wait_ge(ld_v[(u, j)][0].h, ld_v[(u, j)][1])
                    kcb, vcb = kc[:, jj % 2], vc[:, jj % 2]
                    for s in range(8):
                        ti = tidx[(u, j, s)]
                        if ti >= 2:
                            t.wait_ge(sSC.h, exp_done[sub_list[ti - 2]])
                        pb = pS[s % 2]
                        last = None
                        for cc in range(8):
                            last = t.matmul(pb[:, :], kcb[:, cc, 128 * s:128 * s + 128],
                                            qT[:, u % 2, cc, :],
                                            start=(cc == 0), stop=(cc == 7))
                        n = sMM.inc(last); assert n == sc_grp_done[(u, j, s)]
                    for s in range(8):
                        sem, cnt = pt_final(u, j, s)
                        t.wait_ge(sem, cnt)
                        first = (j == 0 and s == 0)
                        if first and u >= 1:
                            t.wait_ge(sVE.h, fin_done[u - 1])
                        lastmm = (j == u and s == 7)
                        ins = t.matmul(pDN[:, :], c_oc[:, :], PT[:, s, :],
                                       start=first, stop=lastmm)
                        if lastmm:
                            n = sMM.inc(ins); assert n == dn_done[u]
                    for w in range(4):
                        gw = 4 * jj + w
                        if gw >= 2:
                            pj, pw = divmod(gw - 2, 4)
                            t.wait_ge(sVE.h, av_ev_done[chunk_list[pj] + (pw,)])
                        pv = pAV[w % 2]
                        for mi in range(2):
                            dvc = 2 * w + mi
                            last = None
                            for s in range(8):
                                last = t.matmul(pv[:, mi, :],
                                                vcb[:, s, 128 * dvc:128 * dvc + 128],
                                                PT[:, s, :],
                                                start=(s == 0), stop=(s == 7))
                        n = sMM.inc(last); assert n == av_w_done[(u, j, w)]
                t.wait_ge(sVE.h, rec_done[u])
                ins = t.matmul(pEP[:, :], c_or32[:, :], st[:, 0, :], start=True, stop=True)
                n = sMM.inc(ins); assert n == bc1_done[u]
                t.wait_ge(sVE.h, cast_done[u])
                last = None
                for cc in range(8):
                    last = t.matmul(pEP[:, :], c_oc[:, :], rTb[:, cc, :],
                                    start=(cc == 0), stop=(cc == 7))
                n = sMM.inc(last); assert n == s1_done[u]
                t.wait_ge(sSC.h, sq_done[u])
                last = None
                for cc in range(8):
                    last = t.matmul(pDN[:, :], c_oc[:, :], r2b[:, cc, :],
                                    start=(cc == 0), stop=(cc == 7))
                n = sMM.inc(last); assert n == s2_done[u]
                t.wait_ge(sVE.h, stats_done[u])
                ins = t.matmul(pEP[:, :], c_or32[:, :], st[:, 3, :], start=True, stop=True)
                n = sMM.inc(ins); assert n == bc2_done[u]
                t.wait_ge(sSC.h, ex2_done[u])
                ins = t.matmul(pDN[:, :], c_or32[:, :], st[:, 2, :], start=True, stop=True)
                n = sMM.inc(ins); assert n == bc3_done[u]

        @blk.scalar
        def _(s_):
            s_.wait_ge(E["sAux"].h, 80)
            for u in slots:
                for j in range(u + 1):
                    jj = cidx[(u, j)]
                    for s in range(8):
                        s_.wait_ge(sMM.h, sc_grp_done[(u, j, s)])
                        if jj >= 1:
                            s_.wait_ge(sMM.h, av_w_done[chunk_list[jj - 1] + (3,)])
                        ins = s_.activation(PT[:, s, :], pS[s % 2][:, :], AF.Exp, scale=SCL)
                        n = sSC.inc(ins); assert n == exp_done[(u, j, s)]
                s_.wait_ge(sVE.h, comb_done[u])
                ins = s_.activation(r2b[:, :, :], rT[:, :, :], AF.Square)
                n = sSC.inc(ins); assert n == sq_done[u]
                s_.wait_ge(sVE.h, stats_done[u])
                ins = s_.activation(tmp[0:1, :], st[0:1, 1, :], AF.Ln, bias=LN_EPS)
                n = sSC.inc(ins); assert n == ln_done[u]
                ins = s_.activation(st[0:1, 2, :], tmp[0:1, :], AF.Exp, scale=-0.5)
                n = sSC.inc(ins); assert n == ex2_done[u]

        @blk.vector
        def _(v):
            n = sVE.inc(v.memset(st[:, :, :], 0.0)); assert n == ms_done
            for u in slots:
                for j in range(u + 1):
                    jj = cidx[(u, j)]
                    if j == u:
                        for s in range(8):
                            v.wait_ge(sSC.h, exp_done[(u, j, s)])
                            ins = v.tensor_tensor(PT[:, s, :], PT[:, s, :], msk[:, s, :],
                                                  ALU.mult)
                            n = sVE.inc(ins); assert n == msk_done[(u, s)]
                    for w in range(4):
                        v.wait_ge(sMM.h, av_w_done[(u, j, w)])
                        pv = pAV[w % 2]
                        ins = None
                        for mi in range(2):
                            dvc = 2 * w + mi
                            if j == 0:
                                ins = v.tensor_copy(oa[:, dvc, :], pv[:, mi, :])
                            else:
                                ins = v.tensor_tensor(oa[:, dvc, :], pv[:, mi, :],
                                                      oa[:, dvc, :], ALU.add)
                        n = sVE.inc(ins); assert n == av_ev_done[(u, j, w)]
                v.wait_ge(sMM.h, dn_done[u])
                ins = v.reciprocal(st[0:1, 0, :], pDN[0:1, :])
                n = sVE.inc(ins); assert n == rec_done[u]
                v.wait_ge(sMM.h, bc1_done[u])
                v.wait_ge(ld_x[u][0].h, ld_x[u][1])
                ins = None
                for cc in range(8):
                    v.tensor_tensor(rT[:, cc, :], oa[:, cc, :], pEP[:, :], ALU.mult)
                    ins = v.tensor_tensor(rT[:, cc, :], rT[:, cc, :],
                                          xr[:, u % 2, cc, :], ALU.add)
                n = sVE.inc(ins); assert n == comb_done[u]
                ins = v.tensor_copy(rTb[:, :, :], rT[:, :, :])
                n = sVE.inc(ins); assert n == cast_done[u]
                v.wait_ge(sMM.h, s1_done[u])
                v.wait_ge(sMM.h, s2_done[u])
                v.tensor_scalar(st[0:1, 3, :], pEP[0:1, :], -1.0 / D, None,
                                ALU.mult, ALU.bypass)
                v.tensor_scalar(st[0:1, 1, :], pEP[0:1, :], 1.0 / D, None,
                                ALU.mult, ALU.bypass)
                v.tensor_tensor(tmp[0:1, :], st[0:1, 1, :], st[0:1, 1, :], ALU.mult)
                v.tensor_scalar(st[0:1, 1, :], pDN[0:1, :], 1.0 / D, None,
                                ALU.mult, ALU.bypass)
                ins = v.tensor_tensor(st[0:1, 1, :], st[0:1, 1, :], tmp[0:1, :],
                                      ALU.subtract)
                n = sVE.inc(ins); assert n == stats_done[u]
                v.wait_ge(sMM.h, bc2_done[u])
                v.wait_ge(sMM.h, bc3_done[u])
                ins = None
                for cc in range(8):
                    v.tensor_tensor(tmp[:, :], rT[:, cc, :], pEP[:, :], ALU.add)
                    ins = v.tensor_tensor(xr[:, u % 2, cc, :], tmp[:, :], pDN[:, :],
                                          ALU.mult)
                n = sVE.inc(ins); assert n == fin_done[u]

        @blk.gpsimd
        def _(gp):
            n = sWL.inc(gp.dma_start(msk[:, :, :],
                                     d_cmask.rearrange("s p j -> p s j")), 16)
            assert n == mask_ld
            for u in slots:
                gp.wait_ge(sVE.h, fin_done[u])
                ap = d_x1Ts.rearrange("(m p) t -> p m t", p=128)[:, :, u * QB:(u + 1) * QB]
                n = st_done[u][0].inc(gp.dma_start(ap, xr[:, u % 2]), 16)
                assert n == st_done[u][1]
            gp.wait_ge(st_done[slots[-2]][0].h, st_done[slots[-2]][1])
            gp.wait_ge(st_done[slots[-1]][0].h, st_done[slots[-1]][1])


F1_NT_OVERRIDE = None


def emit_F1(nc, E):
    sems = {k: E[k] for k in ("sLD", "sWL", "sST", "sMM", "sSC", "sVE", "sVX", "sAux",
                              "sP0", "sP1", "sS0")}
    sLD, sWL, sST, sMM, sSC, sVE, sVX = (E[k] for k in
                                         ("sLD", "sWL", "sST", "sMM", "sSC", "sVE", "sVX"))
    c_id, c_oc, c_or32, c_ordt32, c_pair, c_iota = (
        E["c_id"], E["c_oc"], E["c_or32"], E["c_ordt32"], E["c_pair"], E["c_iota"])
    d_x1Ts, d_intTs = E["d_x1Ts"], E["d_intTs"]

    NT = F1_NT_OVERRIDE or (NQ // F1T)
    T = F1T

    with contextlib.ExitStack() as es:
        ec = es.enter_context
        sb = lambda nm, shp, dt: ec(nc.sbuf_tensor(nm, shp, dt))
        w_r1 = sb("f1_wr1", [128, 8, 256], BF16)
        w_r2 = sb("f1_wr2", [128, 2, 128], BF16)
        w_h1 = sb("f1_wh1", [128, 8, 256], BF16)
        w_h2 = sb("f1_wh2", [128, 2, 128], BF16)
        w_e1 = sb("f1_we1", [128, 16, 512], BF16)
        w_e2 = sb("f1_we2", [128, 4, 3072], BF16)
        w_eb2 = sb("f1_web2", [128, 3072], BF16)
        w_s1 = sb("f1_ws1", [128, 16, 128], BF16)
        w_s2 = sb("f1_ws2", [128, 3072], BF16)
        b_r1 = sb("f1_br1", [128, 2], F32)
        b_r2 = sb("f1_br2", [128, 1], F32)
        b_h1 = sb("f1_bh1", [128, 2], F32)
        b_h2 = sb("f1_bh2", [1, 1], F32)
        b_e1 = sb("f1_be1", [128, 4], F32)
        b_s1 = sb("f1_bs1", [128, 1], F32)
        b_s2 = sb("f1_bs2", [128, 24], F32)
        b_mun = sb("f1_bmun", [128, 8], F32)
        c_shw = sb("f1_shw", [128, 1], F32)
        sw0 = sb("f1_sw0", [128, 1], F32)
        sw1c = sb("f1_sw1", [128, 1], F32)
        x1 = sb("f1_x1", [128, 2, 8, T], F32)
        intT = sb("f1_int", [128, 8, T], F32)
        vT = sb("f1_v", [128, 8, T], F32)
        intTb = sb("f1_intb", [128, 2, 8, T], BF16)
        vTb = sb("f1_vb", [128, 8, T], BF16)
        xm = sb("f1_xm", [128, 8, T], F32)
        xms = sb("f1_xms", [128, 8, T], F32)
        b_muns = sb("f1_bmuns", [128, 8], F32)
        spq = sb("f1_spq", [128, 3, 8, T], F32)
        sgA = sb("f1_sga", [128, 2, 6, T], F32)
        sgG = sb("f1_sgg", [128, 2, 6, T], F32)
        hEb = sb("f1_heb", [128, 4, T], BF16)
        hm1 = sb("f1_hm1", [128, 4, T], BF16)
        hshb = sb("f1_hshb", [128, T], BF16)
        hHb = sb("f1_hhb", [128, 2, T], BF16)
        hr = sb("f1_hr", [128, 2, T], BF16)
        lg = sb("f1_lg", [128, T], F32)
        lgt = sb("f1_lgt", [128, 2, 128], F32)
        mx = sb("f1_mx", [128, 2, 8], F32)
        mx2 = sb("f1_mx2", [128, 2, 8], F32)
        ixu = sb("f1_ixu", [128, 2, 8], U32)
        ix2u = sb("f1_ix2u", [128, 2, 8], U32)
        iif = sb("f1_iif", [128, 4], F32)
        l2t = sb("f1_l2", [128, 2, 8], F32)
        sel1f = sb("f1_s1f", [128, 2, 128], F32)
        sel2f = sb("f1_s2f", [128, 2, 128], F32)
        sttm = sb("f1_sttm", [128, 2, 128], F32)
        dstat = sb("f1_dst", [128, 256], F32)
        wsrc = sb("f1_wsrc", [128, 256], F32)
        wsrc2 = sb("f1_wsrc2", [128, 256], F32)
        hsrc = sb("f1_hsrc", [128, 256], F32)
        selT1 = sb("f1_selT1", [128, 256], BF16)
        selT2 = sb("f1_selT2", [128, 256], BF16)
        selbc1 = sb("f1_sbc1", [128, 4, T], BF16)
        selbc2 = sb("f1_sbc2", [128, 4, T], BF16)
        wAb = sb("f1_wab", [128, T], F32)
        wBb = sb("f1_wbb", [128, T], F32)
        hlb = sb("f1_hlb", [128, T], F32)
        hlDTb = sb("f1_hldtb", [128, T], F32)
        tmpA = sb("f1_tmpa", [128, 2, T], F32)
        tmpB = sb("f1_tmpb", [128, 2, T], F32)
        etmp = sb("f1_etmp", [128, 2, T], F32)
        probin = [ec(nc.psum_tensor(f"f1_pr{i}", [128, 2, T], F32)) for i in range(8)]

        P = Plan(sems)

        def tile_items(t):
            items = [("r1", None)]
            items += [("he0", 0), ("he1", 0), ("hsh", 0), ("hh", 0)]
            items += [("lgmm", None), ("lgtr", None), ("seltr", None)]
            items += [("pair", (p, pp)) for p in range(2) for pp in range(4)]
            for it in range(2):
                if it == 1:
                    items += [("he0", 1), ("he1", 1), ("hsh", 1), ("hh", 1)]
                for cp in range(4):
                    for path in range(3):
                        items.append((f"b{it}", (cp, path)))
                items.append((f"hw{it}", None))
                items.append((f"hlbc{it}", 0))
                items.append((f"hlbc{it}", 1))
                if it == 0:
                    items.append(("wbc", 0))
                    items.append(("wbc", 1))
                for cp in range(4):
                    for rng in ("a", "g"):
                        for path in range(3):
                            items.append((f"s{it}", (cp, rng, path)))
            return items

        all_items = []
        for t in range(NT):
            for tag, arg in tile_items(t):
                all_items.append((t, tag, arg))
        item_idx = {k: i for i, k in enumerate(all_items)}

        VEC_EVAC = {"lgtr": False, "seltr": False}

        def run_streams(eng, part="all"):
            sy, gp, te, sc, ve = (eng[k] for k in ("sync", "gpsimd", "tensor",
                                                   "scalar", "vector"))

            if part in ("all", "sync"):
              s16 = lambda key, dst, src_: P.e(key, "sAux", sy.dma_start(dst, src_), 16)
              s16(("bl", 0), b_r1[:, :], E["d_rb1"][:, :])
              s16(("bl", 1), b_r2[:, :], E["d_rb2"][:, :])
              s16(("bl", 2), b_h1[:, :], E["d_hb1"][:, :])
              s16(("bl", 3), b_h2[:, :], E["d_hb2"][:, :])
              s16(("bl", 4), b_e1[:, :], E["d_eb1"][:, :])
              s16(("bl", 5), b_s1[:, :], E["d_sb1"][:, :])
              s16(("bl", 6), b_s2[:, :], E["d_sb2"][:, :])
              s16(("bl", 7), b_mun[:, :], E["d_mun"][:, :])
              s16(("bl", 8), c_shw[:, :], E["d_shw"][:, :])
              for t in range(NT):
                if t >= 2:
                    P.w(sy, ("el", t - 2, 0))
                ap = d_x1Ts.rearrange("(c p) t -> p c t", p=128)[:, :, t * T:(t + 1) * T]
                P.e(("x1ld", t), f"sP{t % 2}", sy.dma_start(x1[:, t % 2], ap), 16)

            if part not in ("all", "gpsimd"):
                pass
            else:
              g16 = lambda key, dst, src: P.e(key, "sWL", gp.dma_start(dst, src), 16)
              rr = lambda d, p=128: d.rearrange("(c p) n -> p c n", p=p)
              g16(("wl", 0), w_r1[:, :, :], rr(E["d_rw1"]))
              g16(("wl", 1), w_r2[:, :, :], rr(E["d_rw2"]))
              g16(("wl", 2), w_h1[:, :, :], rr(E["d_hw1"]))
              g16(("wl", 3), w_h2[:, :, :], rr(E["d_hw2"]))
              g16(("wl", 4), w_e1[:, :, :], rr(E["d_ew1"]))
              g16(("wl", 5), w_e2[:, :, :], rr(E["d_ew2"]))
              g16(("wl", 6), w_eb2[:, :], E["d_eb2"][:, :])
              g16(("wl", 7), w_s1[:, :, :], rr(E["d_sw1"]))
              g16(("wl", 8), w_s2[:, :], E["d_sw2"][:, :])
              for t in range(NT):
                P.w(gp, ("icast", t))
                ap = d_intTs.rearrange("(c p) t -> p c t", p=128)[:, :, t * T:(t + 1) * T]
                P.e(("ist", t), "sS0", gp.dma_start(ap, intTb[:, t % 2]), 16)

            def robin(t, tag, arg):
                idx = item_idx[(t, tag, arg)]
                if idx >= 8:
                    P.w(te, ("rev",) + all_items[idx - 8])
                return probin[idx % 8]

            def rbank(t, tag, arg):
                return probin[item_idx[(t, tag, arg)] % 8]

            if part in ("all", "tensor"):
              P.w(te, ("wl", 8))
              P.w(te, ("bl", 8))
              P.w(te, ("vmem",))
              for t in range(NT):
                  tb = t % 2
                  P.w(te, ("cast0", t))
                  pb = robin(t, "r1", None)
                  for m in range(2):
                      last = None
                      for c in range(8):
                          last = te.matmul(pb[:, m, :], w_r1[:, c, m * 128:(m + 1) * 128],
                                           intTb[:, t % 2, c, :], start=(c == 0), stop=(c == 7))
                      if m == 1:
                          P.e(("mm", t, "r1", None), "sMM", last)
                  def s1chunks(it):
                      if it == 0:
                          return [(cp, [2 * cp, 2 * cp + 1]) for cp in range(4)]
                      return [(cp, [2 * cp, 2 * cp + 1, 8 + 2 * cp, 9 + 2 * cp])
                              for cp in range(4)]

                  def stage1(it):
                      for half in range(2):
                          pb2 = robin(t, f"he{half}", it)
                          last = None
                          for m in (2 * half, 2 * half + 1):
                              seq = []
                              for cp, cl in s1chunks(it):
                                  for c in cl:
                                      seq.append((cp, c))
                              for k, (cp, c) in enumerate(seq):
                                  if it == 1 and m == 2 * half and half == 0 and c % 2 == 0 and c < 8:
                                      P.w(te, ("cast1p", t, cp))
                                  rhs = intTb[:, t % 2, c, :] if c < 8 else vTb[:, c - 8, :]
                                  last = te.matmul(pb2[:, m % 2, :],
                                                   w_e1[:, c, m * 128:(m + 1) * 128],
                                                   rhs, start=(k == 0),
                                                   stop=(k == len(seq) - 1))
                          P.e(("mm", t, f"he{half}", it), "sMM", last)
                      pb2 = robin(t, "hsh", it)
                      seq = [(cp, c) for cp, cl in s1chunks(it) for c in cl]
                      last = None
                      for k, (cp, c) in enumerate(seq):
                          rhs = intTb[:, t % 2, c, :] if c < 8 else vTb[:, c - 8, :]
                          last = te.matmul(pb2[:, 0, :], w_s1[:, c, :], rhs,
                                           start=(k == 0), stop=(k == len(seq) - 1))
                      P.e(("mm", t, "hsh", it), "sMM", last)
                      pb2 = robin(t, "hh", it)
                      last = None
                      for m in range(2):
                          for c in range(8):
                              last = te.matmul(pb2[:, m, :], w_h1[:, c, m * 128:(m + 1) * 128],
                                               intTb[:, t % 2, c, :], start=(c == 0), stop=(c == 7))
                      P.e(("mm", t, "hh", it), "sMM", last)

                  stage1(0)
                  P.w(te, ("hrev", t))
                  pb = robin(t, "lgmm", None)
                  last = None
                  for cc in range(2):
                      last = te.matmul(pb[:, 0, :], w_r2[:, cc, :], hr[:, cc, :],
                                       start=(cc == 0), stop=(cc == 1))
                  P.e(("mm", t, "lgmm", None), "sMM", last)
                  P.w(te, ("lgev", t))
                  pb = robin(t, "lgtr", None)
                  for j in range(2):
                      last = te.transpose(pb[:, j, 0:128], lg[:, j * 128:(j + 1) * 128],
                                          c_id[:, :])
                  P.e(("mm", t, "lgtr", None), "sMM", last)
                  P.w(te, ("topk", t))
                  pb = robin(t, "seltr", None)
                  for j in range(2):
                      last = te.transpose(pb[:, 0, 128 * j:128 * j + 128],
                                          sel1f[:, j, :], c_id[:, :])
                  for j in range(2):
                      last = te.transpose(pb[:, 1, 128 * j:128 * j + 128],
                                          sel2f[:, j, :], c_id[:, :])
                  P.e(("mm", t, "seltr", None), "sMM", last)
                  for p in range(2):
                      for pp in range(4):
                          pb = robin(t, "pair", (p, pp))
                          if p == 0 and pp == 0:
                              for j in range(2):
                                  te.transpose(pb[:, 1, 128 * j:128 * j + 128],
                                               sttm[:, j, :], c_id[:, :])
                          P.w(te, ("selev", t))
                          selT = selT1 if p == 0 else selT2
                          last = te.matmul(pb[:, 0, :], c_pair[:, pp, :], selT[:, :],
                                           start=True, stop=True)
                          P.e(("mm", t, "pair", (p, pp)), "sMM", last)
                  for it in range(2):
                      if it == 1:
                          stage1(1)
                      for cp in range(4):
                          for path in range(3):
                              pb = robin(t, f"b{it}", (cp, path))
                              if path < 2:
                                  P.w(te, ("mask", t, it, path))
                              else:
                                  P.w(te, ("hshev", t, it))
                              last = None
                              for cj in range(2):
                                  c = 2 * cp + cj
                                  fcol = 8 + c
                                  if path < 2:
                                      hmx = hm1 if path == 0 else hEb
                                      selT = selT1 if path == 0 else selT2
                                      for hc in range(4):
                                          last = te.matmul(pb[:, cj, :],
                                                           w_e2[:, hc, 128 * fcol:128 * fcol + 128],
                                                           hmx[:, hc, :],
                                                           start=(hc == 0), stop=False)
                                      last = te.matmul(pb[:, cj, :],
                                                       w_eb2[:, 128 * fcol:128 * fcol + 128],
                                                       selT[:, :], start=False, stop=True)
                                  else:
                                      last = te.matmul(pb[:, cj, :],
                                                       w_s2[:, 128 * fcol:128 * fcol + 128],
                                                       hshb[:, :], start=True, stop=True)
                              P.e(("mm", t, f"b{it}", (cp, path)), "sMM", last)
                      pb = robin(t, f"hw{it}", None)
                      P.w(te, ("hhev", t, it))
                      last = None
                      for cc in range(2):
                          last = te.matmul(pb[:, 0, :], w_h2[:, cc, :], hHb[:, cc, :],
                                           start=(cc == 0), stop=(cc == 1))
                      P.e(("mm", t, f"hw{it}", None), "sMM", last)
                      P.w(te, ("hsg", t, it))
                      for k in range(2):
                          pb = robin(t, f"hlbc{it}", k)
                          lhs = c_or32 if k == 0 else c_ordt32
                          last = te.matmul(pb[:, 0, :], lhs[:, :], hsrc[:, :],
                                           start=True, stop=True)
                          P.e(("mm", t, f"hlbc{it}", k), "sMM", last)
                      if it == 0:
                          P.w(te, ("wrow", t))
                          for k in range(2):
                              pb = robin(t, "wbc", k)
                              wsx = wsrc if k == 0 else wsrc2
                              last = te.matmul(pb[:, 0, :], c_or32[:, :], wsx[:, :],
                                               start=True, stop=True)
                              P.e(("mm", t, "wbc", k), "sMM", last)
                      for cp in range(4):
                          for rng in ("a", "g"):
                              for path in range(3):
                                  pb = robin(t, f"s{it}", (cp, rng, path))
                                  last = None
                                  for cj in range(2):
                                      c = 2 * cp + cj
                                      fcol = c if rng == "a" else 16 + c
                                      if path < 2:
                                          hmx = hm1 if path == 0 else hEb
                                          selT = selT1 if path == 0 else selT2
                                          for hc in range(4):
                                              last = te.matmul(pb[:, cj, :],
                                                               w_e2[:, hc, 128 * fcol:128 * fcol + 128],
                                                               hmx[:, hc, :],
                                                               start=(hc == 0), stop=False)
                                          last = te.matmul(pb[:, cj, :],
                                                           w_eb2[:, 128 * fcol:128 * fcol + 128],
                                                           selT[:, :], start=False, stop=True)
                                      else:
                                          last = te.matmul(pb[:, cj, :],
                                                           w_s2[:, 128 * fcol:128 * fcol + 128],
                                                           hshb[:, :], start=True, stop=True)
                                  P.e(("mm", t, f"s{it}", (cp, rng, path)), "sMM", last)

            if part in ("all", "scalar"):
              P.w(sc, ("bl", 8))
              ins = sc.activation(sw0[:, :], c_shw[:, :], AF.Sigmoid)
              P.e(("sw0",), "sSC", ins)
              for t in range(NT):
                  for it in range(2):
                      if it == 0:
                          P.w(sc, ("mm", t, "r1", None))
                          ins = None
                          for m in range(2):
                              ins = sc.activation(hr[:, m, :],
                                                  probin[item_idx[(t, "r1", None)] % 8][:, m, :],
                                                  AF.Gelu, bias=b_r1[:, m:m + 1])
                          P.e(("hrev", t), "sSC", ins)
                          P.e(("rev", t, "r1", None), "sSC", ins, 0)
                      for half in range(2):
                          P.w(sc, ("mm", t, f"he{half}", it))
                          pb2 = probin[item_idx[(t, f"he{half}", it)] % 8]
                          ins = None
                          for m in range(2):
                              ins = sc.activation(hEb[:, 2 * half + m, :], pb2[:, m, :],
                                                  AF.Gelu, bias=b_e1[:, 2 * half + m:2 * half + m + 1])
                          P.e(("rev", t, f"he{half}", it), "sSC", ins)
                      P.e(("heev", t, it), "sSC", ins, 0)
                      P.w(sc, ("mm", t, "hsh", it))
                      pb2 = probin[item_idx[(t, "hsh", it)] % 8]
                      ins = sc.activation(hshb[0:64, :], pb2[0:64, 0, :], AF.Gelu,
                                          bias=b_s1[0:64, 0:1])
                      P.e(("rev", t, "hsh", it), "sSC", ins)
                      P.e(("hshev", t, it), "sSC", ins, 0)
                      P.w(sc, ("mm", t, "hh", it))
                      pb2 = probin[item_idx[(t, "hh", it)] % 8]
                      ins = None
                      for m in range(2):
                          ins = sc.activation(hHb[:, m, :], pb2[:, m, :], AF.Gelu,
                                              bias=b_h1[:, m:m + 1])
                      P.e(("rev", t, "hh", it), "sSC", ins)
                      P.e(("hhev", t, it), "sSC", ins, 0)
                      if it == 0:
                          P.w(sc, ("mm", t, "lgmm", None))
                          pb = probin[item_idx[(t, "lgmm", None)] % 8]
                          ins = sc.activation(lg[:, :], pb[:, 0, :], AF.Identity,
                                              bias=b_r2[:, 0:1])
                          P.e(("lgev", t), "sSC", ins)
                          P.e(("rev", t, "lgmm", None), "sSC", ins, 0)
                          P.w(sc, ("mm", t, "lgtr", None))
                          pb = probin[item_idx[(t, "lgtr", None)] % 8]
                          ins = None
                          for j in range(2):
                              ins = sc.activation(lgt[:, j, :], pb[:, j, 0:128], AF.Copy)
                          P.e(("lgtev", t), "sSC", ins)
                          P.e(("rev", t, "lgtr", None), "sSC", ins, 0)
                          P.w(sc, ("mm", t, "seltr", None))
                          pb = probin[item_idx[(t, "seltr", None)] % 8]
                          ins = None
                          for j in range(2):
                              ins = sc.activation(selT1[:, 128 * j:128 * j + 128],
                                                  pb[:, 0, 128 * j:128 * j + 128], AF.Copy)
                              ins = sc.activation(selT2[:, 128 * j:128 * j + 128],
                                                  pb[:, 1, 128 * j:128 * j + 128], AF.Copy)
                          P.e(("selev", t), "sSC", ins)
                          P.e(("rev", t, "seltr", None), "sSC", ins, 0)
                          P.w(sc, ("mm", t, "pair", (0, 0)))
                          pb = probin[item_idx[(t, "pair", (0, 0))] % 8]
                          ins = None
                          for j in range(2):
                              ins = sc.activation(dstat[:, 128 * j:128 * j + 128],
                                                  pb[:, 1, 128 * j:128 * j + 128], AF.Copy)
                          P.e(("dstev", t), "sSC", ins)
                          ins = None
                          for p in range(2):
                              for pp in range(4):
                                  P.w(sc, ("mm", t, "pair", (p, pp)))
                                  pb = probin[item_idx[(t, "pair", (p, pp))] % 8]
                                  dst = (selbc1 if p == 0 else selbc2)[:, pp, :]
                                  ins = sc.activation(dst, pb[:, 0, :], AF.Copy)
                                  P.e(("rev", t, "pair", (p, pp)), "sSC", ins)
                          P.e(("selbcev", t), "sSC", ins, 0)
                      if it == 1:
                          P.w(sc, ("el", t, 0))
                      P.w(sc, ("sw1",))
                      ins = None
                      for c in range(8):
                          src = x1[:, t % 2, c, :] if it == 0 else intT[:, c, :]
                          ins = sc.activation(xm[:, c, :], src, AF.Identity,
                                              bias=b_mun[:, c:c + 1])
                          ins = sc.activation(xms[:, c, :], src, AF.Identity,
                                              scale=sw0[:, 0:1], bias=b_muns[:, c:c + 1])
                      P.e(("xmv", t, it), "sSC", ins)
                      for cp in range(4):
                          for path in range(3):
                              P.w(sc, ("mm", t, f"b{it}", (cp, path)))
                              pb = probin[item_idx[(t, f"b{it}", (cp, path))] % 8]
                              cs_ = slice(2 * cp, 2 * cp + 2)
                              sc.activation(etmp[:, :, :], pb[:, :, :], AF.Exp)
                              ins = sc.activation(spq[:, path, cs_, :], etmp[:, :, :],
                                                  AF.Ln, bias=1.0)
                              P.e(("rev", t, f"b{it}", (cp, path)), "sSC", ins)
                      P.w(sc, ("mm", t, f"hw{it}", None))
                      pb = probin[item_idx[(t, f"hw{it}", None)] % 8]
                      ins = sc.activation(hsrc[0:1, :], pb[0:1, 0, :], AF.Sigmoid,
                                          bias=b_h2[0:1, :])
                      P.e(("hsg", t, it), "sSC", ins)
                      P.e(("rev", t, f"hw{it}", None), "sSC", ins, 0)
                      if it == 0:
                          P.w(sc, ("dstev", t))
                          ins = sc.activation(wsrc[0:1, :], dstat[0:1, :], AF.Sigmoid)
                          P.e(("wsg", t), "sSC", ins)
                      for cp in range(4):
                          for rng in ("a", "g"):
                              for path in range(3):
                                  P.w(sc, ("mm", t, f"s{it}", (cp, rng, path)))
                                  pb = probin[item_idx[(t, f"s{it}", (cp, rng, path))] % 8]
                                  sgx = sgA if rng == "a" else sgG
                                  g = (2 * t + it) * 4 + cp
                                  if g >= 2:
                                      t3, r3 = divmod(g - 2, 8)
                                      it3, cp3 = divmod(r3, 4)
                                      P.w(sc, ("ew", t3, it3, cp3, rng))
                                  dst = sgx[:, ((2 * t + it) * 4 + cp) % 2,
                                            2 * path:2 * path + 2, :]
                                  ins = sc.activation(dst, pb[:, :, :], AF.Sigmoid)
                                  P.e(("rev", t, f"s{it}", (cp, rng, path)), "sSC", ins)
                              P.e(("sgev", t, it, cp, rng), "sSC", ins, 0)

            if part in ("all", "vector"):
              ins = ve.memset(sel1f[:, :, :], 0.0)
              ins = ve.memset(sel2f[:, :, :], 0.0)
              ins = ve.memset(sttm[:, :, :], 0.0)
              ins = ve.memset(wsrc[:, :], 0.0)
              ins = ve.memset(wsrc2[:, :], 0.0)
              ve.memset(hsrc[:, :], 0.0)
              ve.memset(hshb[:, :], 0.0)
              ins = ve.memset(hshb[64:65, :], 1.0)
              P.e(("vmem",), "sVE", ins)
              if P.emitting:
                  ve.wait_ge(E["sAux"].h, 80)
              P.w(ve, ("sw0",))
              ins = ve.tensor_scalar(sw1c[:, :], sw0[:, :], -1.0, 1.0, ALU.mult, ALU.add)
              ins = ve.tensor_scalar(b_muns[:, :], b_mun[:, :], sw0[:, 0:1], None,
                                     ALU.mult, ALU.bypass)
              P.e(("sw1",), "sVE", ins)
              def cast0(t):
                  P.w(ve, ("x1ld", t))
                  if t >= 2:
                      P.w(ve, ("ist", t - 2))
                  ins = ve.tensor_copy(intTb[:, t % 2], x1[:, t % 2])
                  P.e(("cast0", t), "sVE", ins)
              cast0(0)
              for t in range(NT):
                  P.w(ve, ("lgtev", t))
                  if t >= 1:
                      P.w(ve, ("mm", t - 1, "seltr", None))
                      P.w(ve, ("mm", t - 1, "pair", (0, 0)))
                  for j in range(2):
                      sv = lambda key, ins_: P.e(key, "sVX", ins_)
                      sv(("tk", t, j, 0), ve.max(mx[:, j, :], lgt[:, j, 0:8]))
                      P.w(ve, ("tk", t, j, 0))
                      sv(("tk", t, j, 1), ve.max_index(ixu[:, j, :], mx[:, j, :],
                                                       lgt[:, j, 0:8]))
                      P.w(ve, ("tk", t, j, 1))
                      sv(("tk", t, j, 2), ve.tensor_copy(iif[:, 0:1], ixu[:, j, 0:1]))
                      P.w(ve, ("tk", t, j, 2))
                      ve.tensor_scalar(sel1f[:, j, 0:8], c_iota[:, :], iif[:, 0:1], None,
                                       ALU.is_equal, ALU.bypass)
                      sv(("tk", t, j, 3),
                         ve.scalar_tensor_tensor(l2t[:, j, :], sel1f[:, j, 0:8], -1e30,
                                                 lgt[:, j, 0:8], ALU.mult, ALU.add))
                      P.w(ve, ("tk", t, j, 3))
                      sv(("tk", t, j, 4), ve.max(mx2[:, j, :], l2t[:, j, :]))
                      P.w(ve, ("tk", t, j, 4))
                      sv(("tk", t, j, 5), ve.max_index(ix2u[:, j, :], mx2[:, j, :],
                                                       l2t[:, j, :]))
                      P.w(ve, ("tk", t, j, 5))
                      sv(("tk", t, j, 6), ve.tensor_copy(iif[:, 1:2], ix2u[:, j, 0:1]))
                      P.w(ve, ("tk", t, j, 6))
                      ve.tensor_scalar(sel2f[:, j, 0:8], c_iota[:, :], iif[:, 1:2], None,
                                       ALU.is_equal, ALU.bypass)
                      ins = ve.tensor_tensor(sttm[:, j, 0:1], mx[:, j, 0:1],
                                             mx2[:, j, 0:1], ALU.subtract)
                  P.e(("topk", t), "sVE", ins)
                  for it in range(2):
                      P.w(ve, ("heev", t, it))
                      P.w(ve, ("selbcev", t))
                      ve.tensor_tensor(hm1[:, :, :], hEb[:, :, :], selbc1[:, :, :], ALU.mult)
                      ins = ve.tensor_tensor(hEb[:, :, :], hEb[:, :, :], selbc2[:, :, :],
                                             ALU.mult)
                      P.e(("mask", t, it, 0), "sVE", ins, 1)
                      P.e(("mask", t, it, 1), "sVE", ins, 0)
                      P.w(ve, ("xmv", t, it))
                      ins = None
                      for path in range(3):
                          P.w(ve, ("rev", t, f"b{it}", (3, path)))
                          xmx = xm if path < 2 else xms
                          ins = ve.tensor_tensor(spq[:, path, :, :], spq[:, path, :, :],
                                                 xmx[:, :, :], ALU.mult)
                      P.e(("qrdy", t, it), "sVE", ins)
                      for k in range(2):
                          P.w(ve, ("mm", t, f"hlbc{it}", k))
                          pb = probin[item_idx[(t, f"hlbc{it}", k)] % 8]
                          dst = hlb if k == 0 else hlDTb
                          ins = ve.tensor_copy(dst[:, :], pb[:, 0, :])
                          P.e(("rev", t, f"hlbc{it}", k), "sVE", ins)
                      if it == 0:
                          P.w(ve, ("wsg", t))
                          ve.tensor_scalar(wsrc2[0:1, :], wsrc[0:1, :], -1.0, 1.0,
                                           ALU.mult, ALU.add)
                          ve.tensor_scalar(wsrc2[0:1, :], wsrc2[0:1, :], sw1c[0:1, 0:1],
                                           None, ALU.mult, ALU.bypass)
                          ins = ve.tensor_scalar(wsrc[0:1, :], wsrc[0:1, :], sw1c[0:1, 0:1],
                                                 None, ALU.mult, ALU.bypass)
                          P.e(("wrow", t), "sVE", ins)
                          for k in range(2):
                              P.w(ve, ("mm", t, "wbc", k))
                              pb = probin[item_idx[(t, "wbc", k)] % 8]
                              dst = wAb if k == 0 else wBb
                              ins = ve.tensor_copy(dst[:, :], pb[:, 0, :])
                              P.e(("rev", t, "wbc", k), "sVE", ins)
                      for cp in range(4):
                          cs = slice(2 * cp, 2 * cp + 2)
                          db = ((2 * t + it) * 4 + cp) % 2
                          P.w(ve, ("sgev", t, it, cp, "a"))
                          P.w(ve, ("sgev", t, it, cp, "g"))
                          P.w(ve, ("qrdy", t, it))
                          wA2, wB2 = bcast2(wAb[:, :], 2), bcast2(wBb[:, :], 2)
                          hl2, hlDT2 = bcast2(hlb[:, :], 2), bcast2(hlDTb[:, :], 2)
                          vnv = [spq[:, p, cs, :] for p in range(3)]
                          if it == 1:
                              for p in range(2):
                                  tgt = sgA[:, db, 2 * p:2 * p + 2, :]
                                  ve.tensor_tensor(tgt, tgt, vT[:, cs, :], ALU.mult)
                                  ve.tensor_tensor(vnv[p], tgt, vnv[p], ALU.subtract)
                              tgt = sgA[:, db, 4:6, :]
                              ve.scalar_tensor_tensor(tgt, tgt, sw0[:, 0:1], vT[:, cs, :],
                                                      ALU.mult, ALU.mult)
                              ve.tensor_tensor(vnv[2], tgt, vnv[2], ALU.subtract)
                          for p in range(3):
                              tgt = sgG[:, db, 2 * p:2 * p + 2, :]
                              ve.tensor_tensor(tgt, tgt, vnv[p], ALU.mult)
                          u1 = sgG[:, db, 0:2, :]
                          u2 = sgG[:, db, 2:4, :]
                          us = sgG[:, db, 4:6, :]
                          ve.tensor_tensor(tmpA[:, :, :], wA2, u1, ALU.mult)
                          ve.tensor_tensor(tmpB[:, :, :], wB2, u2, ALU.mult)
                          ve.tensor_tensor(tmpA[:, :, :], tmpA[:, :, :], tmpB[:, :, :],
                                           ALU.add)
                          ve.tensor_tensor(tmpA[:, :, :], tmpA[:, :, :], us, ALU.add)
                          ve.tensor_tensor(tmpA[:, :, :], hlDT2, tmpA[:, :, :], ALU.mult)
                          xs = x1[:, t % 2, cs, :] if it == 0 else intT[:, cs, :]
                          ve.tensor_tensor(intT[:, cs, :], xs, tmpA[:, :, :],
                                           ALU.subtract if it == 0 else ALU.add)
                          ve.tensor_tensor(tmpB[:, :, :], wA2, vnv[0], ALU.mult)
                          ve.tensor_tensor(tmpA[:, :, :], wB2, vnv[1], ALU.mult)
                          ve.tensor_tensor(tmpB[:, :, :], tmpB[:, :, :], tmpA[:, :, :],
                                           ALU.add)
                          ve.tensor_tensor(tmpB[:, :, :], tmpB[:, :, :], vnv[2], ALU.add)
                          if it == 1:
                              ve.tensor_tensor(tmpB[:, :, :], tmpB[:, :, :], vT[:, cs, :],
                                               ALU.subtract)
                              ve.tensor_tensor(tmpB[:, :, :], hl2, tmpB[:, :, :], ALU.mult)
                              ins = ve.tensor_tensor(vT[:, cs, :], vT[:, cs, :],
                                                     tmpB[:, :, :], ALU.add)
                          else:
                              ve.tensor_tensor(tmpB[:, :, :], hl2, tmpB[:, :, :], ALU.mult)
                              ins = ve.tensor_scalar(vT[:, cs, :], tmpB[:, :, :], -1.0,
                                                     None, ALU.mult, ALU.bypass)
                          P.e(("ew", t, it, cp, "a"), "sVE", ins, 1)
                          P.e(("ew", t, it, cp, "g"), "sVE", ins, 0)
                          if it == 0:
                              ve.tensor_copy(intTb[:, t % 2, cs, :], intT[:, cs, :])
                              ins = ve.tensor_copy(vTb[:, cs, :], vT[:, cs, :])
                              P.e(("cast1p", t, cp), "sVE", ins)
                      P.e(("el", t, it), "sVE", ins, 0)
                      if it == 0:
                          if t + 1 < NT:
                              cast0(t + 1)
                      else:
                          ins = ve.tensor_copy(intTb[:, t % 2], intT[:, :, :])
                          P.e(("icast", t), "sVE", ins)

        null = {k: _Null() for k in ("sync", "gpsimd", "tensor", "scalar", "vector")}
        run_streams(null, "all")
        P.start_emit()

        def mk(name):
            def body(engine):
                eng = dict(null)
                eng[name] = engine
                run_streams(eng, name)
            return body

        with nc.Block("F1") as blk:
            blk.sync(mk("sync"))
            blk.gpsimd(mk("gpsimd"))
            blk.tensor(mk("tensor"))
            blk.scalar(mk("scalar"))
            blk.vector(mk("vector"))

            @blk.gpsimd
            def _(gp):
                gp.wait_ge(E["sS0"].h, P.ev[("ist", NT - 1)][1])


def emit_F2(nc, E):
    sems = {k: E[k] for k in ("sLD", "sWL", "sST", "sMM", "sSC", "sVE", "sAux",
                              "sP2", "sP3", "sP4", "sP5", "sS1")}
    sST = E["sST"]
    c_oc, c_or32 = E["c_oc"], E["c_or32"]
    d_x1Ts, d_intTs, d_outT = E["d_x1Ts"], E["d_intTs"], E["d_outT"]

    NT = NQ // F2T
    T = F2T

    with contextlib.ExitStack() as es:
        ec = es.enter_context
        sb = lambda nm, shp, dt: ec(nc.sbuf_tensor(nm, shp, dt))
        w_f1 = sb("f2_wf1", [128, 8, 2 * D], BF16)
        w_f2 = sb("f2_wf2", [128, 16, D], BF16)
        b_f1 = sb("f2_bf1", [128, 16], F32)
        b_f2 = sb("f2_bf2", [128, 8], F32)
        b_iw = sb("f2_biw", [128, 8], F32)
        ib = sb("f2_ib", [128, 2, 8, T], BF16)
        x1f = sb("f2_x1", [128, 2, 8, T], F32)
        rh = sb("f2_rh", [128, 16, T], BF16)
        reft = sb("f2_reft", [128, T], F32)
        pre2 = sb("f2_pre2", [128, 8, T], F32)
        p2b = sb("f2_p2b", [128, 8, T], BF16)
        p2sq = sb("f2_p2sq", [128, 8, T], BF16)
        st2 = sb("f2_st2", [128, 4, T], F32)
        tmp2 = sb("f2_tmp2", [128, T], F32)
        x2 = sb("f2_x2", [128, 8, T], F32)
        probin = [ec(nc.psum_tensor(f"f2_pr{i}", [128, T], F32)) for i in range(4)]

        P = Plan(sems)

        def tile_items(t):
            return ([("rh", m) for m in range(16)] + [("rf", m) for m in range(8)]
                    + [("s1", None), ("s2", None), ("nmb", None), ("rsb", None)])

        all_items = [(t, tag, arg) for t in range(NT) for tag, arg in tile_items(t)]
        item_idx = {k: i for i, k in enumerate(all_items)}

        def run_streams(eng, part="all"):
            sy, gp, te, sc, ve = (eng[k] for k in ("sync", "gpsimd", "tensor",
                                                   "scalar", "vector"))

            if part in ("all", "sync"):
                s16 = lambda key, dst, src_: P.e(key, "sAux", sy.dma_start(dst, src_), 16)
                s16(("b2l", 0), b_f1[:, :], E["d_fb1"][:, :])
                s16(("b2l", 1), b_f2[:, :], E["d_fb2"][:, :])
                s16(("b2l", 2), b_iw[:, :], E["d_iw"][:, :])
                for t in range(NT):
                    if t >= 2:
                        P.w(sy, ("fin2", t - 2))
                    ap = d_intTs.rearrange("(c p) t -> p c t", p=128)[:, :, t * T:(t + 1) * T]
                    P.e(("ibld", t), f"sP{2 + t % 2}", sy.dma_start(ib[:, t % 2], ap), 16)
                    ap = d_x1Ts.rearrange("(c p) t -> p c t", p=128)[:, :, t * T:(t + 1) * T]
                    P.e(("x1ld2", t), f"sP{4 + t % 2}", sy.dma_start(x1f[:, t % 2], ap), 16)

            if part in ("all", "gpsimd"):
                rr = lambda d: d.rearrange("(c p) n -> p c n", p=128)
                P.e(("w2l", 0), "sWL", gp.dma_start(w_f1[:, :, :], rr(E["d_fw1"])), 16)
                P.e(("w2l", 1), "sWL", gp.dma_start(w_f2[:, :, :], rr(E["d_fw2"])), 16)
                for t in range(NT):
                    P.w(gp, ("x2dn", t))
                    ap = d_outT.rearrange("(c p) t -> p c t", p=128)[:, :, t * T:(t + 1) * T]
                    P.e(("ost", t), "sS1", gp.dma_start(ap, x2[:, :, :]), 16)

            def robin(t, tag, arg):
                idx = item_idx[(t, tag, arg)]
                if idx >= 4:
                    P.w(te, ("rev2",) + all_items[idx - 4])
                return probin[idx % 4]

            if part in ("all", "tensor"):
                P.w(te, ("w2l", 1))
                P.w(te, ("b2l", 2))
                for t in range(NT):
                    P.w(te, ("ibld", t))
                    for m in range(16):
                        pb = robin(t, "rh", m)
                        last = None
                        for c in range(8):
                            last = te.matmul(pb[:, :], w_f1[:, c, 128 * m:128 * m + 128],
                                             ib[:, t % 2, c, :], start=(c == 0), stop=(c == 7))
                        P.e(("mm2", t, "rh", m), "sMM", last)
                    for m in range(8):
                        pb = robin(t, "rf", m)
                        P.w(te, ("rhev", t, 15) if m == 0 else ("rhev", t, 15))
                        last = None
                        for hc in range(16):
                            last = te.matmul(pb[:, :], w_f2[:, hc, 128 * m:128 * m + 128],
                                             rh[:, hc, :], start=(hc == 0), stop=(hc == 15))
                        P.e(("mm2", t, "rf", m), "sMM", last)
                    pb = robin(t, "s1", None)
                    P.w(te, ("castb", t))
                    last = None
                    for c in range(8):
                        last = te.matmul(pb[:, :], c_oc[:, :], p2b[:, c, :],
                                         start=(c == 0), stop=(c == 7))
                    P.e(("mm2", t, "s1", None), "sMM", last)
                    pb = robin(t, "s2", None)
                    P.w(te, ("sq2", t))
                    last = None
                    for c in range(8):
                        last = te.matmul(pb[:, :], c_oc[:, :], p2sq[:, c, :],
                                         start=(c == 0), stop=(c == 7))
                    P.e(("mm2", t, "s2", None), "sMM", last)
                    pb = robin(t, "nmb", None)
                    P.w(te, ("st2v", t))
                    ins = te.matmul(pb[:, :], c_or32[:, :], st2[:, 3, :], start=True, stop=True)
                    P.e(("mm2", t, "nmb", None), "sMM", ins)
                    pb = robin(t, "rsb", None)
                    P.w(te, ("rstd2", t))
                    ins = te.matmul(pb[:, :], c_or32[:, :], st2[:, 2, :], start=True, stop=True)
                    P.e(("mm2", t, "rsb", None), "sMM", ins)

            if part in ("all", "scalar"):
                P.w(sc, ("b2l", 2))
                for t in range(NT):
                    for m in range(16):
                        P.w(sc, ("mm2", t, "rh", m))
                        pb = probin[item_idx[(t, "rh", m)] % 4]
                        ins = sc.activation(rh[:, m, :], pb[:, :], AF.Gelu,
                                            bias=b_f1[:, m:m + 1])
                        P.e(("rev2", t, "rh", m), "sSC", ins)
                        if m == 15:
                            P.e(("rhev", t, 15), "sSC", ins, 0)
                    for m in range(8):
                        P.w(sc, ("mm2", t, "rf", m))
                        if m > 0:
                            P.w(sc, ("pr2", t, m - 1))
                        elif t > 0:
                            P.w(sc, ("pr2", t - 1, 7))
                        pb = probin[item_idx[(t, "rf", m)] % 4]
                        ins = sc.activation(reft[:, :], pb[:, :], AF.Identity,
                                            bias=b_f2[:, m:m + 1])
                        P.e(("rev2", t, "rf", m), "sSC", ins)
                        P.e(("rfev", t, m), "sSC", ins, 0)
                    P.w(sc, ("cast2", t))
                    ins = sc.activation(p2sq[:, :, :], pre2[:, :, :], AF.Square)
                    P.e(("sq2", t), "sSC", ins)
                    P.w(sc, ("st2v", t))
                    ins = sc.activation(tmp2[0:1, :], st2[0:1, 1, :], AF.Ln, bias=LN_EPS)
                    P.e(("ln2", t), "sSC", ins)
                    ins = sc.activation(st2[0:1, 2, :], tmp2[0:1, :], AF.Exp, scale=-0.5)
                    P.e(("rstd2", t), "sSC", ins)

            if part in ("all", "vector"):
                ins = ve.memset(st2[:, :, :], 0.0)
                P.e(("vm2",), "sVE", ins)
                for t in range(NT):
                    P.w(ve, ("x1ld2", t))
                    ins = None
                    for m in range(8):
                        P.w(ve, ("rfev", t, m))
                        ve.scalar_tensor_tensor(pre2[:, m, :], reft[:, :], b_iw[:, m:m + 1],
                                                x1f[:, t % 2, m, :], ALU.mult, ALU.add)
                        ins = ve.tensor_tensor(pre2[:, m, :], pre2[:, m, :],
                                               x1f[:, t % 2, m, :], ALU.add)
                        P.e(("pr2", t, m), "sVE", ins)
                    P.e(("cast2", t), "sVE", ins, 0)
                    ins = ve.tensor_copy(p2b[:, :, :], pre2[:, :, :])
                    P.e(("castb", t), "sVE", ins)
                    P.w(ve, ("mm2", t, "s1", None))
                    P.w(ve, ("mm2", t, "s2", None))
                    pb1 = probin[item_idx[(t, "s1", None)] % 4]
                    pb2 = probin[item_idx[(t, "s2", None)] % 4]
                    ve.tensor_scalar(st2[0:1, 3, :], pb1[0:1, :], -1.0 / D, None,
                                     ALU.mult, ALU.bypass)
                    ve.tensor_scalar(st2[0:1, 1, :], pb1[0:1, :], 1.0 / D, None,
                                     ALU.mult, ALU.bypass)
                    ve.tensor_tensor(tmp2[0:1, :], st2[0:1, 1, :], st2[0:1, 1, :], ALU.mult)
                    ve.tensor_scalar(st2[0:1, 1, :], pb2[0:1, :], 1.0 / D, None,
                                     ALU.mult, ALU.bypass)
                    ins = ve.tensor_tensor(st2[0:1, 1, :], st2[0:1, 1, :], tmp2[0:1, :],
                                           ALU.subtract)
                    P.e(("st2v", t), "sVE", ins)
                    P.e(("rev2", t, "s1", None), "sVE", ins, 0)
                    P.e(("rev2", t, "s2", None), "sVE", ins, 0)
                    P.w(ve, ("mm2", t, "nmb", None))
                    P.w(ve, ("mm2", t, "rsb", None))
                    if t >= 1:
                        P.w(ve, ("ost", t - 1))
                    pbn = probin[item_idx[(t, "nmb", None)] % 4]
                    pbr = probin[item_idx[(t, "rsb", None)] % 4]
                    ins = None
                    for m in range(8):
                        ve.tensor_tensor(tmp2[:, :], pre2[:, m, :], pbn[:, :], ALU.add)
                        ins = ve.tensor_tensor(x2[:, m, :], tmp2[:, :], pbr[:, :], ALU.mult)
                    P.e(("x2dn", t), "sVE", ins)
                    P.e(("rev2", t, "nmb", None), "sVE", ins, 0)
                    P.e(("rev2", t, "rsb", None), "sVE", ins, 0)
                    P.e(("fin2", t), "sVE", ins, 0)

        null = {k: _Null() for k in ("sync", "gpsimd", "tensor", "scalar", "vector")}
        run_streams(null, "all")
        P.start_emit()

        def mk(name):
            def body(engine):
                eng = dict(null)
                eng[name] = engine
                run_streams(eng, name)
            return body

        with nc.Block("F2") as blk:
            blk.sync(mk("sync"))
            blk.gpsimd(mk("gpsimd"))
            blk.tensor(mk("tensor"))
            blk.scalar(mk("scalar"))
            blk.vector(mk("vector"))

            @blk.gpsimd
            def _(gp):
                gp.wait_ge(E["sS1"].h, P.ev[("ost", NT - 1)][1])



def build_program(upto="F2", dbg=False, phases=None):
    nc = bass.Bass("TRN2", target_bir_lowering=False)
    scratch_kind = "ExternalOutput" if dbg else "Internal"
    order = ["A1", "A2", "F1", "F2"]
    if phases is None:
        phases = order[:order.index(upto) + 1]

    E = {}
    E["d_xT"] = nc.dram_tensor("xT", [D, NSEQ], F32, kind="ExternalInput").ap()
    E["d_xqT"] = nc.dram_tensor("xqT", [D, NQ], F32, kind="ExternalInput").ap()
    E["d_wqkv"] = nc.dram_tensor("wqkv", [D, 3 * D], F32, kind="ExternalInput").ap()
    for nm, shp, dt in [
        ("identf", [128, 128], F32), ("onescol", [128, 128], F32),
        ("onesrow", [128, 128], F32), ("onesrowdt", [128, 128], F32),
        ("pairpat", [4, 128, 128], F32), ("iota8", [128, 8], F32),
        ("cmask", [8, 128, QB], F32),
        ("rw1", [D, 256], F32), ("rb1", [128, 2], F32),
        ("rw2", [256, 128], F32), ("rb2", [128, 1], F32),
        ("hw1", [D, 256], F32), ("hb1", [128, 2], F32),
        ("hw2", [256, 128], F32), ("hb2", [1, 1], F32),
        ("ew1", [2 * D, 512], F32), ("eb1", [128, 4], F32),
        ("ew2", [512, 3 * D], F32), ("eb2", [128, 3 * D], F32),
        ("sw1", [2 * D, 128], F32), ("sb1", [128, 1], F32),
        ("sw2", [128, 3 * D], F32), ("sb2", [128, 24], F32),
        ("shw", [128, 1], F32),
        ("fw1", [D, 2 * D], F32), ("fb1", [128, 16], F32),
        ("fw2", [2 * D, D], F32), ("fb2", [128, 8], F32),
        ("mun", [128, 8], F32), ("iw", [128, 8], F32),
        ("misc", [128, 4], F32),
    ]:
        E["d_" + nm] = nc.dram_tensor(nm, shp, dt, kind="ExternalInput").ap()

    kind_of = lambda prod: scratch_kind if prod in phases else "ExternalInput"
    E["d_kTs"] = nc.dram_tensor("kTs", [D, NSEQ], BF16, kind=kind_of("A1")).ap()
    E["d_Vs"] = nc.dram_tensor("Vs", [NSEQ, D], BF16, kind=kind_of("A1")).ap()
    E["d_qTs"] = nc.dram_tensor("qTs", [D, NQ], BF16, kind=kind_of("A1")).ap()
    E["d_x1Ts"] = nc.dram_tensor("x1Ts", [D, NQ], F32, kind=kind_of("A2")).ap()
    E["d_intTs"] = nc.dram_tensor("intTs", [D, NQ], BF16, kind=kind_of("F1")).ap()
    E["d_outT"] = nc.dram_tensor("outT", [D, NQ], F32, kind="ExternalOutput").ap()

    with contextlib.ExitStack() as es:
        ec = es.enter_context
        for nm in ("sLD", "sWL", "sST", "sMM", "sSC", "sVE", "sVX", "sAux",
                   "sP0", "sP1", "sP2", "sP3", "sP4", "sP5", "sP6", "sP7",
                   "sS0", "sS1"):
            E[nm] = Sem(ec(nc.semaphore(nm)))

        E["c_id"] = ec(nc.sbuf_tensor("c_id", [128, 128], F32))
        E["c_oc"] = ec(nc.sbuf_tensor("c_oc", [128, 128], BF16))
        E["c_or32"] = ec(nc.sbuf_tensor("c_or32", [128, 128], F32))
        E["c_ordt32"] = ec(nc.sbuf_tensor("c_ordt32", [128, 128], F32))
        E["c_pair"] = ec(nc.sbuf_tensor("c_pair", [128, 4, 128], BF16))
        E["c_iota"] = ec(nc.sbuf_tensor("c_iota", [128, 8], F32))
        E["c_misc"] = ec(nc.sbuf_tensor("c_misc", [128, 4], F32))
        nc.const_aps.register(F32, LN_EPS, E["c_misc"][:, 0:1])
        nc.const_aps.register(F32, 1.0, E["c_misc"][:, 1:2])
        nc.const_aps.register(F32, 0.0, E["c_misc"][:, 2:3])

        sLD, sWL = E["sLD"], E["sWL"]
        with nc.Block("INIT") as blk:
            @blk.sync
            def _(sync):
                sAux = E["sAux"]
                sAux.inc(sync.dma_start(E["c_id"][:, :], E["d_identf"][:, :]), 16)
                sAux.inc(sync.dma_start(E["c_iota"][:, :], E["d_iota8"][:, :]), 16)
                sAux.inc(sync.dma_start(E["c_misc"][:, :], E["d_misc"][:, :]), 16)
                sAux.inc(sync.dma_start(E["c_or32"][:, :], E["d_onesrow"][:, :]), 16)
                sAux.inc(sync.dma_start(E["c_ordt32"][:, :], E["d_onesrowdt"][:, :]), 16)

            @blk.gpsimd
            def _(g):
                sWL.inc(g.dma_start(E["c_oc"][:, :], E["d_onescol"][:, :]), 16)
                sWL.inc(g.dma_start(E["c_pair"][:, :, :],
                                    E["d_pairpat"].rearrange("k p f -> p k f")), 16)

        if "A1" in phases:
            with nc.named_scope("A1"):
                emit_A1(nc, E)
        if "A2" in phases:
            with nc.named_scope("A2"):
                emit_A2(nc, E)
        if "F1" in phases:
            with nc.named_scope("F1"):
                emit_F1(nc, E)
        if "F2" in phases:
            with nc.named_scope("F2"):
                emit_F2(nc, E)

    return nc



def stage_inputs(inputs):
    x = np.asarray(inputs["L0_attn_x"])
    wqkv = np.ascontiguousarray(inputs["L0_attn_W_qkv"], dtype=np.float32)

    ident = np.eye(128, dtype=np.float32)
    onescol = np.zeros((128, 128), np.float32); onescol[:, 0] = 1.0
    onesrow = np.zeros((128, 128), np.float32); onesrow[0, :] = 1.0
    onesrowdt = np.zeros((128, 128), np.float32); onesrowdt[0, :] = DT_STEP
    pairpat = np.zeros((4, 128, 128), np.float32)
    for P in range(4):
        pairpat[P, 2 * P, 0:64] = 1.0
        pairpat[P, 2 * P + 1, 64:128] = 1.0
    iota8 = np.tile(np.arange(8, dtype=np.float32)[None, :], (128, 1))

    p_idx = np.arange(128)[:, None]
    j_idx = np.arange(QB)[None, :]
    cmask = {}
    for h in (0, 1):
        m = np.zeros((8, 128, QB), np.float32)
        for s in range(8):
            if h == 0:
                if s < 4:
                    m[s] = (128 * s + p_idx <= j_idx)
            else:
                if s < 4:
                    m[s] = 1.0
                else:
                    m[s] = (128 * (s - 4) + p_idx <= j_idx)
        cmask[h] = m

    f32 = lambda a: np.ascontiguousarray(np.asarray(a), dtype=np.float32)
    rw1 = f32(inputs["L0_ffn_router_w1"])
    rb1 = f32(inputs["L0_ffn_router_b1"]).reshape(2, 128).T.copy()
    rw2 = np.zeros((256, 128), np.float32); rw2[:, :8] = f32(inputs["L0_ffn_router_w2"])
    rb2 = np.zeros((128, 1), np.float32); rb2[:8, 0] = f32(inputs["L0_ffn_router_b2"])
    hw1 = f32(inputs["L0_ffn_halt_w1"])
    hb1 = f32(inputs["L0_ffn_halt_b1"]).reshape(2, 128).T.copy()
    hw2 = np.zeros((256, 128), np.float32); hw2[:, 0:1] = f32(inputs["L0_ffn_halt_w2"])
    hb2 = f32(inputs["L0_ffn_halt_b2"]).reshape(1, 1)
    ew1 = np.ascontiguousarray(
        np.transpose(f32(inputs["L0_ffn_expert_w1"]), (1, 0, 2)).reshape(2 * D, 512))
    eb1 = f32(inputs["L0_ffn_expert_b1"]).reshape(512).reshape(4, 128).T.copy()
    ew2 = f32(inputs["L0_ffn_expert_w2"]).reshape(512, 3 * D)
    eb2 = np.zeros((128, 3 * D), np.float32); eb2[:8] = f32(inputs["L0_ffn_expert_b2"])
    sw1 = np.zeros((2 * D, 128), np.float32); sw1[:, :HID] = f32(inputs["L0_ffn_shared_w1"])
    sb1 = np.zeros((128, 1), np.float32); sb1[:HID, 0] = f32(inputs["L0_ffn_shared_b1"])
    sw2 = np.zeros((128, 3 * D), np.float32); sw2[:HID] = f32(inputs["L0_ffn_shared_w2"])
    sw2[HID] = f32(inputs["L0_ffn_shared_b2"])
    sb2 = f32(inputs["L0_ffn_shared_b2"]).reshape(24, 128).T.copy()
    shw = np.full((128, 1), np.float32(inputs["L0_ffn_shared_weight"]), np.float32)
    fw1 = f32(inputs["L0_ffn_refine_w1"])
    fb1 = f32(inputs["L0_ffn_refine_b1"]).reshape(16, 128).T.copy()
    fw2 = f32(inputs["L0_ffn_refine_w2"])
    fb2 = f32(inputs["L0_ffn_refine_b2"]).reshape(8, 128).T.copy()
    mun = (-f32(inputs["L0_ffn_mu"])).reshape(8, 128).T.copy()
    iw = f32(inputs["L0_ffn_integration_weight"]).reshape(8, 128).T.copy()

    misc = np.tile(np.array([[LN_EPS, 1.0, 0.0, 0.0]], np.float32), (128, 1))
    shared = dict(misc=misc, identf=ident, onescol=onescol, onesrow=onesrow, onesrowdt=onesrowdt,
                  pairpat=pairpat, iota8=iota8, wqkv=wqkv,
                  rw1=rw1, rb1=rb1, rw2=rw2, rb2=rb2, hw1=hw1, hb1=hb1, hw2=hw2,
                  hb2=hb2, ew1=ew1, eb1=eb1, ew2=ew2, eb2=eb2, sw1=sw1, sb1=sb1,
                  sw2=sw2, sb2=sb2, shw=shw, fw1=fw1, fb1=fb1, fw2=fw2, fb2=fb2,
                  mun=mun, iw=iw)

    xT = [np.ascontiguousarray(x[b].T, dtype=np.float32) for b in range(NB)]
    in_maps = []
    for c in range(NCORE):
        b, h = c // 2, c % 2
        cols = np.concatenate([np.arange(QB * (2 * u + h), QB * (2 * u + h) + QB)
                               for u in range(NBLK)])
        m = dict(shared)
        m["xT"] = xT[b]
        m["xqT"] = np.ascontiguousarray(xT[b][:, cols])
        m["cmask"] = cmask[h]
        in_maps.append(m)
    return in_maps


def assemble_output(results):
    y = np.empty((NB, NSEQ, D), np.float32)
    for c in range(NCORE):
        b, h = c // 2, c % 2
        outT = results[c]["outT"]
        for u in range(NBLK):
            y[b, QB * (2 * u + h):QB * (2 * u + h) + QB, :] = \
                outT[:, QB * u:QB * u + QB].T
    return y


def kernel(**inputs):
    nc = build_program(upto="F2")
    in_maps = stage_inputs(inputs)
    res = run_bass_kernel_spmd(nc, in_maps, core_ids=list(range(NCORE)))
    return assemble_output(res.results)


# version 09
# speedup vs baseline: 1.3775x; 1.0021x over previous
import contextlib
import numpy as np
import ml_dtypes

import concourse.bass as bass
import concourse.mybir as mybir
from concourse.bass_utils import run_bass_kernel_spmd
from concourse.alu_op_type import AluOpType

F32 = mybir.dt.float32
BF16 = mybir.dt.bfloat16
U32 = mybir.dt.uint32
AF = mybir.ActivationFunctionType
AX = mybir.AxisListType.X
ALU = AluOpType

D = 1024
NSEQ = 8192
NB = 4
NCORE = 8
NQ = 4096
NBLK = 8
QB = 512
KC = 1024
EXP = 8
HID = 64
DT_STEP = 0.1
LN_EPS = 1e-5
SCL = 1.0 / 32.0

A1T = 512
F1T = 256
F2T = 512


class Sem:
    def __init__(self, h):
        self.h = h
        self.n = 0

    def inc(self, instr, d=1):
        instr.then_inc(self.h, d)
        self.n += d
        return self.n


def bcast2(ap, n):
    return bass.AP(ap.tensor, ap.offset,
                   [list(ap.ap[0]), [0, n], list(ap.ap[1])])


class Plan:

    def __init__(self, sems):
        self.sems = sems
        self.ev = {}
        self.emitting = False
        self.shadow = {k: s.n for k, s in sems.items()}

    def start_emit(self):
        self.emitting = True
        for k, s in self.sems.items():
            assert self.shadow[k] >= s.n

    def e(self, key, semname, instr, d=1):
        s = self.sems[semname]
        if not self.emitting:
            self.shadow[semname] += d
            self.ev[key] = (semname, self.shadow[semname])
            return None
        if d:
            s.inc(instr, d)
        assert (semname, s.n) == self.ev[key], (key, s.n, self.ev[key])
        return instr

    def w(self, eng, key):
        if not self.emitting:
            return
        semname, cnt = self.ev[key]
        eng.wait_ge(self.sems[semname].h, cnt)


class _Null:

    def __getattr__(self, k):
        return lambda *a, **kw: None



def emit_A1(nc, E):
    sLD, sWL, sST, sMM, sSC, sVE = E["sLD"], E["sWL"], E["sST"], E["sMM"], E["sSC"], E["sVE"]
    d_xT, d_xqT, d_wqkv = E["d_xT"], E["d_xqT"], E["d_wqkv"]
    d_kTs, d_Vs, d_qTs = E["d_kTs"], E["d_Vs"], E["d_qTs"]

    NT_KV = NSEQ // A1T
    NT_Q = NQ // A1T
    jobs = [("kv", i) for i in range(NT_KV)] + [("q", i) for i in range(NT_Q)]
    ngrp = [16 if k == "kv" else 8 for k, _ in jobs]
    gbase = np.cumsum([0] + ngrp).tolist()
    NG = gbase[-1]

    sP = [E["sP0"], E["sP1"]]
    sSK, sSV = E["sS0"], E["sS1"]
    bWL, bMM, bSC, bVE = sWL.n, sMM.n, sSC.n, sVE.n
    bP = [sP[0].n, sP[1].n]
    bSK, bSV = sSK.n, sSV.n
    ld_done = [(sP[j % 2], bP[j % 2] + 16 * (j // 2 + 1)) for j in range(len(jobs))]
    cast_done = [bVE + j + 1 for j in range(len(jobs))]
    mm_done = [bMM + q + 1 for q in range(NG)]
    ev_done = [bSC + q + 1 for q in range(NG)]
    stK_done = [bSK + 16 * (j + 1) for j in range(len(jobs))]
    stV_done = [bSV + 16 * (j + 1) for j in range(NT_KV)]

    with (
        nc.sbuf_tensor("a1_w", [128, 8, 3 * D], BF16) as w,
        nc.sbuf_tensor("a1_x", [128, 2, 8, A1T], F32) as xt,
        nc.sbuf_tensor("a1_xb", [128, 2, 8, A1T], BF16) as xtb,
        nc.sbuf_tensor("a1_evK", [128, 8, A1T], BF16) as evK,
        nc.sbuf_tensor("a1_evV", [128, 4, D], BF16) as evV,
        nc.psum_tensor("a1_p0", [128, A1T], F32) as p0,
        nc.psum_tensor("a1_p1", [128, A1T], F32) as p1,
        nc.psum_tensor("a1_p2", [128, A1T], F32) as p2,
        nc.psum_tensor("a1_p3", [128, A1T], F32) as p3,
        nc.Block("A1", no_gpsimd_drain=True) as blk,
    ):
        ps = [p0, p1, p2, p3]

        @blk.sync
        def _(sync):
            for j, (kind, i) in enumerate(jobs):
                if j >= 2:
                    sync.wait_ge(sVE.h, cast_done[j - 2])
                src = d_xT if kind == "kv" else d_xqT
                t0 = i * A1T
                ap = src.rearrange("(c p) t -> p c t", p=128)[:, :, t0:t0 + A1T]
                n = ld_done[j][0].inc(sync.dma_start(xt[:, j % 2], ap), 16)
                assert n == ld_done[j][1]

        @blk.vector
        def _(v):
            for j in range(len(jobs)):
                v.wait_ge(ld_done[j][0].h, ld_done[j][1])
                if j >= 2:
                    v.wait_ge(sMM.h, mm_done[gbase[j - 1] - 1])
                n = sVE.inc(v.tensor_copy(xtb[:, j % 2], xt[:, j % 2]))
                assert n == cast_done[j]

        @blk.tensor
        def _(t):
            t.wait_ge(sWL.h, bWL + 16)
            for j, (kind, i) in enumerate(jobs):
                t.wait_ge(sVE.h, cast_done[j])
                xb = xtb[:, j % 2]
                for grp in range(ngrp[j]):
                    q = gbase[j] + grp
                    if q >= 4:
                        t.wait_ge(sSC.h, ev_done[q - 4])
                    pb = ps[q % 4]
                    last = None
                    if kind == "q" or grp < 8:
                        col0 = (0 if kind == "q" else D) + grp * 128
                        for c in range(8):
                            last = t.matmul(pb[:, :], w[:, c, col0:col0 + 128],
                                            xb[:, c, :], start=(c == 0), stop=(c == 7))
                    else:
                        gg = grp - 8
                        ts0 = (gg // 2) * 128
                        dv0 = 2 * D + (gg % 2) * 512
                        for c in range(8):
                            last = t.matmul(pb[:, :], xb[:, c, ts0:ts0 + 128],
                                            w[:, c, dv0:dv0 + 512],
                                            start=(c == 0), stop=(c == 7))
                    n = sMM.inc(last)
                    assert n == mm_done[q]

        @blk.scalar
        def _(s):
            for j, (kind, i) in enumerate(jobs):
                for grp in range(ngrp[j]):
                    q = gbase[j] + grp
                    s.wait_ge(sMM.h, mm_done[q])
                    if grp == 0 and j > 0:
                        s.wait_ge(sSK.h, stK_done[j - 1])
                    if grp == 8 and j > 0:
                        s.wait_ge(sSV.h, stV_done[j - 1])
                    if kind == "q" or grp < 8:
                        dst = evK[:, grp, :]
                    else:
                        gg = grp - 8
                        dst = evV[:, gg // 2, (gg % 2) * 512:(gg % 2) * 512 + 512]
                    n = sSC.inc(s.activation(dst, ps[q % 4][:, :], AF.Copy))
                    assert n == ev_done[q]

        @blk.gpsimd
        def _(gp):
            n = sWL.inc(gp.dma_start(w[:, :, :],
                                     d_wqkv.rearrange("(c p) n -> p c n", p=128)), 16)
            assert n == bWL + 16
            for j, (kind, i) in enumerate(jobs):
                t0 = i * A1T
                if kind == "kv":
                    gp.wait_ge(sSC.h, ev_done[gbase[j] + 7])
                    ap = d_kTs.rearrange("(m p) t -> p m t", p=128)[:, :, t0:t0 + A1T]
                    n = sSK.inc(gp.dma_start(ap, evK[:, :, :]), 16)
                    assert n == stK_done[j]
                    gp.wait_ge(sSC.h, ev_done[gbase[j] + 15])
                    ap = d_Vs.rearrange("(g p) d -> p g d", p=128)[:, 4 * i:4 * i + 4, :]
                    n = sSV.inc(gp.dma_start(ap, evV[:, :, :]), 16)
                    assert n == stV_done[j]
                else:
                    gp.wait_ge(sSC.h, ev_done[gbase[j] + 7])
                    ap = d_qTs.rearrange("(m p) t -> p m t", p=128)[:, :, t0:t0 + A1T]
                    n = sSK.inc(gp.dma_start(ap, evK[:, :, :]), 16)
                    assert n == stK_done[j]
            gp.wait_ge(sSK.h, stK_done[-1])
            gp.wait_ge(sSV.h, stV_done[-1])



def emit_A2(nc, E):
    sLD, sWL, sST, sMM, sSC, sVE = E["sLD"], E["sWL"], E["sST"], E["sMM"], E["sSC"], E["sVE"]
    d_qTs, d_kTs, d_Vs, d_xqT, d_x1Ts, d_cmask = (
        E["d_qTs"], E["d_kTs"], E["d_Vs"], E["d_xqT"], E["d_x1Ts"], E["d_cmask"])
    c_oc, c_or32 = E["c_oc"], E["c_or32"]

    slots = list(range(NBLK))
    chunk_list = [(u, j) for u in slots for j in range(u + 1)]
    cidx = {uj: i for i, uj in enumerate(chunk_list)}
    sub_list = [(u, j, s) for (u, j) in chunk_list for s in range(8)]
    tidx = {ujs: i for i, ujs in enumerate(sub_list)}

    sPq = [E["sP0"], E["sP1"]]
    sPx = [E["sP2"], E["sP3"]]
    sPk = [E["sP4"], E["sP5"]]
    sPv = [E["sP6"], E["sP7"]]
    sSx = [E["sS0"], E["sS1"]]
    ld_q, ld_x, ld_k, ld_v = {}, {}, {}, {}
    bq = [sPq[0].n, sPq[1].n]; bx = [sPx[0].n, sPx[1].n]
    bk = [sPk[0].n, sPk[1].n]; bv = [sPv[0].n, sPv[1].n]
    for u in slots:
        ld_q[u] = (sPq[u % 2], bq[u % 2] + 16 * (u // 2 + 1))
        ld_x[u] = (sPx[u % 2], bx[u % 2] + 16 * (u // 2 + 1))
    for jj, (u, j) in enumerate(chunk_list):
        ld_k[(u, j)] = (sPk[jj % 2], bk[jj % 2] + 16 * (jj // 2 + 1))
        ld_v[(u, j)] = (sPv[jj % 2], bv[jj % 2] + 16 * (jj // 2 + 1))

    mask_ld = sWL.n + 16

    exp_done, sq_done, ln_done, ex2_done = {}, {}, {}, {}
    c = sSC.n
    for u in slots:
        for j in range(u + 1):
            for s in range(8):
                c += 1; exp_done[(u, j, s)] = c
        c += 1; sq_done[u] = c
        c += 1; ln_done[u] = c
        c += 1; ex2_done[u] = c
    sc_end = c

    msk_done, av_ev_done = {}, {}
    rec_done, comb_done, cast_done, stats_done, fin_done = {}, {}, {}, {}, {}
    c = sVE.n
    c += 1
    ms_done = c
    for u in slots:
        for j in range(u + 1):
            if j == u:
                for s in range(8):
                    c += 1; msk_done[(u, s)] = c
            for w in range(4):
                c += 1; av_ev_done[(u, j, w)] = c
        c += 1; rec_done[u] = c
        c += 1; comb_done[u] = c
        c += 1; cast_done[u] = c
        c += 1; stats_done[u] = c
        c += 1; fin_done[u] = c
    ve_end = c

    sc_grp_done, av_w_done, dn_done = {}, {}, {}
    bc1_done, s1_done, s2_done, bc2_done, bc3_done = {}, {}, {}, {}, {}
    c = sMM.n
    for u in slots:
        for j in range(u + 1):
            for s in range(8):
                c += 1; sc_grp_done[(u, j, s)] = c
            if j == u:
                c += 1; dn_done[u] = c
            for w in range(4):
                c += 1; av_w_done[(u, j, w)] = c
        c += 1; bc1_done[u] = c
        c += 1; s1_done[u] = c
        c += 1; s2_done[u] = c
        c += 1; bc2_done[u] = c
        c += 1; bc3_done[u] = c
    mm_end = c

    st_done = {}
    bs = [sSx[0].n, sSx[1].n]
    for u in slots:
        st_done[u] = (sSx[u % 2], bs[u % 2] + 16 * (u // 2 + 1))

    with (
        nc.sbuf_tensor("a2_qT", [128, 2, 8, QB], BF16) as qT,
        nc.sbuf_tensor("a2_kc", [128, 2, 8, KC], BF16) as kc,
        nc.sbuf_tensor("a2_vc", [128, 2, 8, KC], BF16) as vc,
        nc.sbuf_tensor("a2_PT", [128, 8, QB], BF16) as PT,
        nc.sbuf_tensor("a2_oa", [128, 8, QB], F32) as oa,
        nc.sbuf_tensor("a2_msk", [128, 8, QB], BF16) as msk,
        nc.sbuf_tensor("a2_xr", [128, 2, 8, QB], F32) as xr,
        nc.sbuf_tensor("a2_rT", [128, 8, QB], F32) as rT,
        nc.sbuf_tensor("a2_rTb", [128, 8, QB], BF16) as rTb,
        nc.sbuf_tensor("a2_r2b", [128, 8, QB], BF16) as r2b,
        nc.sbuf_tensor("a2_st", [128, 4, QB], F32) as st,
        nc.sbuf_tensor("a2_tmp", [128, QB], F32) as tmp,
        nc.psum_tensor("a2_pS0", [128, QB], F32) as pS0,
        nc.psum_tensor("a2_pS1", [128, QB], F32) as pS1,
        nc.psum_tensor("a2_pDN", [128, QB], F32) as pDN,
        nc.psum_tensor("a2_pAVa", [128, 2, QB], F32) as pAVa,
        nc.psum_tensor("a2_pAVb", [128, 2, QB], F32) as pAVb,
        nc.psum_tensor("a2_pEP", [128, QB], F32) as pEP,
        nc.Block("A2", no_gpsimd_drain=True) as blk,
    ):
        pS = [pS0, pS1]
        pAV = [pAVa, pAVb]

        def pt_final(u, j, s):
            if j == u:
                return (sVE.h, msk_done[(u, s)])
            return (sSC.h, exp_done[(u, j, s)])

        @blk.sync
        def _(sync):
            for u in slots:
                if u >= 2:
                    sync.wait_ge(sVE.h, fin_done[u - 2])
                    sync.wait_ge(st_done[u - 2][0].h, st_done[u - 2][1])
                ap = d_qTs.rearrange("(m p) t -> p m t", p=128)[:, :, u * QB:(u + 1) * QB]
                n = ld_q[u][0].inc(sync.dma_start(qT[:, u % 2], ap), 16)
                assert n == ld_q[u][1]
                ap = d_xqT.rearrange("(m p) t -> p m t", p=128)[:, :, u * QB:(u + 1) * QB]
                n = ld_x[u][0].inc(sync.dma_start(xr[:, u % 2], ap), 16)
                assert n == ld_x[u][1]
                for j in range(u + 1):
                    jj = cidx[(u, j)]
                    if jj >= 2:
                        sync.wait_ge(sMM.h, av_w_done[chunk_list[jj - 2] + (3,)])
                    ap = d_kTs.rearrange("(m p) t -> p m t", p=128)[:, :, j * KC:(j + 1) * KC]
                    n = ld_k[(u, j)][0].inc(sync.dma_start(kc[:, jj % 2], ap), 16)
                    assert n == ld_k[(u, j)][1]
                    ap = d_Vs.rearrange("(g p) d -> p g d", p=128)[:, 8 * j:8 * j + 8, :]
                    n = ld_v[(u, j)][0].inc(sync.dma_start(vc[:, jj % 2], ap), 16)
                    assert n == ld_v[(u, j)][1]

        @blk.tensor
        def _(t):
            t.wait_ge(sWL.h, mask_ld)
            t.wait_ge(E["sAux"].h, 80)
            for u in slots:
                t.wait_ge(ld_q[u][0].h, ld_q[u][1])
                for j in range(u + 1):
                    jj = cidx[(u, j)]
                    t.wait_ge(ld_k[(u, j)][0].h, ld_k[(u, j)][1])
                    t.wait_ge(ld_v[(u, j)][0].h, ld_v[(u, j)][1])
                    kcb, vcb = kc[:, jj % 2], vc[:, jj % 2]
                    for s in range(8):
                        ti = tidx[(u, j, s)]
                        if ti >= 2:
                            t.wait_ge(sSC.h, exp_done[sub_list[ti - 2]])
                        pb = pS[s % 2]
                        last = None
                        for cc in range(8):
                            last = t.matmul(pb[:, :], kcb[:, cc, 128 * s:128 * s + 128],
                                            qT[:, u % 2, cc, :],
                                            start=(cc == 0), stop=(cc == 7))
                        n = sMM.inc(last); assert n == sc_grp_done[(u, j, s)]
                    for s in range(8):
                        sem, cnt = pt_final(u, j, s)
                        t.wait_ge(sem, cnt)
                        first = (j == 0 and s == 0)
                        if first and u >= 1:
                            t.wait_ge(sVE.h, fin_done[u - 1])
                        lastmm = (j == u and s == 7)
                        ins = t.matmul(pDN[:, :], c_oc[:, :], PT[:, s, :],
                                       start=first, stop=lastmm)
                        if lastmm:
                            n = sMM.inc(ins); assert n == dn_done[u]
                    for w in range(4):
                        gw = 4 * jj + w
                        if gw >= 2:
                            pj, pw = divmod(gw - 2, 4)
                            t.wait_ge(sVE.h, av_ev_done[chunk_list[pj] + (pw,)])
                        pv = pAV[w % 2]
                        for mi in range(2):
                            dvc = 2 * w + mi
                            last = None
                            for s in range(8):
                                last = t.matmul(pv[:, mi, :],
                                                vcb[:, s, 128 * dvc:128 * dvc + 128],
                                                PT[:, s, :],
                                                start=(s == 0), stop=(s == 7))
                        n = sMM.inc(last); assert n == av_w_done[(u, j, w)]
                t.wait_ge(sVE.h, rec_done[u])
                ins = t.matmul(pEP[:, :], c_or32[:, :], st[:, 0, :], start=True, stop=True)
                n = sMM.inc(ins); assert n == bc1_done[u]
                t.wait_ge(sVE.h, cast_done[u])
                last = None
                for cc in range(8):
                    last = t.matmul(pEP[:, :], c_oc[:, :], rTb[:, cc, :],
                                    start=(cc == 0), stop=(cc == 7))
                n = sMM.inc(last); assert n == s1_done[u]
                t.wait_ge(sSC.h, sq_done[u])
                last = None
                for cc in range(8):
                    last = t.matmul(pDN[:, :], c_oc[:, :], r2b[:, cc, :],
                                    start=(cc == 0), stop=(cc == 7))
                n = sMM.inc(last); assert n == s2_done[u]
                t.wait_ge(sVE.h, stats_done[u])
                ins = t.matmul(pEP[:, :], c_or32[:, :], st[:, 3, :], start=True, stop=True)
                n = sMM.inc(ins); assert n == bc2_done[u]
                t.wait_ge(sSC.h, ex2_done[u])
                ins = t.matmul(pDN[:, :], c_or32[:, :], st[:, 2, :], start=True, stop=True)
                n = sMM.inc(ins); assert n == bc3_done[u]

        @blk.scalar
        def _(s_):
            s_.wait_ge(E["sAux"].h, 80)
            for u in slots:
                for j in range(u + 1):
                    jj = cidx[(u, j)]
                    for s in range(8):
                        s_.wait_ge(sMM.h, sc_grp_done[(u, j, s)])
                        if jj >= 1:
                            s_.wait_ge(sMM.h, av_w_done[chunk_list[jj - 1] + (3,)])
                        ins = s_.activation(PT[:, s, :], pS[s % 2][:, :], AF.Exp, scale=SCL)
                        n = sSC.inc(ins); assert n == exp_done[(u, j, s)]
                s_.wait_ge(sVE.h, comb_done[u])
                ins = s_.activation(r2b[:, :, :], rT[:, :, :], AF.Square)
                n = sSC.inc(ins); assert n == sq_done[u]
                s_.wait_ge(sVE.h, stats_done[u])
                ins = s_.activation(tmp[0:1, :], st[0:1, 1, :], AF.Ln, bias=LN_EPS)
                n = sSC.inc(ins); assert n == ln_done[u]
                ins = s_.activation(st[0:1, 2, :], tmp[0:1, :], AF.Exp, scale=-0.5)
                n = sSC.inc(ins); assert n == ex2_done[u]

        @blk.vector
        def _(v):
            n = sVE.inc(v.memset(st[:, :, :], 0.0)); assert n == ms_done
            for u in slots:
                for j in range(u + 1):
                    jj = cidx[(u, j)]
                    if j == u:
                        for s in range(8):
                            v.wait_ge(sSC.h, exp_done[(u, j, s)])
                            ins = v.tensor_tensor(PT[:, s, :], PT[:, s, :], msk[:, s, :],
                                                  ALU.mult)
                            n = sVE.inc(ins); assert n == msk_done[(u, s)]
                    for w in range(4):
                        v.wait_ge(sMM.h, av_w_done[(u, j, w)])
                        pv = pAV[w % 2]
                        ins = None
                        for mi in range(2):
                            dvc = 2 * w + mi
                            if j == 0:
                                ins = v.tensor_copy(oa[:, dvc, :], pv[:, mi, :])
                            else:
                                ins = v.tensor_tensor(oa[:, dvc, :], pv[:, mi, :],
                                                      oa[:, dvc, :], ALU.add)
                        n = sVE.inc(ins); assert n == av_ev_done[(u, j, w)]
                v.wait_ge(sMM.h, dn_done[u])
                ins = v.reciprocal(st[0:1, 0, :], pDN[0:1, :])
                n = sVE.inc(ins); assert n == rec_done[u]
                v.wait_ge(sMM.h, bc1_done[u])
                v.wait_ge(ld_x[u][0].h, ld_x[u][1])
                ins = None
                for cc in range(8):
                    v.tensor_tensor(rT[:, cc, :], oa[:, cc, :], pEP[:, :], ALU.mult)
                    ins = v.tensor_tensor(rT[:, cc, :], rT[:, cc, :],
                                          xr[:, u % 2, cc, :], ALU.add)
                n = sVE.inc(ins); assert n == comb_done[u]
                ins = v.tensor_copy(rTb[:, :, :], rT[:, :, :])
                n = sVE.inc(ins); assert n == cast_done[u]
                v.wait_ge(sMM.h, s1_done[u])
                v.wait_ge(sMM.h, s2_done[u])
                v.tensor_scalar(st[0:1, 3, :], pEP[0:1, :], -1.0 / D, None,
                                ALU.mult, ALU.bypass)
                v.tensor_scalar(st[0:1, 1, :], pEP[0:1, :], 1.0 / D, None,
                                ALU.mult, ALU.bypass)
                v.tensor_tensor(tmp[0:1, :], st[0:1, 1, :], st[0:1, 1, :], ALU.mult)
                v.tensor_scalar(st[0:1, 1, :], pDN[0:1, :], 1.0 / D, None,
                                ALU.mult, ALU.bypass)
                ins = v.tensor_tensor(st[0:1, 1, :], st[0:1, 1, :], tmp[0:1, :],
                                      ALU.subtract)
                n = sVE.inc(ins); assert n == stats_done[u]
                v.wait_ge(sMM.h, bc2_done[u])
                v.wait_ge(sMM.h, bc3_done[u])
                ins = None
                for cc in range(8):
                    v.tensor_tensor(tmp[:, :], rT[:, cc, :], pEP[:, :], ALU.add)
                    ins = v.tensor_tensor(xr[:, u % 2, cc, :], tmp[:, :], pDN[:, :],
                                          ALU.mult)
                n = sVE.inc(ins); assert n == fin_done[u]

        @blk.gpsimd
        def _(gp):
            n = sWL.inc(gp.dma_start(msk[:, :, :],
                                     d_cmask.rearrange("s p j -> p s j")), 16)
            assert n == mask_ld
            for u in slots:
                gp.wait_ge(sVE.h, fin_done[u])
                ap = d_x1Ts.rearrange("(m p) t -> p m t", p=128)[:, :, u * QB:(u + 1) * QB]
                n = st_done[u][0].inc(gp.dma_start(ap, xr[:, u % 2]), 16)
                assert n == st_done[u][1]
            gp.wait_ge(st_done[slots[-2]][0].h, st_done[slots[-2]][1])
            gp.wait_ge(st_done[slots[-1]][0].h, st_done[slots[-1]][1])


F1_NT_OVERRIDE = None


def emit_F1(nc, E):
    sems = {k: E[k] for k in ("sLD", "sWL", "sST", "sMM", "sSC", "sVE", "sVX", "sAux",
                              "sP0", "sP1", "sS0")}
    sLD, sWL, sST, sMM, sSC, sVE, sVX = (E[k] for k in
                                         ("sLD", "sWL", "sST", "sMM", "sSC", "sVE", "sVX"))
    c_id, c_oc, c_or32, c_ordt32, c_pair, c_iota = (
        E["c_id"], E["c_oc"], E["c_or32"], E["c_ordt32"], E["c_pair"], E["c_iota"])
    d_x1Ts, d_intTs = E["d_x1Ts"], E["d_intTs"]

    NT = F1_NT_OVERRIDE or (NQ // F1T)
    T = F1T

    with contextlib.ExitStack() as es:
        ec = es.enter_context
        sb = lambda nm, shp, dt: ec(nc.sbuf_tensor(nm, shp, dt))
        w_r1 = sb("f1_wr1", [128, 8, 256], BF16)
        w_r2 = sb("f1_wr2", [128, 2, 128], BF16)
        w_h1 = sb("f1_wh1", [128, 8, 256], BF16)
        w_h2 = sb("f1_wh2", [128, 2, 128], BF16)
        w_e1 = sb("f1_we1", [128, 16, 512], BF16)
        w_e2 = sb("f1_we2", [128, 4, 3072], BF16)
        w_eb2 = sb("f1_web2", [128, 3072], BF16)
        w_s1 = sb("f1_ws1", [128, 16, 128], BF16)
        w_s2 = sb("f1_ws2", [128, 3072], BF16)
        b_r1 = sb("f1_br1", [128, 2], F32)
        b_r2 = sb("f1_br2", [128, 1], F32)
        b_h1 = sb("f1_bh1", [128, 2], F32)
        b_h2 = sb("f1_bh2", [1, 1], F32)
        b_e1 = sb("f1_be1", [128, 4], F32)
        b_s1 = sb("f1_bs1", [128, 1], F32)
        b_s2 = sb("f1_bs2", [128, 24], F32)
        b_mun = sb("f1_bmun", [128, 8], F32)
        c_shw = sb("f1_shw", [128, 1], F32)
        sw0 = sb("f1_sw0", [128, 1], F32)
        sw1c = sb("f1_sw1", [128, 1], F32)
        x1 = sb("f1_x1", [128, 2, 8, T], F32)
        intT = sb("f1_int", [128, 8, T], F32)
        vT = sb("f1_v", [128, 8, T], F32)
        intTb = sb("f1_intb", [128, 2, 8, T], BF16)
        vTb = sb("f1_vb", [128, 8, T], BF16)
        xm = sb("f1_xm", [128, 8, T], F32)
        xms = sb("f1_xms", [128, 8, T], F32)
        b_muns = sb("f1_bmuns", [128, 8], F32)
        spq = sb("f1_spq", [128, 3, 8, T], F32)
        sgA = sb("f1_sga", [128, 2, 6, T], F32)
        sgG = sb("f1_sgg", [128, 2, 6, T], F32)
        hEb = sb("f1_heb", [128, 4, T], BF16)
        hm1 = sb("f1_hm1", [128, 4, T], BF16)
        hshb = sb("f1_hshb", [128, T], BF16)
        hHb = sb("f1_hhb", [128, 2, T], BF16)
        hr = sb("f1_hr", [128, 2, T], BF16)
        lg = sb("f1_lg", [128, T], F32)
        lgt = sb("f1_lgt", [128, 2, 128], F32)
        mx = sb("f1_mx", [128, 2, 8], F32)
        mx2 = sb("f1_mx2", [128, 2, 8], F32)
        ixu = sb("f1_ixu", [128, 2, 8], U32)
        ix2u = sb("f1_ix2u", [128, 2, 8], U32)
        iif = sb("f1_iif", [128, 4], F32)
        l2t = sb("f1_l2", [128, 2, 8], F32)
        sel1f = sb("f1_s1f", [128, 2, 128], F32)
        sel2f = sb("f1_s2f", [128, 2, 128], F32)
        sttm = sb("f1_sttm", [128, 2, 128], F32)
        dstat = sb("f1_dst", [128, 256], F32)
        wsrc = sb("f1_wsrc", [128, 256], F32)
        wsrc2 = sb("f1_wsrc2", [128, 256], F32)
        hsrc = sb("f1_hsrc", [128, 256], F32)
        selT1 = sb("f1_selT1", [128, 256], BF16)
        selT2 = sb("f1_selT2", [128, 256], BF16)
        selbc1 = sb("f1_sbc1", [128, 4, T], BF16)
        selbc2 = sb("f1_sbc2", [128, 4, T], BF16)
        wAb = sb("f1_wab", [128, T], F32)
        wBb = sb("f1_wbb", [128, T], F32)
        hlb = sb("f1_hlb", [128, T], F32)
        hlDTb = sb("f1_hldtb", [128, T], F32)
        tmpA = sb("f1_tmpa", [128, 2, T], F32)
        tmpB = sb("f1_tmpb", [128, 2, T], F32)
        etmp = sb("f1_etmp", [128, 2, T], F32)
        probin = [ec(nc.psum_tensor(f"f1_pr{i}", [128, 2, T], F32)) for i in range(8)]

        P = Plan(sems)

        def tile_items(t):
            items = [("r1", None)]
            items += [("he0", 0), ("he1", 0), ("hsh", 0), ("hh", 0)]
            items += [("lgmm", None), ("lgtr", None), ("seltr", None)]
            items += [("pair", (p, pp)) for p in range(2) for pp in range(4)]
            for it in range(2):
                if it == 1:
                    items += [("he0", 1), ("he1", 1), ("hsh", 1), ("hh", 1)]
                for cp in range(4):
                    for path in range(3):
                        items.append((f"b{it}", (cp, path)))
                items.append((f"hw{it}", None))
                items.append((f"hlbc{it}", 0))
                items.append((f"hlbc{it}", 1))
                if it == 0:
                    items.append(("wbc", 0))
                    items.append(("wbc", 1))
                for cp in range(4):
                    for rng in ("a", "g"):
                        for path in range(3):
                            items.append((f"s{it}", (cp, rng, path)))
            return items

        all_items = []
        for t in range(NT):
            for tag, arg in tile_items(t):
                all_items.append((t, tag, arg))
        item_idx = {k: i for i, k in enumerate(all_items)}

        VEC_EVAC = {"lgtr": False, "seltr": False}

        def run_streams(eng, part="all"):
            sy, gp, te, sc, ve = (eng[k] for k in ("sync", "gpsimd", "tensor",
                                                   "scalar", "vector"))

            if part in ("all", "sync"):
              s16 = lambda key, dst, src_: P.e(key, "sAux", sy.dma_start(dst, src_), 16)
              s16(("bl", 0), b_r1[:, :], E["d_rb1"][:, :])
              s16(("bl", 1), b_r2[:, :], E["d_rb2"][:, :])
              s16(("bl", 2), b_h1[:, :], E["d_hb1"][:, :])
              s16(("bl", 3), b_h2[:, :], E["d_hb2"][:, :])
              s16(("bl", 4), b_e1[:, :], E["d_eb1"][:, :])
              s16(("bl", 5), b_s1[:, :], E["d_sb1"][:, :])
              s16(("bl", 6), b_s2[:, :], E["d_sb2"][:, :])
              s16(("bl", 7), b_mun[:, :], E["d_mun"][:, :])
              s16(("bl", 8), c_shw[:, :], E["d_shw"][:, :])
              for t in range(NT):
                if t >= 2:
                    P.w(sy, ("el", t - 2, 0))
                ap = d_x1Ts.rearrange("(c p) t -> p c t", p=128)[:, :, t * T:(t + 1) * T]
                P.e(("x1ld", t), f"sP{t % 2}", sy.dma_start(x1[:, t % 2], ap), 16)

            if part not in ("all", "gpsimd"):
                pass
            else:
              g16 = lambda key, dst, src: P.e(key, "sWL", gp.dma_start(dst, src), 16)
              rr = lambda d, p=128: d.rearrange("(c p) n -> p c n", p=p)
              g16(("wl", 0), w_r1[:, :, :], rr(E["d_rw1"]))
              g16(("wl", 1), w_r2[:, :, :], rr(E["d_rw2"]))
              g16(("wl", 2), w_h1[:, :, :], rr(E["d_hw1"]))
              g16(("wl", 3), w_h2[:, :, :], rr(E["d_hw2"]))
              g16(("wl", 4), w_e1[:, :, :], rr(E["d_ew1"]))
              g16(("wl", 5), w_e2[:, :, :], rr(E["d_ew2"]))
              g16(("wl", 6), w_eb2[:, :], E["d_eb2"][:, :])
              g16(("wl", 7), w_s1[:, :, :], rr(E["d_sw1"]))
              g16(("wl", 8), w_s2[:, :], E["d_sw2"][:, :])
              for t in range(NT):
                P.w(gp, ("icast", t))
                ap = d_intTs.rearrange("(c p) t -> p c t", p=128)[:, :, t * T:(t + 1) * T]
                P.e(("ist", t), "sS0", gp.dma_start(ap, intTb[:, t % 2]), 16)

            def robin(t, tag, arg):
                idx = item_idx[(t, tag, arg)]
                if idx >= 8:
                    P.w(te, ("rev",) + all_items[idx - 8])
                return probin[idx % 8]

            def rbank(t, tag, arg):
                return probin[item_idx[(t, tag, arg)] % 8]

            if part in ("all", "tensor"):
              P.w(te, ("wl", 8))
              P.w(te, ("bl", 8))
              P.w(te, ("vmem",))
              for t in range(NT):
                  tb = t % 2
                  P.w(te, ("cast0", t))
                  pb = robin(t, "r1", None)
                  for m in range(2):
                      last = None
                      for c in range(8):
                          last = te.matmul(pb[:, m, :], w_r1[:, c, m * 128:(m + 1) * 128],
                                           intTb[:, t % 2, c, :], start=(c == 0), stop=(c == 7))
                      if m == 1:
                          P.e(("mm", t, "r1", None), "sMM", last)
                  def s1chunks(it):
                      if it == 0:
                          return [(cp, [2 * cp, 2 * cp + 1]) for cp in range(4)]
                      return [(cp, [2 * cp, 2 * cp + 1, 8 + 2 * cp, 9 + 2 * cp])
                              for cp in range(4)]

                  def stage1(it):
                      for half in range(2):
                          pb2 = robin(t, f"he{half}", it)
                          last = None
                          for m in (2 * half, 2 * half + 1):
                              seq = []
                              for cp, cl in s1chunks(it):
                                  for c in cl:
                                      seq.append((cp, c))
                              for k, (cp, c) in enumerate(seq):
                                  if it == 1 and m == 2 * half and half == 0 and c % 2 == 0 and c < 8:
                                      P.w(te, ("cast1p", t, cp))
                                  rhs = intTb[:, t % 2, c, :] if c < 8 else vTb[:, c - 8, :]
                                  last = te.matmul(pb2[:, m % 2, :],
                                                   w_e1[:, c, m * 128:(m + 1) * 128],
                                                   rhs, start=(k == 0),
                                                   stop=(k == len(seq) - 1))
                          P.e(("mm", t, f"he{half}", it), "sMM", last)
                      pb2 = robin(t, "hsh", it)
                      seq = [(cp, c) for cp, cl in s1chunks(it) for c in cl]
                      last = None
                      for k, (cp, c) in enumerate(seq):
                          rhs = intTb[:, t % 2, c, :] if c < 8 else vTb[:, c - 8, :]
                          last = te.matmul(pb2[:, 0, :], w_s1[:, c, :], rhs,
                                           start=(k == 0), stop=(k == len(seq) - 1))
                      P.e(("mm", t, "hsh", it), "sMM", last)
                      pb2 = robin(t, "hh", it)
                      last = None
                      for m in range(2):
                          for c in range(8):
                              last = te.matmul(pb2[:, m, :], w_h1[:, c, m * 128:(m + 1) * 128],
                                               intTb[:, t % 2, c, :], start=(c == 0), stop=(c == 7))
                      P.e(("mm", t, "hh", it), "sMM", last)

                  stage1(0)
                  P.w(te, ("hrev", t))
                  pb = robin(t, "lgmm", None)
                  last = None
                  for cc in range(2):
                      last = te.matmul(pb[:, 0, :], w_r2[:, cc, :], hr[:, cc, :],
                                       start=(cc == 0), stop=(cc == 1))
                  P.e(("mm", t, "lgmm", None), "sMM", last)
                  P.w(te, ("lgev", t))
                  pb = robin(t, "lgtr", None)
                  for j in range(2):
                      last = te.transpose(pb[:, j, 0:128], lg[:, j * 128:(j + 1) * 128],
                                          c_id[:, :])
                  P.e(("mm", t, "lgtr", None), "sMM", last)
                  P.w(te, ("topk", t))
                  pb = robin(t, "seltr", None)
                  for j in range(2):
                      last = te.transpose(pb[:, 0, 128 * j:128 * j + 128],
                                          sel1f[:, j, :], c_id[:, :])
                  for j in range(2):
                      last = te.transpose(pb[:, 1, 128 * j:128 * j + 128],
                                          sel2f[:, j, :], c_id[:, :])
                  P.e(("mm", t, "seltr", None), "sMM", last)
                  for p in range(2):
                      for pp in range(4):
                          pb = robin(t, "pair", (p, pp))
                          if p == 0 and pp == 0:
                              for j in range(2):
                                  te.transpose(pb[:, 1, 128 * j:128 * j + 128],
                                               sttm[:, j, :], c_id[:, :])
                          P.w(te, ("selev", t))
                          selT = selT1 if p == 0 else selT2
                          last = te.matmul(pb[:, 0, :], c_pair[:, pp, :], selT[:, :],
                                           start=True, stop=True)
                          P.e(("mm", t, "pair", (p, pp)), "sMM", last)
                  for it in range(2):
                      if it == 1:
                          stage1(1)
                      for cp in range(4):
                          for path in range(3):
                              pb = robin(t, f"b{it}", (cp, path))
                              if path < 2:
                                  P.w(te, ("mask", t, it, path))
                              else:
                                  P.w(te, ("hshev", t, it))
                              last = None
                              for cj in range(2):
                                  c = 2 * cp + cj
                                  fcol = 8 + c
                                  if path < 2:
                                      hmx = hm1 if path == 0 else hEb
                                      selT = selT1 if path == 0 else selT2
                                      for hc in range(4):
                                          last = te.matmul(pb[:, cj, :],
                                                           w_e2[:, hc, 128 * fcol:128 * fcol + 128],
                                                           hmx[:, hc, :],
                                                           start=(hc == 0), stop=False)
                                      last = te.matmul(pb[:, cj, :],
                                                       w_eb2[:, 128 * fcol:128 * fcol + 128],
                                                       selT[:, :], start=False, stop=True)
                                  else:
                                      last = te.matmul(pb[:, cj, :],
                                                       w_s2[:, 128 * fcol:128 * fcol + 128],
                                                       hshb[:, :], start=True, stop=True)
                              P.e(("mm", t, f"b{it}", (cp, path)), "sMM", last)
                      pb = robin(t, f"hw{it}", None)
                      P.w(te, ("hhev", t, it))
                      last = None
                      for cc in range(2):
                          last = te.matmul(pb[:, 0, :], w_h2[:, cc, :], hHb[:, cc, :],
                                           start=(cc == 0), stop=(cc == 1))
                      P.e(("mm", t, f"hw{it}", None), "sMM", last)
                      P.w(te, ("hsg", t, it))
                      for k in range(2):
                          pb = robin(t, f"hlbc{it}", k)
                          lhs = c_or32 if k == 0 else c_ordt32
                          last = te.matmul(pb[:, 0, :], lhs[:, :], hsrc[:, :],
                                           start=True, stop=True)
                          P.e(("mm", t, f"hlbc{it}", k), "sMM", last)
                      if it == 0:
                          P.w(te, ("wrow", t))
                          for k in range(2):
                              pb = robin(t, "wbc", k)
                              wsx = wsrc if k == 0 else wsrc2
                              last = te.matmul(pb[:, 0, :], c_or32[:, :], wsx[:, :],
                                               start=True, stop=True)
                              P.e(("mm", t, "wbc", k), "sMM", last)
                      for cp in range(4):
                          for rng in ("a", "g"):
                              for path in range(3):
                                  pb = robin(t, f"s{it}", (cp, rng, path))
                                  last = None
                                  for cj in range(2):
                                      c = 2 * cp + cj
                                      fcol = c if rng == "a" else 16 + c
                                      if path < 2:
                                          hmx = hm1 if path == 0 else hEb
                                          selT = selT1 if path == 0 else selT2
                                          for hc in range(4):
                                              last = te.matmul(pb[:, cj, :],
                                                               w_e2[:, hc, 128 * fcol:128 * fcol + 128],
                                                               hmx[:, hc, :],
                                                               start=(hc == 0), stop=False)
                                          last = te.matmul(pb[:, cj, :],
                                                           w_eb2[:, 128 * fcol:128 * fcol + 128],
                                                           selT[:, :], start=False, stop=True)
                                      else:
                                          last = te.matmul(pb[:, cj, :],
                                                           w_s2[:, 128 * fcol:128 * fcol + 128],
                                                           hshb[:, :], start=True, stop=True)
                                  P.e(("mm", t, f"s{it}", (cp, rng, path)), "sMM", last)

            if part in ("all", "scalar"):
              P.w(sc, ("bl", 8))
              ins = sc.activation(sw0[:, :], c_shw[:, :], AF.Sigmoid)
              P.e(("sw0",), "sSC", ins)
              for t in range(NT):
                  for it in range(2):
                      if it == 0:
                          P.w(sc, ("mm", t, "r1", None))
                          ins = None
                          for m in range(2):
                              ins = sc.activation(hr[:, m, :],
                                                  probin[item_idx[(t, "r1", None)] % 8][:, m, :],
                                                  AF.Gelu, bias=b_r1[:, m:m + 1])
                          P.e(("hrev", t), "sSC", ins)
                          P.e(("rev", t, "r1", None), "sSC", ins, 0)
                      for half in range(2):
                          P.w(sc, ("mm", t, f"he{half}", it))
                          pb2 = probin[item_idx[(t, f"he{half}", it)] % 8]
                          ins = None
                          for m in range(2):
                              ins = sc.activation(hEb[:, 2 * half + m, :], pb2[:, m, :],
                                                  AF.Gelu, bias=b_e1[:, 2 * half + m:2 * half + m + 1])
                          P.e(("rev", t, f"he{half}", it), "sSC", ins)
                      P.e(("heev", t, it), "sSC", ins, 0)
                      P.w(sc, ("mm", t, "hsh", it))
                      pb2 = probin[item_idx[(t, "hsh", it)] % 8]
                      ins = sc.activation(hshb[0:64, :], pb2[0:64, 0, :], AF.Gelu,
                                          bias=b_s1[0:64, 0:1])
                      P.e(("rev", t, "hsh", it), "sSC", ins)
                      P.e(("hshev", t, it), "sSC", ins, 0)
                      P.w(sc, ("mm", t, "hh", it))
                      pb2 = probin[item_idx[(t, "hh", it)] % 8]
                      ins = None
                      for m in range(2):
                          ins = sc.activation(hHb[:, m, :], pb2[:, m, :], AF.Gelu,
                                              bias=b_h1[:, m:m + 1])
                      P.e(("rev", t, "hh", it), "sSC", ins)
                      P.e(("hhev", t, it), "sSC", ins, 0)
                      if it == 0:
                          P.w(sc, ("mm", t, "lgmm", None))
                          pb = probin[item_idx[(t, "lgmm", None)] % 8]
                          ins = sc.activation(lg[:, :], pb[:, 0, :], AF.Identity,
                                              bias=b_r2[:, 0:1])
                          P.e(("lgev", t), "sSC", ins)
                          P.e(("rev", t, "lgmm", None), "sSC", ins, 0)
                          P.w(sc, ("mm", t, "lgtr", None))
                          pb = probin[item_idx[(t, "lgtr", None)] % 8]
                          ins = None
                          for j in range(2):
                              ins = sc.activation(lgt[:, j, :], pb[:, j, 0:128], AF.Copy)
                          P.e(("lgtev", t), "sSC", ins)
                          P.e(("rev", t, "lgtr", None), "sSC", ins, 0)
                          P.w(sc, ("mm", t, "seltr", None))
                          pb = probin[item_idx[(t, "seltr", None)] % 8]
                          ins = None
                          for j in range(2):
                              ins = sc.activation(selT1[:, 128 * j:128 * j + 128],
                                                  pb[:, 0, 128 * j:128 * j + 128], AF.Copy)
                              ins = sc.activation(selT2[:, 128 * j:128 * j + 128],
                                                  pb[:, 1, 128 * j:128 * j + 128], AF.Copy)
                          P.e(("selev", t), "sSC", ins)
                          P.e(("rev", t, "seltr", None), "sSC", ins, 0)
                          P.w(sc, ("mm", t, "pair", (0, 0)))
                          pb = probin[item_idx[(t, "pair", (0, 0))] % 8]
                          ins = None
                          for j in range(2):
                              ins = sc.activation(dstat[:, 128 * j:128 * j + 128],
                                                  pb[:, 1, 128 * j:128 * j + 128], AF.Copy)
                          P.e(("dstev", t), "sSC", ins)
                          ins = None
                          for p in range(2):
                              for pp in range(4):
                                  P.w(sc, ("mm", t, "pair", (p, pp)))
                                  pb = probin[item_idx[(t, "pair", (p, pp))] % 8]
                                  dst = (selbc1 if p == 0 else selbc2)[:, pp, :]
                                  ins = sc.activation(dst, pb[:, 0, :], AF.Copy)
                                  P.e(("rev", t, "pair", (p, pp)), "sSC", ins)
                          P.e(("selbcev", t), "sSC", ins, 0)
                      if it == 1:
                          P.w(sc, ("el", t, 0))
                      P.w(sc, ("sw1",))
                      ins = None
                      for c in range(8):
                          src = x1[:, t % 2, c, :] if it == 0 else intT[:, c, :]
                          ins = sc.activation(xm[:, c, :], src, AF.Identity,
                                              bias=b_mun[:, c:c + 1])
                          ins = sc.activation(xms[:, c, :], src, AF.Identity,
                                              scale=sw0[:, 0:1], bias=b_muns[:, c:c + 1])
                      P.e(("xmv", t, it), "sSC", ins)
                      for cp in range(4):
                          for path in range(3):
                              P.w(sc, ("mm", t, f"b{it}", (cp, path)))
                              pb = probin[item_idx[(t, f"b{it}", (cp, path))] % 8]
                              cs_ = slice(2 * cp, 2 * cp + 2)
                              sc.activation(etmp[:, :, :], pb[:, :, :], AF.Exp)
                              ins = sc.activation(spq[:, path, cs_, :], etmp[:, :, :],
                                                  AF.Ln, bias=1.0)
                              P.e(("rev", t, f"b{it}", (cp, path)), "sSC", ins)
                      P.w(sc, ("mm", t, f"hw{it}", None))
                      pb = probin[item_idx[(t, f"hw{it}", None)] % 8]
                      ins = sc.activation(hsrc[0:1, :], pb[0:1, 0, :], AF.Sigmoid,
                                          bias=b_h2[0:1, :])
                      P.e(("hsg", t, it), "sSC", ins)
                      P.e(("rev", t, f"hw{it}", None), "sSC", ins, 0)
                      if it == 0:
                          P.w(sc, ("dstev", t))
                          ins = sc.activation(wsrc[0:1, :], dstat[0:1, :], AF.Sigmoid)
                          P.e(("wsg", t), "sSC", ins)
                      for cp in range(4):
                          for rng in ("a", "g"):
                              for path in range(3):
                                  P.w(sc, ("mm", t, f"s{it}", (cp, rng, path)))
                                  pb = probin[item_idx[(t, f"s{it}", (cp, rng, path))] % 8]
                                  sgx = sgA if rng == "a" else sgG
                                  g = (2 * t + it) * 4 + cp
                                  if g >= 2:
                                      t3, r3 = divmod(g - 2, 8)
                                      it3, cp3 = divmod(r3, 4)
                                      P.w(sc, ("ew", t3, it3, cp3, rng))
                                  dst = sgx[:, ((2 * t + it) * 4 + cp) % 2,
                                            2 * path:2 * path + 2, :]
                                  ins = sc.activation(dst, pb[:, :, :], AF.Sigmoid)
                                  P.e(("rev", t, f"s{it}", (cp, rng, path)), "sSC", ins)
                              P.e(("sgev", t, it, cp, rng), "sSC", ins, 0)

            if part in ("all", "vector"):
              ins = ve.memset(sel1f[:, :, :], 0.0)
              ins = ve.memset(sel2f[:, :, :], 0.0)
              ins = ve.memset(sttm[:, :, :], 0.0)
              ins = ve.memset(wsrc[:, :], 0.0)
              ins = ve.memset(wsrc2[:, :], 0.0)
              ve.memset(hsrc[:, :], 0.0)
              ve.memset(hshb[:, :], 0.0)
              ins = ve.memset(hshb[64:65, :], 1.0)
              P.e(("vmem",), "sVE", ins)
              if P.emitting:
                  ve.wait_ge(E["sAux"].h, 80)
              P.w(ve, ("sw0",))
              ins = ve.tensor_scalar(sw1c[:, :], sw0[:, :], -1.0, 1.0, ALU.mult, ALU.add)
              ins = ve.tensor_scalar(b_muns[:, :], b_mun[:, :], sw0[:, 0:1], None,
                                     ALU.mult, ALU.bypass)
              P.e(("sw1",), "sVE", ins)
              def cast0(t):
                  P.w(ve, ("x1ld", t))
                  if t >= 2:
                      P.w(ve, ("ist", t - 2))
                  ins = ve.tensor_copy(intTb[:, t % 2], x1[:, t % 2])
                  P.e(("cast0", t), "sVE", ins)
              cast0(0)
              for t in range(NT):
                  P.w(ve, ("lgtev", t))
                  if t >= 1:
                      P.w(ve, ("mm", t - 1, "seltr", None))
                      P.w(ve, ("mm", t - 1, "pair", (0, 0)))
                  for j in range(2):
                      sv = lambda key, ins_: P.e(key, "sVX", ins_)
                      sv(("tk", t, j, 0), ve.max(mx[:, j, :], lgt[:, j, 0:8]))
                      P.w(ve, ("tk", t, j, 0))
                      sv(("tk", t, j, 1), ve.max_index(ixu[:, j, :], mx[:, j, :],
                                                       lgt[:, j, 0:8]))
                      P.w(ve, ("tk", t, j, 1))
                      sv(("tk", t, j, 2), ve.tensor_copy(iif[:, 0:1], ixu[:, j, 0:1]))
                      P.w(ve, ("tk", t, j, 2))
                      ve.tensor_scalar(sel1f[:, j, 0:8], c_iota[:, :], iif[:, 0:1], None,
                                       ALU.is_equal, ALU.bypass)
                      sv(("tk", t, j, 3),
                         ve.scalar_tensor_tensor(l2t[:, j, :], sel1f[:, j, 0:8], -1e30,
                                                 lgt[:, j, 0:8], ALU.mult, ALU.add))
                      P.w(ve, ("tk", t, j, 3))
                      sv(("tk", t, j, 4), ve.max(mx2[:, j, :], l2t[:, j, :]))
                      P.w(ve, ("tk", t, j, 4))
                      sv(("tk", t, j, 5), ve.max_index(ix2u[:, j, :], mx2[:, j, :],
                                                       l2t[:, j, :]))
                      P.w(ve, ("tk", t, j, 5))
                      sv(("tk", t, j, 6), ve.tensor_copy(iif[:, 1:2], ix2u[:, j, 0:1]))
                      P.w(ve, ("tk", t, j, 6))
                      ve.tensor_scalar(sel2f[:, j, 0:8], c_iota[:, :], iif[:, 1:2], None,
                                       ALU.is_equal, ALU.bypass)
                      ins = ve.tensor_tensor(sttm[:, j, 0:1], mx[:, j, 0:1],
                                             mx2[:, j, 0:1], ALU.subtract)
                  P.e(("topk", t), "sVE", ins)
                  for it in range(2):
                      P.w(ve, ("heev", t, it))
                      P.w(ve, ("selbcev", t))
                      ve.tensor_tensor(hm1[:, :, :], hEb[:, :, :], selbc1[:, :, :], ALU.mult)
                      ins = ve.tensor_tensor(hEb[:, :, :], hEb[:, :, :], selbc2[:, :, :],
                                             ALU.mult)
                      P.e(("mask", t, it, 0), "sVE", ins, 1)
                      P.e(("mask", t, it, 1), "sVE", ins, 0)
                      P.w(ve, ("xmv", t, it))
                      ins = None
                      for path in range(3):
                          P.w(ve, ("rev", t, f"b{it}", (3, path)))
                          xmx = xm if path < 2 else xms
                          ins = ve.tensor_tensor(spq[:, path, :, :], spq[:, path, :, :],
                                                 xmx[:, :, :], ALU.mult)
                      P.e(("qrdy", t, it), "sVE", ins)
                      for k in range(2):
                          P.w(ve, ("mm", t, f"hlbc{it}", k))
                          pb = probin[item_idx[(t, f"hlbc{it}", k)] % 8]
                          dst = hlb if k == 0 else hlDTb
                          ins = ve.tensor_copy(dst[:, :], pb[:, 0, :])
                          P.e(("rev", t, f"hlbc{it}", k), "sVE", ins)
                      if it == 0:
                          P.w(ve, ("wsg", t))
                          ve.tensor_scalar(wsrc2[0:1, :], wsrc[0:1, :], -1.0, 1.0,
                                           ALU.mult, ALU.add)
                          ve.tensor_scalar(wsrc2[0:1, :], wsrc2[0:1, :], sw1c[0:1, 0:1],
                                           None, ALU.mult, ALU.bypass)
                          ins = ve.tensor_scalar(wsrc[0:1, :], wsrc[0:1, :], sw1c[0:1, 0:1],
                                                 None, ALU.mult, ALU.bypass)
                          P.e(("wrow", t), "sVE", ins)
                          for k in range(2):
                              P.w(ve, ("mm", t, "wbc", k))
                              pb = probin[item_idx[(t, "wbc", k)] % 8]
                              dst = wAb if k == 0 else wBb
                              ins = ve.tensor_copy(dst[:, :], pb[:, 0, :])
                              P.e(("rev", t, "wbc", k), "sVE", ins)
                      for cp in range(4):
                          cs = slice(2 * cp, 2 * cp + 2)
                          db = ((2 * t + it) * 4 + cp) % 2
                          P.w(ve, ("sgev", t, it, cp, "a"))
                          P.w(ve, ("sgev", t, it, cp, "g"))
                          P.w(ve, ("qrdy", t, it))
                          wA2, wB2 = bcast2(wAb[:, :], 2), bcast2(wBb[:, :], 2)
                          hl2, hlDT2 = bcast2(hlb[:, :], 2), bcast2(hlDTb[:, :], 2)
                          vnv = [spq[:, p, cs, :] for p in range(3)]
                          if it == 1:
                              for p in range(2):
                                  tgt = sgA[:, db, 2 * p:2 * p + 2, :]
                                  ve.tensor_tensor(tgt, tgt, vT[:, cs, :], ALU.mult)
                                  ve.tensor_tensor(vnv[p], tgt, vnv[p], ALU.subtract)
                              tgt = sgA[:, db, 4:6, :]
                              ve.scalar_tensor_tensor(tgt, tgt, sw0[:, 0:1], vT[:, cs, :],
                                                      ALU.mult, ALU.mult)
                              ve.tensor_tensor(vnv[2], tgt, vnv[2], ALU.subtract)
                          for p in range(3):
                              tgt = sgG[:, db, 2 * p:2 * p + 2, :]
                              ve.tensor_tensor(tgt, tgt, vnv[p], ALU.mult)
                          u1 = sgG[:, db, 0:2, :]
                          u2 = sgG[:, db, 2:4, :]
                          us = sgG[:, db, 4:6, :]
                          ve.tensor_tensor(tmpA[:, :, :], wA2, u1, ALU.mult)
                          ve.tensor_tensor(tmpB[:, :, :], wB2, u2, ALU.mult)
                          ve.tensor_tensor(tmpA[:, :, :], tmpA[:, :, :], tmpB[:, :, :],
                                           ALU.add)
                          ve.tensor_tensor(tmpA[:, :, :], tmpA[:, :, :], us, ALU.add)
                          ve.tensor_tensor(tmpA[:, :, :], hlDT2, tmpA[:, :, :], ALU.mult)
                          xs = x1[:, t % 2, cs, :] if it == 0 else intT[:, cs, :]
                          ve.tensor_tensor(intT[:, cs, :], xs, tmpA[:, :, :],
                                           ALU.subtract if it == 0 else ALU.add)
                          ve.tensor_tensor(tmpB[:, :, :], wA2, vnv[0], ALU.mult)
                          ve.tensor_tensor(tmpA[:, :, :], wB2, vnv[1], ALU.mult)
                          ve.tensor_tensor(tmpB[:, :, :], tmpB[:, :, :], tmpA[:, :, :],
                                           ALU.add)
                          ve.tensor_tensor(tmpB[:, :, :], tmpB[:, :, :], vnv[2], ALU.add)
                          if it == 1:
                              ve.tensor_tensor(tmpB[:, :, :], tmpB[:, :, :], vT[:, cs, :],
                                               ALU.subtract)
                              ve.tensor_tensor(tmpB[:, :, :], hl2, tmpB[:, :, :], ALU.mult)
                              ins = ve.tensor_tensor(vT[:, cs, :], vT[:, cs, :],
                                                     tmpB[:, :, :], ALU.add)
                          else:
                              ve.tensor_tensor(tmpB[:, :, :], hl2, tmpB[:, :, :], ALU.mult)
                              ins = ve.tensor_scalar(vT[:, cs, :], tmpB[:, :, :], -1.0,
                                                     None, ALU.mult, ALU.bypass)
                          P.e(("ew", t, it, cp, "a"), "sVE", ins, 1)
                          P.e(("ew", t, it, cp, "g"), "sVE", ins, 0)
                          if it == 0:
                              ve.tensor_copy(intTb[:, t % 2, cs, :], intT[:, cs, :])
                              ins = ve.tensor_copy(vTb[:, cs, :], vT[:, cs, :])
                              P.e(("cast1p", t, cp), "sVE", ins)
                      P.e(("el", t, it), "sVE", ins, 0)
                      if it == 0:
                          if t + 1 < NT:
                              cast0(t + 1)
                      else:
                          ins = ve.tensor_copy(intTb[:, t % 2], intT[:, :, :])
                          P.e(("icast", t), "sVE", ins)

        null = {k: _Null() for k in ("sync", "gpsimd", "tensor", "scalar", "vector")}
        run_streams(null, "all")
        P.start_emit()

        def mk(name):
            def body(engine):
                eng = dict(null)
                eng[name] = engine
                run_streams(eng, name)
            return body

        with nc.Block("F1", no_gpsimd_drain=True) as blk:
            blk.sync(mk("sync"))
            blk.gpsimd(mk("gpsimd"))
            blk.tensor(mk("tensor"))
            blk.scalar(mk("scalar"))
            blk.vector(mk("vector"))

            @blk.gpsimd
            def _(gp):
                gp.wait_ge(E["sS0"].h, P.ev[("ist", NT - 1)][1])


def emit_F2(nc, E):
    sems = {k: E[k] for k in ("sLD", "sWL", "sST", "sMM", "sSC", "sVE", "sAux",
                              "sP2", "sP3", "sP4", "sP5", "sS1")}
    sST = E["sST"]
    c_oc, c_or32 = E["c_oc"], E["c_or32"]
    d_x1Ts, d_intTs, d_outT = E["d_x1Ts"], E["d_intTs"], E["d_outT"]

    NT = NQ // F2T
    T = F2T

    with contextlib.ExitStack() as es:
        ec = es.enter_context
        sb = lambda nm, shp, dt: ec(nc.sbuf_tensor(nm, shp, dt))
        w_f1 = sb("f2_wf1", [128, 8, 2 * D], BF16)
        w_f2 = sb("f2_wf2", [128, 16, D], BF16)
        b_f1 = sb("f2_bf1", [128, 16], F32)
        b_f2 = sb("f2_bf2", [128, 8], F32)
        b_iw = sb("f2_biw", [128, 8], F32)
        ib = sb("f2_ib", [128, 2, 8, T], BF16)
        x1f = sb("f2_x1", [128, 2, 8, T], F32)
        rh = sb("f2_rh", [128, 16, T], BF16)
        reft = sb("f2_reft", [128, 2, T], F32)
        pre2 = sb("f2_pre2", [128, 8, T], F32)
        p2b = sb("f2_p2b", [128, 8, T], BF16)
        p2sq = sb("f2_p2sq", [128, 8, T], BF16)
        st2 = sb("f2_st2", [128, 4, T], F32)
        tmp2 = sb("f2_tmp2", [128, T], F32)
        x2 = sb("f2_x2", [128, 8, T], F32)
        probin = [ec(nc.psum_tensor(f"f2_pr{i}", [128, T], F32)) for i in range(4)]

        P = Plan(sems)

        def tile_items(t):
            return ([("rh", m) for m in range(16)] + [("rf", m) for m in range(8)]
                    + [("s1", None), ("s2", None), ("nmb", None), ("rsb", None)])

        all_items = [(t, tag, arg) for t in range(NT) for tag, arg in tile_items(t)]
        item_idx = {k: i for i, k in enumerate(all_items)}

        def run_streams(eng, part="all"):
            sy, gp, te, sc, ve = (eng[k] for k in ("sync", "gpsimd", "tensor",
                                                   "scalar", "vector"))

            if part in ("all", "sync"):
                s16 = lambda key, dst, src_: P.e(key, "sAux", sy.dma_start(dst, src_), 16)
                s16(("b2l", 0), b_f1[:, :], E["d_fb1"][:, :])
                s16(("b2l", 1), b_f2[:, :], E["d_fb2"][:, :])
                s16(("b2l", 2), b_iw[:, :], E["d_iw"][:, :])
                for t in range(NT):
                    if t >= 2:
                        P.w(sy, ("fin2", t - 2))
                    ap = d_intTs.rearrange("(c p) t -> p c t", p=128)[:, :, t * T:(t + 1) * T]
                    P.e(("ibld", t), f"sP{2 + t % 2}", sy.dma_start(ib[:, t % 2], ap), 16)
                    ap = d_x1Ts.rearrange("(c p) t -> p c t", p=128)[:, :, t * T:(t + 1) * T]
                    P.e(("x1ld2", t), f"sP{4 + t % 2}", sy.dma_start(x1f[:, t % 2], ap), 16)

            if part in ("all", "gpsimd"):
                rr = lambda d: d.rearrange("(c p) n -> p c n", p=128)
                P.e(("w2l", 0), "sWL", gp.dma_start(w_f1[:, :, :], rr(E["d_fw1"])), 16)
                P.e(("w2l", 1), "sWL", gp.dma_start(w_f2[:, :, :], rr(E["d_fw2"])), 16)
                for t in range(NT):
                    P.w(gp, ("x2dn", t))
                    ap = d_outT.rearrange("(c p) t -> p c t", p=128)[:, :, t * T:(t + 1) * T]
                    P.e(("ost", t), "sS1", gp.dma_start(ap, x2[:, :, :]), 16)

            def robin(t, tag, arg):
                idx = item_idx[(t, tag, arg)]
                if idx >= 4:
                    P.w(te, ("rev2",) + all_items[idx - 4])
                return probin[idx % 4]

            if part in ("all", "tensor"):
                P.w(te, ("w2l", 1))
                P.w(te, ("b2l", 2))
                for t in range(NT):
                    P.w(te, ("ibld", t))
                    for m in range(16):
                        pb = robin(t, "rh", m)
                        last = None
                        for c in range(8):
                            last = te.matmul(pb[:, :], w_f1[:, c, 128 * m:128 * m + 128],
                                             ib[:, t % 2, c, :], start=(c == 0), stop=(c == 7))
                        P.e(("mm2", t, "rh", m), "sMM", last)
                    for m in range(8):
                        pb = robin(t, "rf", m)
                        P.w(te, ("rhev", t, 15) if m == 0 else ("rhev", t, 15))
                        last = None
                        for hc in range(16):
                            last = te.matmul(pb[:, :], w_f2[:, hc, 128 * m:128 * m + 128],
                                             rh[:, hc, :], start=(hc == 0), stop=(hc == 15))
                        P.e(("mm2", t, "rf", m), "sMM", last)
                    pb = robin(t, "s1", None)
                    P.w(te, ("castb", t))
                    last = None
                    for c in range(8):
                        last = te.matmul(pb[:, :], c_oc[:, :], p2b[:, c, :],
                                         start=(c == 0), stop=(c == 7))
                    P.e(("mm2", t, "s1", None), "sMM", last)
                    pb = robin(t, "s2", None)
                    P.w(te, ("sq2", t))
                    last = None
                    for c in range(8):
                        last = te.matmul(pb[:, :], c_oc[:, :], p2sq[:, c, :],
                                         start=(c == 0), stop=(c == 7))
                    P.e(("mm2", t, "s2", None), "sMM", last)
                    pb = robin(t, "nmb", None)
                    P.w(te, ("st2v", t))
                    ins = te.matmul(pb[:, :], c_or32[:, :], st2[:, 3, :], start=True, stop=True)
                    P.e(("mm2", t, "nmb", None), "sMM", ins)
                    pb = robin(t, "rsb", None)
                    P.w(te, ("rstd2", t))
                    ins = te.matmul(pb[:, :], c_or32[:, :], st2[:, 2, :], start=True, stop=True)
                    P.e(("mm2", t, "rsb", None), "sMM", ins)

            if part in ("all", "scalar"):
                P.w(sc, ("b2l", 2))
                for t in range(NT):
                    for m in range(16):
                        P.w(sc, ("mm2", t, "rh", m))
                        pb = probin[item_idx[(t, "rh", m)] % 4]
                        ins = sc.activation(rh[:, m, :], pb[:, :], AF.Gelu,
                                            bias=b_f1[:, m:m + 1])
                        P.e(("rev2", t, "rh", m), "sSC", ins)
                        if m == 15:
                            P.e(("rhev", t, 15), "sSC", ins, 0)
                    for m in range(8):
                        P.w(sc, ("mm2", t, "rf", m))
                        if m > 1:
                            P.w(sc, ("pr2", t, m - 2))
                        elif t > 0:
                            P.w(sc, ("pr2", t - 1, 6 + m))
                        pb = probin[item_idx[(t, "rf", m)] % 4]
                        ins = sc.activation(reft[:, m % 2, :], pb[:, :], AF.Identity,
                                            bias=b_f2[:, m:m + 1])
                        P.e(("rev2", t, "rf", m), "sSC", ins)
                        P.e(("rfev", t, m), "sSC", ins, 0)
                    P.w(sc, ("cast2", t))
                    ins = sc.activation(p2sq[:, :, :], pre2[:, :, :], AF.Square)
                    P.e(("sq2", t), "sSC", ins)
                    P.w(sc, ("st2v", t))
                    ins = sc.activation(tmp2[0:1, :], st2[0:1, 1, :], AF.Ln, bias=LN_EPS)
                    P.e(("ln2", t), "sSC", ins)
                    ins = sc.activation(st2[0:1, 2, :], tmp2[0:1, :], AF.Exp, scale=-0.5)
                    P.e(("rstd2", t), "sSC", ins)

            if part in ("all", "vector"):
                ins = ve.memset(st2[:, :, :], 0.0)
                P.e(("vm2",), "sVE", ins)
                for t in range(NT):
                    P.w(ve, ("x1ld2", t))
                    ins = None
                    for m in range(8):
                        P.w(ve, ("rfev", t, m))
                        ve.scalar_tensor_tensor(pre2[:, m, :], reft[:, m % 2, :],
                                                b_iw[:, m:m + 1],
                                                x1f[:, t % 2, m, :], ALU.mult, ALU.add)
                        ins = ve.tensor_tensor(pre2[:, m, :], pre2[:, m, :],
                                               x1f[:, t % 2, m, :], ALU.add)
                        P.e(("pr2", t, m), "sVE", ins)
                    P.e(("cast2", t), "sVE", ins, 0)
                    ins = ve.tensor_copy(p2b[:, :, :], pre2[:, :, :])
                    P.e(("castb", t), "sVE", ins)
                    P.w(ve, ("mm2", t, "s1", None))
                    P.w(ve, ("mm2", t, "s2", None))
                    pb1 = probin[item_idx[(t, "s1", None)] % 4]
                    pb2 = probin[item_idx[(t, "s2", None)] % 4]
                    ve.tensor_scalar(st2[0:1, 3, :], pb1[0:1, :], -1.0 / D, None,
                                     ALU.mult, ALU.bypass)
                    ve.tensor_scalar(st2[0:1, 1, :], pb1[0:1, :], 1.0 / D, None,
                                     ALU.mult, ALU.bypass)
                    ve.tensor_tensor(tmp2[0:1, :], st2[0:1, 1, :], st2[0:1, 1, :], ALU.mult)
                    ve.tensor_scalar(st2[0:1, 1, :], pb2[0:1, :], 1.0 / D, None,
                                     ALU.mult, ALU.bypass)
                    ins = ve.tensor_tensor(st2[0:1, 1, :], st2[0:1, 1, :], tmp2[0:1, :],
                                           ALU.subtract)
                    P.e(("st2v", t), "sVE", ins)
                    P.e(("rev2", t, "s1", None), "sVE", ins, 0)
                    P.e(("rev2", t, "s2", None), "sVE", ins, 0)
                    P.w(ve, ("mm2", t, "nmb", None))
                    P.w(ve, ("mm2", t, "rsb", None))
                    if t >= 1:
                        P.w(ve, ("ost", t - 1))
                    pbn = probin[item_idx[(t, "nmb", None)] % 4]
                    pbr = probin[item_idx[(t, "rsb", None)] % 4]
                    ins = None
                    for m in range(8):
                        ve.tensor_tensor(tmp2[:, :], pre2[:, m, :], pbn[:, :], ALU.add)
                        ins = ve.tensor_tensor(x2[:, m, :], tmp2[:, :], pbr[:, :], ALU.mult)
                    P.e(("x2dn", t), "sVE", ins)
                    P.e(("rev2", t, "nmb", None), "sVE", ins, 0)
                    P.e(("rev2", t, "rsb", None), "sVE", ins, 0)
                    P.e(("fin2", t), "sVE", ins, 0)

        null = {k: _Null() for k in ("sync", "gpsimd", "tensor", "scalar", "vector")}
        run_streams(null, "all")
        P.start_emit()

        def mk(name):
            def body(engine):
                eng = dict(null)
                eng[name] = engine
                run_streams(eng, name)
            return body

        with nc.Block("F2", no_gpsimd_drain=True) as blk:
            blk.sync(mk("sync"))
            blk.gpsimd(mk("gpsimd"))
            blk.tensor(mk("tensor"))
            blk.scalar(mk("scalar"))
            blk.vector(mk("vector"))

            @blk.gpsimd
            def _(gp):
                gp.wait_ge(E["sS1"].h, P.ev[("ost", NT - 1)][1])



def build_program(upto="F2", dbg=False, phases=None):
    nc = bass.Bass("TRN2", target_bir_lowering=False)
    scratch_kind = "ExternalOutput" if dbg else "Internal"
    order = ["A1", "A2", "F1", "F2"]
    if phases is None:
        phases = order[:order.index(upto) + 1]

    E = {}
    E["d_xT"] = nc.dram_tensor("xT", [D, NSEQ], F32, kind="ExternalInput").ap()
    E["d_xqT"] = nc.dram_tensor("xqT", [D, NQ], F32, kind="ExternalInput").ap()
    E["d_wqkv"] = nc.dram_tensor("wqkv", [D, 3 * D], F32, kind="ExternalInput").ap()
    for nm, shp, dt in [
        ("identf", [128, 128], F32), ("onescol", [128, 128], F32),
        ("onesrow", [128, 128], F32), ("onesrowdt", [128, 128], F32),
        ("pairpat", [4, 128, 128], F32), ("iota8", [128, 8], F32),
        ("cmask", [8, 128, QB], F32),
        ("rw1", [D, 256], F32), ("rb1", [128, 2], F32),
        ("rw2", [256, 128], F32), ("rb2", [128, 1], F32),
        ("hw1", [D, 256], F32), ("hb1", [128, 2], F32),
        ("hw2", [256, 128], F32), ("hb2", [1, 1], F32),
        ("ew1", [2 * D, 512], F32), ("eb1", [128, 4], F32),
        ("ew2", [512, 3 * D], F32), ("eb2", [128, 3 * D], F32),
        ("sw1", [2 * D, 128], F32), ("sb1", [128, 1], F32),
        ("sw2", [128, 3 * D], F32), ("sb2", [128, 24], F32),
        ("shw", [128, 1], F32),
        ("fw1", [D, 2 * D], F32), ("fb1", [128, 16], F32),
        ("fw2", [2 * D, D], F32), ("fb2", [128, 8], F32),
        ("mun", [128, 8], F32), ("iw", [128, 8], F32),
        ("misc", [128, 4], F32),
    ]:
        E["d_" + nm] = nc.dram_tensor(nm, shp, dt, kind="ExternalInput").ap()

    kind_of = lambda prod: scratch_kind if prod in phases else "ExternalInput"
    E["d_kTs"] = nc.dram_tensor("kTs", [D, NSEQ], BF16, kind=kind_of("A1")).ap()
    E["d_Vs"] = nc.dram_tensor("Vs", [NSEQ, D], BF16, kind=kind_of("A1")).ap()
    E["d_qTs"] = nc.dram_tensor("qTs", [D, NQ], BF16, kind=kind_of("A1")).ap()
    E["d_x1Ts"] = nc.dram_tensor("x1Ts", [D, NQ], F32, kind=kind_of("A2")).ap()
    E["d_intTs"] = nc.dram_tensor("intTs", [D, NQ], BF16, kind=kind_of("F1")).ap()
    E["d_outT"] = nc.dram_tensor("outT", [D, NQ], F32, kind="ExternalOutput").ap()

    with contextlib.ExitStack() as es:
        ec = es.enter_context
        for nm in ("sLD", "sWL", "sST", "sMM", "sSC", "sVE", "sVX", "sAux",
                   "sP0", "sP1", "sP2", "sP3", "sP4", "sP5", "sP6", "sP7",
                   "sS0", "sS1"):
            E[nm] = Sem(ec(nc.semaphore(nm)))

        E["c_id"] = ec(nc.sbuf_tensor("c_id", [128, 128], F32))
        E["c_oc"] = ec(nc.sbuf_tensor("c_oc", [128, 128], BF16))
        E["c_or32"] = ec(nc.sbuf_tensor("c_or32", [128, 128], F32))
        E["c_ordt32"] = ec(nc.sbuf_tensor("c_ordt32", [128, 128], F32))
        E["c_pair"] = ec(nc.sbuf_tensor("c_pair", [128, 4, 128], BF16))
        E["c_iota"] = ec(nc.sbuf_tensor("c_iota", [128, 8], F32))
        E["c_misc"] = ec(nc.sbuf_tensor("c_misc", [128, 4], F32))
        nc.const_aps.register(F32, LN_EPS, E["c_misc"][:, 0:1])
        nc.const_aps.register(F32, 1.0, E["c_misc"][:, 1:2])
        nc.const_aps.register(F32, 0.0, E["c_misc"][:, 2:3])

        sLD, sWL = E["sLD"], E["sWL"]
        with nc.Block("INIT", no_gpsimd_drain=True) as blk:
            @blk.sync
            def _(sync):
                sAux = E["sAux"]
                sAux.inc(sync.dma_start(E["c_id"][:, :], E["d_identf"][:, :]), 16)
                sAux.inc(sync.dma_start(E["c_iota"][:, :], E["d_iota8"][:, :]), 16)
                sAux.inc(sync.dma_start(E["c_misc"][:, :], E["d_misc"][:, :]), 16)
                sAux.inc(sync.dma_start(E["c_or32"][:, :], E["d_onesrow"][:, :]), 16)
                sAux.inc(sync.dma_start(E["c_ordt32"][:, :], E["d_onesrowdt"][:, :]), 16)

            @blk.gpsimd
            def _(g):
                sWL.inc(g.dma_start(E["c_oc"][:, :], E["d_onescol"][:, :]), 16)
                sWL.inc(g.dma_start(E["c_pair"][:, :, :],
                                    E["d_pairpat"].rearrange("k p f -> p k f")), 16)

        if "A1" in phases:
            with nc.named_scope("A1"):
                emit_A1(nc, E)
        if "A2" in phases:
            with nc.named_scope("A2"):
                emit_A2(nc, E)
        if "F1" in phases:
            with nc.named_scope("F1"):
                emit_F1(nc, E)
        if "F2" in phases:
            with nc.named_scope("F2"):
                emit_F2(nc, E)

    return nc



def stage_inputs(inputs):
    x = np.asarray(inputs["L0_attn_x"])
    wqkv = np.ascontiguousarray(inputs["L0_attn_W_qkv"], dtype=np.float32)

    ident = np.eye(128, dtype=np.float32)
    onescol = np.zeros((128, 128), np.float32); onescol[:, 0] = 1.0
    onesrow = np.zeros((128, 128), np.float32); onesrow[0, :] = 1.0
    onesrowdt = np.zeros((128, 128), np.float32); onesrowdt[0, :] = DT_STEP
    pairpat = np.zeros((4, 128, 128), np.float32)
    for P in range(4):
        pairpat[P, 2 * P, 0:64] = 1.0
        pairpat[P, 2 * P + 1, 64:128] = 1.0
    iota8 = np.tile(np.arange(8, dtype=np.float32)[None, :], (128, 1))

    p_idx = np.arange(128)[:, None]
    j_idx = np.arange(QB)[None, :]
    cmask = {}
    for h in (0, 1):
        m = np.zeros((8, 128, QB), np.float32)
        for s in range(8):
            if h == 0:
                if s < 4:
                    m[s] = (128 * s + p_idx <= j_idx)
            else:
                if s < 4:
                    m[s] = 1.0
                else:
                    m[s] = (128 * (s - 4) + p_idx <= j_idx)
        cmask[h] = m

    f32 = lambda a: np.ascontiguousarray(np.asarray(a), dtype=np.float32)
    rw1 = f32(inputs["L0_ffn_router_w1"])
    rb1 = f32(inputs["L0_ffn_router_b1"]).reshape(2, 128).T.copy()
    rw2 = np.zeros((256, 128), np.float32); rw2[:, :8] = f32(inputs["L0_ffn_router_w2"])
    rb2 = np.zeros((128, 1), np.float32); rb2[:8, 0] = f32(inputs["L0_ffn_router_b2"])
    hw1 = f32(inputs["L0_ffn_halt_w1"])
    hb1 = f32(inputs["L0_ffn_halt_b1"]).reshape(2, 128).T.copy()
    hw2 = np.zeros((256, 128), np.float32); hw2[:, 0:1] = f32(inputs["L0_ffn_halt_w2"])
    hb2 = f32(inputs["L0_ffn_halt_b2"]).reshape(1, 1)
    ew1 = np.ascontiguousarray(
        np.transpose(f32(inputs["L0_ffn_expert_w1"]), (1, 0, 2)).reshape(2 * D, 512))
    eb1 = f32(inputs["L0_ffn_expert_b1"]).reshape(512).reshape(4, 128).T.copy()
    ew2 = f32(inputs["L0_ffn_expert_w2"]).reshape(512, 3 * D)
    eb2 = np.zeros((128, 3 * D), np.float32); eb2[:8] = f32(inputs["L0_ffn_expert_b2"])
    sw1 = np.zeros((2 * D, 128), np.float32); sw1[:, :HID] = f32(inputs["L0_ffn_shared_w1"])
    sb1 = np.zeros((128, 1), np.float32); sb1[:HID, 0] = f32(inputs["L0_ffn_shared_b1"])
    sw2 = np.zeros((128, 3 * D), np.float32); sw2[:HID] = f32(inputs["L0_ffn_shared_w2"])
    sw2[HID] = f32(inputs["L0_ffn_shared_b2"])
    sb2 = f32(inputs["L0_ffn_shared_b2"]).reshape(24, 128).T.copy()
    shw = np.full((128, 1), np.float32(inputs["L0_ffn_shared_weight"]), np.float32)
    fw1 = f32(inputs["L0_ffn_refine_w1"])
    fb1 = f32(inputs["L0_ffn_refine_b1"]).reshape(16, 128).T.copy()
    fw2 = f32(inputs["L0_ffn_refine_w2"])
    fb2 = f32(inputs["L0_ffn_refine_b2"]).reshape(8, 128).T.copy()
    mun = (-f32(inputs["L0_ffn_mu"])).reshape(8, 128).T.copy()
    iw = f32(inputs["L0_ffn_integration_weight"]).reshape(8, 128).T.copy()

    misc = np.tile(np.array([[LN_EPS, 1.0, 0.0, 0.0]], np.float32), (128, 1))
    shared = dict(misc=misc, identf=ident, onescol=onescol, onesrow=onesrow, onesrowdt=onesrowdt,
                  pairpat=pairpat, iota8=iota8, wqkv=wqkv,
                  rw1=rw1, rb1=rb1, rw2=rw2, rb2=rb2, hw1=hw1, hb1=hb1, hw2=hw2,
                  hb2=hb2, ew1=ew1, eb1=eb1, ew2=ew2, eb2=eb2, sw1=sw1, sb1=sb1,
                  sw2=sw2, sb2=sb2, shw=shw, fw1=fw1, fb1=fb1, fw2=fw2, fb2=fb2,
                  mun=mun, iw=iw)

    xT = [np.ascontiguousarray(x[b].T, dtype=np.float32) for b in range(NB)]
    in_maps = []
    for c in range(NCORE):
        b, h = c // 2, c % 2
        cols = np.concatenate([np.arange(QB * (2 * u + h), QB * (2 * u + h) + QB)
                               for u in range(NBLK)])
        m = dict(shared)
        m["xT"] = xT[b]
        m["xqT"] = np.ascontiguousarray(xT[b][:, cols])
        m["cmask"] = cmask[h]
        in_maps.append(m)
    return in_maps


def assemble_output(results):
    y = np.empty((NB, NSEQ, D), np.float32)
    for c in range(NCORE):
        b, h = c // 2, c % 2
        outT = results[c]["outT"]
        for u in range(NBLK):
            y[b, QB * (2 * u + h):QB * (2 * u + h) + QB, :] = \
                outT[:, QB * u:QB * u + QB].T
    return y


def kernel(**inputs):
    nc = build_program(upto="F2")
    in_maps = stage_inputs(inputs)
    res = run_bass_kernel_spmd(nc, in_maps, core_ids=list(range(NCORE)))
    return assemble_output(res.results)


# version 10 (final)
# speedup vs baseline: 1.4264x; 1.0355x over previous
import contextlib
import numpy as np
import ml_dtypes

import concourse.bass as bass
import concourse.mybir as mybir
from concourse.bass_utils import run_bass_kernel_spmd
from concourse.alu_op_type import AluOpType

F32 = mybir.dt.float32
BF16 = mybir.dt.bfloat16
U32 = mybir.dt.uint32
AF = mybir.ActivationFunctionType
AX = mybir.AxisListType.X
ALU = AluOpType

D = 1024
NSEQ = 8192
NB = 4
NCORE = 8
NQ = 4096
NBLK = 8
QB = 512
KC = 1024
EXP = 8
HID = 64
DT_STEP = 0.1
LN_EPS = 1e-5
SCL = 1.0 / 32.0

A1T = 512
F1T = 256
F2T = 512


class Sem:
    def __init__(self, h):
        self.h = h
        self.n = 0

    def inc(self, instr, d=1):
        instr.then_inc(self.h, d)
        self.n += d
        return self.n


def bcast2(ap, n):
    return bass.AP(ap.tensor, ap.offset,
                   [list(ap.ap[0]), [0, n], list(ap.ap[1])])


class Plan:

    def __init__(self, sems):
        self.sems = sems
        self.ev = {}
        self.emitting = False
        self.shadow = {k: s.n for k, s in sems.items()}

    def start_emit(self):
        self.emitting = True
        for k, s in self.sems.items():
            assert self.shadow[k] >= s.n

    def e(self, key, semname, instr, d=1):
        s = self.sems[semname]
        if not self.emitting:
            self.shadow[semname] += d
            self.ev[key] = (semname, self.shadow[semname])
            return None
        if d:
            s.inc(instr, d)
        assert (semname, s.n) == self.ev[key], (key, s.n, self.ev[key])
        return instr

    def w(self, eng, key):
        if not self.emitting:
            return
        semname, cnt = self.ev[key]
        eng.wait_ge(self.sems[semname].h, cnt)


class _Null:

    def __getattr__(self, k):
        return lambda *a, **kw: None



def emit_A1(nc, E):
    sLD, sWL, sST, sMM, sSC, sVE = E["sLD"], E["sWL"], E["sST"], E["sMM"], E["sSC"], E["sVE"]
    d_xT, d_xqT, d_wqkv = E["d_xT"], E["d_xqT"], E["d_wqkv"]
    d_kTs, d_Vs, d_qTs = E["d_kTs"], E["d_Vs"], E["d_qTs"]

    NT_KV = NSEQ // A1T
    NT_Q = NQ // A1T
    jobs = [("kv", i) for i in range(NT_KV)] + [("q", i) for i in range(NT_Q)]
    ngrp = [16 if k == "kv" else 8 for k, _ in jobs]
    gbase = np.cumsum([0] + ngrp).tolist()
    NG = gbase[-1]

    sP = [E["sP0"], E["sP1"]]
    sSK, sSV = E["sS0"], E["sS1"]
    bWL, bMM, bSC, bVE = sWL.n, sMM.n, sSC.n, sVE.n
    bP = [sP[0].n, sP[1].n]
    bSK, bSV = sSK.n, sSV.n
    ld_done = [(sP[j % 2], bP[j % 2] + 16 * (j // 2 + 1)) for j in range(len(jobs))]
    cast_done = [bVE + j + 1 for j in range(len(jobs))]
    mm_done = [bMM + q + 1 for q in range(NG)]
    ev_done = [bSC + q + 1 for q in range(NG)]
    stK_done = [bSK + 16 * (j + 1) for j in range(len(jobs))]
    stV_done = [bSV + 16 * (j + 1) for j in range(NT_KV)]

    with (
        nc.sbuf_tensor("a1_w", [128, 8, 3 * D], BF16) as w,
        nc.sbuf_tensor("a1_x", [128, 2, 8, A1T], F32) as xt,
        nc.sbuf_tensor("a1_xb", [128, 2, 8, A1T], BF16) as xtb,
        nc.sbuf_tensor("a1_evK", [128, 8, A1T], BF16) as evK,
        nc.sbuf_tensor("a1_evV", [128, 4, D], BF16) as evV,
        nc.psum_tensor("a1_p0", [128, A1T], F32) as p0,
        nc.psum_tensor("a1_p1", [128, A1T], F32) as p1,
        nc.psum_tensor("a1_p2", [128, A1T], F32) as p2,
        nc.psum_tensor("a1_p3", [128, A1T], F32) as p3,
        nc.Block("A1", no_gpsimd_drain=True) as blk,
    ):
        ps = [p0, p1, p2, p3]

        @blk.sync
        def _(sync):
            for j, (kind, i) in enumerate(jobs):
                if j >= 2:
                    sync.wait_ge(sVE.h, cast_done[j - 2])
                src = d_xT if kind == "kv" else d_xqT
                t0 = i * A1T
                ap = src.rearrange("(c p) t -> p c t", p=128)[:, :, t0:t0 + A1T]
                n = ld_done[j][0].inc(sync.dma_start(xt[:, j % 2], ap), 16)
                assert n == ld_done[j][1]

        @blk.vector
        def _(v):
            for j in range(len(jobs)):
                v.wait_ge(ld_done[j][0].h, ld_done[j][1])
                if j >= 2:
                    v.wait_ge(sMM.h, mm_done[gbase[j - 1] - 1])
                n = sVE.inc(v.tensor_copy(xtb[:, j % 2], xt[:, j % 2]))
                assert n == cast_done[j]

        @blk.tensor
        def _(t):
            t.wait_ge(sWL.h, bWL + 16)
            for j, (kind, i) in enumerate(jobs):
                t.wait_ge(sVE.h, cast_done[j])
                xb = xtb[:, j % 2]
                for grp in range(ngrp[j]):
                    q = gbase[j] + grp
                    if q >= 4:
                        t.wait_ge(sSC.h, ev_done[q - 4])
                    pb = ps[q % 4]
                    last = None
                    if kind == "q" or grp < 8:
                        col0 = (0 if kind == "q" else D) + grp * 128
                        for c in range(8):
                            last = t.matmul(pb[:, :], w[:, c, col0:col0 + 128],
                                            xb[:, c, :], start=(c == 0), stop=(c == 7))
                    else:
                        gg = grp - 8
                        ts0 = (gg // 2) * 128
                        dv0 = 2 * D + (gg % 2) * 512
                        for c in range(8):
                            last = t.matmul(pb[:, :], xb[:, c, ts0:ts0 + 128],
                                            w[:, c, dv0:dv0 + 512],
                                            start=(c == 0), stop=(c == 7))
                    n = sMM.inc(last)
                    assert n == mm_done[q]

        @blk.scalar
        def _(s):
            for j, (kind, i) in enumerate(jobs):
                for grp in range(ngrp[j]):
                    q = gbase[j] + grp
                    s.wait_ge(sMM.h, mm_done[q])
                    if grp == 0 and j > 0:
                        s.wait_ge(sSK.h, stK_done[j - 1])
                    if grp == 8 and j > 0:
                        s.wait_ge(sSV.h, stV_done[j - 1])
                    if kind == "q" or grp < 8:
                        dst = evK[:, grp, :]
                    else:
                        gg = grp - 8
                        dst = evV[:, gg // 2, (gg % 2) * 512:(gg % 2) * 512 + 512]
                    n = sSC.inc(s.activation(dst, ps[q % 4][:, :], AF.Copy))
                    assert n == ev_done[q]

        @blk.gpsimd
        def _(gp):
            n = sWL.inc(gp.dma_start(w[:, :, :],
                                     d_wqkv.rearrange("(c p) n -> p c n", p=128)), 16)
            assert n == bWL + 16
            for j, (kind, i) in enumerate(jobs):
                t0 = i * A1T
                if kind == "kv":
                    gp.wait_ge(sSC.h, ev_done[gbase[j] + 7])
                    ap = d_kTs.rearrange("(m p) t -> p m t", p=128)[:, :, t0:t0 + A1T]
                    n = sSK.inc(gp.dma_start(ap, evK[:, :, :]), 16)
                    assert n == stK_done[j]
                    gp.wait_ge(sSC.h, ev_done[gbase[j] + 15])
                    ap = d_Vs.rearrange("(g p) d -> p g d", p=128)[:, 4 * i:4 * i + 4, :]
                    n = sSV.inc(gp.dma_start(ap, evV[:, :, :]), 16)
                    assert n == stV_done[j]
                else:
                    gp.wait_ge(sSC.h, ev_done[gbase[j] + 7])
                    ap = d_qTs.rearrange("(m p) t -> p m t", p=128)[:, :, t0:t0 + A1T]
                    n = sSK.inc(gp.dma_start(ap, evK[:, :, :]), 16)
                    assert n == stK_done[j]
            gp.wait_ge(sSK.h, stK_done[-1])
            gp.wait_ge(sSV.h, stV_done[-1])



def emit_A2(nc, E):
    sLD, sWL, sST, sMM, sSC, sVE = E["sLD"], E["sWL"], E["sST"], E["sMM"], E["sSC"], E["sVE"]
    d_qTs, d_kTs, d_Vs, d_xqT, d_x1Ts, d_cmask = (
        E["d_qTs"], E["d_kTs"], E["d_Vs"], E["d_xqT"], E["d_x1Ts"], E["d_cmask"])
    c_oc, c_or32 = E["c_oc"], E["c_or32"]

    slots = list(range(NBLK))
    chunk_list = [(u, j) for u in slots for j in range(u + 1)]
    cidx = {uj: i for i, uj in enumerate(chunk_list)}
    sub_list = [(u, j, s) for (u, j) in chunk_list for s in range(8)]
    tidx = {ujs: i for i, ujs in enumerate(sub_list)}

    sPq = [E["sP0"], E["sP1"]]
    sPx = [E["sP2"], E["sP3"]]
    sPk = [E["sP4"], E["sP5"]]
    sPv = [E["sP6"], E["sP7"]]
    sSx = [E["sS0"], E["sS1"]]
    ld_q, ld_x, ld_k, ld_v = {}, {}, {}, {}
    bq = [sPq[0].n, sPq[1].n]; bx = [sPx[0].n, sPx[1].n]
    bk = [sPk[0].n, sPk[1].n]; bv = [sPv[0].n, sPv[1].n]
    for u in slots:
        ld_q[u] = (sPq[u % 2], bq[u % 2] + 16 * (u // 2 + 1))
        ld_x[u] = (sPx[u % 2], bx[u % 2] + 16 * (u // 2 + 1))
    for jj, (u, j) in enumerate(chunk_list):
        ld_k[(u, j)] = (sPk[jj % 2], bk[jj % 2] + 16 * (jj // 2 + 1))
        ld_v[(u, j)] = (sPv[jj % 2], bv[jj % 2] + 16 * (jj // 2 + 1))

    mask_ld = sWL.n + 16

    exp_done, sq_done, ln_done, ex2_done = {}, {}, {}, {}
    c = sSC.n
    for u in slots:
        for j in range(u + 1):
            for s in range(8):
                c += 1; exp_done[(u, j, s)] = c
        c += 1; sq_done[u] = c
        c += 1; ln_done[u] = c
        c += 1; ex2_done[u] = c
    sc_end = c

    msk_done, av_ev_done = {}, {}
    rec_done, comb_done, cast_done, stats_done, fin_done = {}, {}, {}, {}, {}
    c = sVE.n
    c += 1
    ms_done = c
    for u in slots:
        for j in range(u + 1):
            if j == u:
                for s in range(8):
                    c += 1; msk_done[(u, s)] = c
            for w in range(4):
                c += 1; av_ev_done[(u, j, w)] = c
        c += 1; rec_done[u] = c
        c += 1; comb_done[u] = c
        c += 1; cast_done[u] = c
        c += 1; stats_done[u] = c
        c += 1; fin_done[u] = c
    ve_end = c

    sc_grp_done, av_w_done, dn_done = {}, {}, {}
    bc1_done, s1_done, s2_done, bc2_done, bc3_done = {}, {}, {}, {}, {}
    c = sMM.n
    for u in slots:
        for j in range(u + 1):
            for s in range(8):
                c += 1; sc_grp_done[(u, j, s)] = c
            if j == u:
                c += 1; dn_done[u] = c
            for w in range(4):
                c += 1; av_w_done[(u, j, w)] = c
        c += 1; bc1_done[u] = c
        c += 1; s1_done[u] = c
        c += 1; s2_done[u] = c
        c += 1; bc2_done[u] = c
        c += 1; bc3_done[u] = c
    mm_end = c

    st_done = {}
    bs = [sSx[0].n, sSx[1].n]
    for u in slots:
        st_done[u] = (sSx[u % 2], bs[u % 2] + 16 * (u // 2 + 1))

    with (
        nc.sbuf_tensor("a2_qT", [128, 2, 8, QB], BF16) as qT,
        nc.sbuf_tensor("a2_kc", [128, 2, 8, KC], BF16) as kc,
        nc.sbuf_tensor("a2_vc", [128, 2, 8, KC], BF16) as vc,
        nc.sbuf_tensor("a2_PT", [128, 8, QB], BF16) as PT,
        nc.sbuf_tensor("a2_oa", [128, 8, QB], F32) as oa,
        nc.sbuf_tensor("a2_msk", [128, 8, QB], BF16) as msk,
        nc.sbuf_tensor("a2_xr", [128, 2, 8, QB], F32) as xr,
        nc.sbuf_tensor("a2_rT", [128, 8, QB], F32) as rT,
        nc.sbuf_tensor("a2_rTb", [128, 8, QB], BF16) as rTb,
        nc.sbuf_tensor("a2_r2b", [128, 8, QB], BF16) as r2b,
        nc.sbuf_tensor("a2_st", [128, 4, QB], F32) as st,
        nc.sbuf_tensor("a2_tmp", [128, QB], F32) as tmp,
        nc.psum_tensor("a2_pS0", [128, QB], F32) as pS0,
        nc.psum_tensor("a2_pS1", [128, QB], F32) as pS1,
        nc.psum_tensor("a2_pDN", [128, QB], F32) as pDN,
        nc.psum_tensor("a2_pAVa", [128, 2, QB], F32) as pAVa,
        nc.psum_tensor("a2_pAVb", [128, 2, QB], F32) as pAVb,
        nc.psum_tensor("a2_pEP", [128, QB], F32) as pEP,
        nc.Block("A2", no_gpsimd_drain=True) as blk,
    ):
        pS = [pS0, pS1]
        pAV = [pAVa, pAVb]

        def pt_final(u, j, s):
            if j == u:
                return (sVE.h, msk_done[(u, s)])
            return (sSC.h, exp_done[(u, j, s)])

        @blk.sync
        def _(sync):
            for u in slots:
                if u >= 2:
                    sync.wait_ge(sVE.h, fin_done[u - 2])
                    sync.wait_ge(st_done[u - 2][0].h, st_done[u - 2][1])
                ap = d_qTs.rearrange("(m p) t -> p m t", p=128)[:, :, u * QB:(u + 1) * QB]
                n = ld_q[u][0].inc(sync.dma_start(qT[:, u % 2], ap), 16)
                assert n == ld_q[u][1]
                ap = d_xqT.rearrange("(m p) t -> p m t", p=128)[:, :, u * QB:(u + 1) * QB]
                n = ld_x[u][0].inc(sync.dma_start(xr[:, u % 2], ap), 16)
                assert n == ld_x[u][1]
                for j in range(u + 1):
                    jj = cidx[(u, j)]
                    if jj >= 2:
                        sync.wait_ge(sMM.h, av_w_done[chunk_list[jj - 2] + (3,)])
                    ap = d_kTs.rearrange("(m p) t -> p m t", p=128)[:, :, j * KC:(j + 1) * KC]
                    n = ld_k[(u, j)][0].inc(sync.dma_start(kc[:, jj % 2], ap), 16)
                    assert n == ld_k[(u, j)][1]
                    ap = d_Vs.rearrange("(g p) d -> p g d", p=128)[:, 8 * j:8 * j + 8, :]
                    n = ld_v[(u, j)][0].inc(sync.dma_start(vc[:, jj % 2], ap), 16)
                    assert n == ld_v[(u, j)][1]

        @blk.tensor
        def _(t):
            t.wait_ge(sWL.h, mask_ld)
            t.wait_ge(E["sAux"].h, 80)
            for u in slots:
                t.wait_ge(ld_q[u][0].h, ld_q[u][1])
                for j in range(u + 1):
                    jj = cidx[(u, j)]
                    t.wait_ge(ld_k[(u, j)][0].h, ld_k[(u, j)][1])
                    t.wait_ge(ld_v[(u, j)][0].h, ld_v[(u, j)][1])
                    kcb, vcb = kc[:, jj % 2], vc[:, jj % 2]
                    for s in range(8):
                        ti = tidx[(u, j, s)]
                        if ti >= 2:
                            t.wait_ge(sSC.h, exp_done[sub_list[ti - 2]])
                        pb = pS[s % 2]
                        last = None
                        for cc in range(8):
                            last = t.matmul(pb[:, :], kcb[:, cc, 128 * s:128 * s + 128],
                                            qT[:, u % 2, cc, :],
                                            start=(cc == 0), stop=(cc == 7))
                        n = sMM.inc(last); assert n == sc_grp_done[(u, j, s)]
                    for s in range(8):
                        sem, cnt = pt_final(u, j, s)
                        t.wait_ge(sem, cnt)
                        first = (j == 0 and s == 0)
                        if first and u >= 1:
                            t.wait_ge(sVE.h, fin_done[u - 1])
                        lastmm = (j == u and s == 7)
                        ins = t.matmul(pDN[:, :], c_oc[:, :], PT[:, s, :],
                                       start=first, stop=lastmm)
                        if lastmm:
                            n = sMM.inc(ins); assert n == dn_done[u]
                    for w in range(4):
                        gw = 4 * jj + w
                        if gw >= 2:
                            pj, pw = divmod(gw - 2, 4)
                            t.wait_ge(sVE.h, av_ev_done[chunk_list[pj] + (pw,)])
                        pv = pAV[w % 2]
                        for mi in range(2):
                            dvc = 2 * w + mi
                            last = None
                            for s in range(8):
                                last = t.matmul(pv[:, mi, :],
                                                vcb[:, s, 128 * dvc:128 * dvc + 128],
                                                PT[:, s, :],
                                                start=(s == 0), stop=(s == 7))
                        n = sMM.inc(last); assert n == av_w_done[(u, j, w)]
                t.wait_ge(sVE.h, rec_done[u])
                ins = t.matmul(pEP[:, :], c_or32[:, :], st[:, 0, :], start=True, stop=True)
                n = sMM.inc(ins); assert n == bc1_done[u]
                t.wait_ge(sVE.h, cast_done[u])
                last = None
                for cc in range(8):
                    last = t.matmul(pEP[:, :], c_oc[:, :], rTb[:, cc, :],
                                    start=(cc == 0), stop=(cc == 7))
                n = sMM.inc(last); assert n == s1_done[u]
                t.wait_ge(sSC.h, sq_done[u])
                last = None
                for cc in range(8):
                    last = t.matmul(pDN[:, :], c_oc[:, :], r2b[:, cc, :],
                                    start=(cc == 0), stop=(cc == 7))
                n = sMM.inc(last); assert n == s2_done[u]
                t.wait_ge(sVE.h, stats_done[u])
                ins = t.matmul(pEP[:, :], c_or32[:, :], st[:, 3, :], start=True, stop=True)
                n = sMM.inc(ins); assert n == bc2_done[u]
                t.wait_ge(sSC.h, ex2_done[u])
                ins = t.matmul(pDN[:, :], c_or32[:, :], st[:, 2, :], start=True, stop=True)
                n = sMM.inc(ins); assert n == bc3_done[u]

        @blk.scalar
        def _(s_):
            s_.wait_ge(E["sAux"].h, 80)
            for u in slots:
                for j in range(u + 1):
                    jj = cidx[(u, j)]
                    for s in range(8):
                        s_.wait_ge(sMM.h, sc_grp_done[(u, j, s)])
                        if jj >= 1:
                            s_.wait_ge(sMM.h, av_w_done[chunk_list[jj - 1] + (3,)])
                        ins = s_.activation(PT[:, s, :], pS[s % 2][:, :], AF.Exp, scale=SCL)
                        n = sSC.inc(ins); assert n == exp_done[(u, j, s)]
                s_.wait_ge(sVE.h, comb_done[u])
                ins = s_.activation(r2b[:, :, :], rT[:, :, :], AF.Square)
                n = sSC.inc(ins); assert n == sq_done[u]
                s_.wait_ge(sVE.h, stats_done[u])
                ins = s_.activation(tmp[0:1, :], st[0:1, 1, :], AF.Ln, bias=LN_EPS)
                n = sSC.inc(ins); assert n == ln_done[u]
                ins = s_.activation(st[0:1, 2, :], tmp[0:1, :], AF.Exp, scale=-0.5)
                n = sSC.inc(ins); assert n == ex2_done[u]

        @blk.vector
        def _(v):
            n = sVE.inc(v.memset(st[:, :, :], 0.0)); assert n == ms_done
            for u in slots:
                for j in range(u + 1):
                    jj = cidx[(u, j)]
                    if j == u:
                        for s in range(8):
                            v.wait_ge(sSC.h, exp_done[(u, j, s)])
                            ins = v.tensor_tensor(PT[:, s, :], PT[:, s, :], msk[:, s, :],
                                                  ALU.mult)
                            n = sVE.inc(ins); assert n == msk_done[(u, s)]
                    for w in range(4):
                        v.wait_ge(sMM.h, av_w_done[(u, j, w)])
                        pv = pAV[w % 2]
                        ins = None
                        for mi in range(2):
                            dvc = 2 * w + mi
                            if j == 0:
                                ins = v.tensor_copy(oa[:, dvc, :], pv[:, mi, :])
                            else:
                                ins = v.tensor_tensor(oa[:, dvc, :], pv[:, mi, :],
                                                      oa[:, dvc, :], ALU.add)
                        n = sVE.inc(ins); assert n == av_ev_done[(u, j, w)]
                v.wait_ge(sMM.h, dn_done[u])
                ins = v.reciprocal(st[0:1, 0, :], pDN[0:1, :])
                n = sVE.inc(ins); assert n == rec_done[u]
                v.wait_ge(sMM.h, bc1_done[u])
                v.wait_ge(ld_x[u][0].h, ld_x[u][1])
                ins = None
                for cc in range(8):
                    v.tensor_tensor(rT[:, cc, :], oa[:, cc, :], pEP[:, :], ALU.mult)
                    ins = v.tensor_tensor(rT[:, cc, :], rT[:, cc, :],
                                          xr[:, u % 2, cc, :], ALU.add)
                n = sVE.inc(ins); assert n == comb_done[u]
                ins = v.tensor_copy(rTb[:, :, :], rT[:, :, :])
                n = sVE.inc(ins); assert n == cast_done[u]
                v.wait_ge(sMM.h, s1_done[u])
                v.wait_ge(sMM.h, s2_done[u])
                v.tensor_scalar(st[0:1, 3, :], pEP[0:1, :], -1.0 / D, None,
                                ALU.mult, ALU.bypass)
                v.tensor_scalar(st[0:1, 1, :], pEP[0:1, :], 1.0 / D, None,
                                ALU.mult, ALU.bypass)
                v.tensor_tensor(tmp[0:1, :], st[0:1, 1, :], st[0:1, 1, :], ALU.mult)
                v.tensor_scalar(st[0:1, 1, :], pDN[0:1, :], 1.0 / D, None,
                                ALU.mult, ALU.bypass)
                ins = v.tensor_tensor(st[0:1, 1, :], st[0:1, 1, :], tmp[0:1, :],
                                      ALU.subtract)
                n = sVE.inc(ins); assert n == stats_done[u]
                v.wait_ge(sMM.h, bc2_done[u])
                v.wait_ge(sMM.h, bc3_done[u])
                ins = None
                for cc in range(8):
                    v.tensor_tensor(tmp[:, :], rT[:, cc, :], pEP[:, :], ALU.add)
                    ins = v.tensor_tensor(xr[:, u % 2, cc, :], tmp[:, :], pDN[:, :],
                                          ALU.mult)
                n = sVE.inc(ins); assert n == fin_done[u]

        @blk.gpsimd
        def _(gp):
            n = sWL.inc(gp.dma_start(msk[:, :, :],
                                     d_cmask.rearrange("s p j -> p s j")), 16)
            assert n == mask_ld
            for u in slots:
                gp.wait_ge(sVE.h, fin_done[u])
                ap = d_x1Ts.rearrange("(m p) t -> p m t", p=128)[:, :, u * QB:(u + 1) * QB]
                n = st_done[u][0].inc(gp.dma_start(ap, xr[:, u % 2]), 16)
                assert n == st_done[u][1]
            gp.wait_ge(st_done[slots[-2]][0].h, st_done[slots[-2]][1])
            gp.wait_ge(st_done[slots[-1]][0].h, st_done[slots[-1]][1])


F1_NT_OVERRIDE = None


def emit_F1(nc, E):
    sems = {k: E[k] for k in ("sLD", "sWL", "sST", "sMM", "sSC", "sVE", "sVX", "sAux",
                              "sP0", "sP1", "sS0")}
    sLD, sWL, sST, sMM, sSC, sVE, sVX = (E[k] for k in
                                         ("sLD", "sWL", "sST", "sMM", "sSC", "sVE", "sVX"))
    c_id, c_oc, c_or32, c_ordt32, c_pair, c_iota = (
        E["c_id"], E["c_oc"], E["c_or32"], E["c_ordt32"], E["c_pair"], E["c_iota"])
    d_x1Ts, d_intTs = E["d_x1Ts"], E["d_intTs"]

    NT = F1_NT_OVERRIDE or (NQ // F1T)
    T = F1T

    with contextlib.ExitStack() as es:
        ec = es.enter_context
        sb = lambda nm, shp, dt: ec(nc.sbuf_tensor(nm, shp, dt))
        w_r1 = sb("f1_wr1", [128, 8, 256], BF16)
        w_r2 = sb("f1_wr2", [128, 2, 128], BF16)
        w_h1 = sb("f1_wh1", [128, 8, 256], BF16)
        w_h2 = sb("f1_wh2", [128, 2, 128], BF16)
        w_e1 = sb("f1_we1", [128, 16, 512], BF16)
        w_e2 = sb("f1_we2", [128, 4, 3072], BF16)
        w_eb2 = sb("f1_web2", [128, 3072], BF16)
        w_s1 = sb("f1_ws1", [128, 16, 128], BF16)
        w_s2 = sb("f1_ws2", [128, 3072], BF16)
        b_r1 = sb("f1_br1", [128, 2], F32)
        b_r2 = sb("f1_br2", [128, 1], F32)
        b_h1 = sb("f1_bh1", [128, 2], F32)
        b_h2 = sb("f1_bh2", [1, 1], F32)
        b_e1 = sb("f1_be1", [128, 4], F32)
        b_s1 = sb("f1_bs1", [128, 1], F32)
        b_s2 = sb("f1_bs2", [128, 24], F32)
        b_mun = sb("f1_bmun", [128, 8], F32)
        c_shw = sb("f1_shw", [128, 1], F32)
        sw0 = sb("f1_sw0", [128, 1], F32)
        sw1c = sb("f1_sw1", [128, 1], F32)
        x1 = sb("f1_x1", [128, 2, 8, T], F32)
        intT = sb("f1_int", [128, 8, T], F32)
        vT = sb("f1_v", [128, 8, T], F32)
        intTb = sb("f1_intb", [128, 2, 8, T], BF16)
        vTb = sb("f1_vb", [128, 8, T], BF16)
        xm = sb("f1_xm", [128, 8, T], F32)
        xms = sb("f1_xms", [128, 8, T], F32)
        b_muns = sb("f1_bmuns", [128, 8], F32)
        spq = sb("f1_spq", [128, 3, 8, T], F32)
        sgA = sb("f1_sga", [128, 2, 6, T], F32)
        sgG = sb("f1_sgg", [128, 2, 6, T], F32)
        hEb = sb("f1_heb", [128, 4, T], BF16)
        hm1 = sb("f1_hm1", [128, 4, T], BF16)
        hshb = sb("f1_hshb", [128, T], BF16)
        hHb = sb("f1_hhb", [128, 2, T], BF16)
        hr = sb("f1_hr", [128, 2, T], BF16)
        lg = sb("f1_lg", [128, T], F32)
        lgt = sb("f1_lgt", [128, 2, 128], F32)
        mx = sb("f1_mx", [128, 2, 8], F32)
        mx2 = sb("f1_mx2", [128, 2, 8], F32)
        ixu = sb("f1_ixu", [128, 2, 8], U32)
        ix2u = sb("f1_ix2u", [128, 2, 8], U32)
        iif = sb("f1_iif", [128, 4], F32)
        l2t = sb("f1_l2", [128, 2, 8], F32)
        sel1f = sb("f1_s1f", [128, 2, 128], F32)
        sel2f = sb("f1_s2f", [128, 2, 128], F32)
        sttm = sb("f1_sttm", [128, 2, 128], F32)
        dstat = sb("f1_dst", [128, 256], F32)
        wsrc = sb("f1_wsrc", [128, 256], F32)
        wsrc2 = sb("f1_wsrc2", [128, 256], F32)
        hsrc = sb("f1_hsrc", [128, 256], F32)
        selT1 = sb("f1_selT1", [128, 256], BF16)
        selT2 = sb("f1_selT2", [128, 256], BF16)
        selbc1 = sb("f1_sbc1", [128, 4, T], BF16)
        selbc2 = sb("f1_sbc2", [128, 4, T], BF16)
        wAb = sb("f1_wab", [128, T], F32)
        wBb = sb("f1_wbb", [128, T], F32)
        hlb = sb("f1_hlb", [128, T], F32)
        hlDTb = sb("f1_hldtb", [128, T], F32)
        tmpA = sb("f1_tmpa", [128, 2, T], F32)
        tmpB = sb("f1_tmpb", [128, 2, T], F32)
        etmp = sb("f1_etmp", [128, 2, T], F32)
        probin = [ec(nc.psum_tensor(f"f1_pr{i}", [128, 2, T], F32)) for i in range(8)]

        P = Plan(sems)

        all_items = [(0, "r1", None), (0, "lgmm", None), (0, "lgtr", None)]
        for t in range(NT):
            items = [("he0", 0), ("he1", 0), ("hsh", 0), ("hh", 0)]
            items += [("seltr", None)]
            items += [("pair", (p, pp)) for p in range(2) for pp in range(4)]
            for cp in range(4):
                for path in range(3):
                    items.append(("b0", (cp, path)))
            items += [("hw0", None), ("hlbc0", 0), ("hlbc0", 1), ("wbc", 0), ("wbc", 1)]
            for cp in range(4):
                for rng in ("a", "g"):
                    for path in range(3):
                        items.append(("s0", (cp, rng, path)))
            items += [("he0", 1), ("he1", 1), ("hsh", 1), ("hh", 1)]
            all_items += [(t, tag, arg) for tag, arg in items]
            if t + 1 < NT:
                all_items.append((t + 1, "r1", None))
                all_items.append((t + 1, "lgmm", None))
                all_items.append((t + 1, "lgtr", None))
            items = []
            for cp in range(4):
                for path in range(3):
                    items.append(("b1", (cp, path)))
            items += [("hw1", None), ("hlbc1", 0), ("hlbc1", 1)]
            for cp in range(4):
                for rng in ("a", "g"):
                    for path in range(3):
                        items.append(("s1", (cp, rng, path)))
            all_items += [(t, tag, arg) for tag, arg in items]
        item_idx = {k: i for i, k in enumerate(all_items)}

        VEC_EVAC = {"lgtr": False, "seltr": False}

        def run_streams(eng, part="all"):
            sy, gp, te, sc, ve = (eng[k] for k in ("sync", "gpsimd", "tensor",
                                                   "scalar", "vector"))

            if part in ("all", "sync"):
              s16 = lambda key, dst, src_: P.e(key, "sAux", sy.dma_start(dst, src_), 16)
              s16(("bl", 0), b_r1[:, :], E["d_rb1"][:, :])
              s16(("bl", 1), b_r2[:, :], E["d_rb2"][:, :])
              s16(("bl", 2), b_h1[:, :], E["d_hb1"][:, :])
              s16(("bl", 3), b_h2[:, :], E["d_hb2"][:, :])
              s16(("bl", 4), b_e1[:, :], E["d_eb1"][:, :])
              s16(("bl", 5), b_s1[:, :], E["d_sb1"][:, :])
              s16(("bl", 6), b_s2[:, :], E["d_sb2"][:, :])
              s16(("bl", 7), b_mun[:, :], E["d_mun"][:, :])
              s16(("bl", 8), c_shw[:, :], E["d_shw"][:, :])
              for t in range(NT):
                if t >= 2:
                    P.w(sy, ("el", t - 2, 0))
                ap = d_x1Ts.rearrange("(c p) t -> p c t", p=128)[:, :, t * T:(t + 1) * T]
                P.e(("x1ld", t), f"sP{t % 2}", sy.dma_start(x1[:, t % 2], ap), 16)

            if part not in ("all", "gpsimd"):
                pass
            else:
              g16 = lambda key, dst, src: P.e(key, "sWL", gp.dma_start(dst, src), 16)
              rr = lambda d, p=128: d.rearrange("(c p) n -> p c n", p=p)
              g16(("wl", 0), w_r1[:, :, :], rr(E["d_rw1"]))
              g16(("wl", 1), w_r2[:, :, :], rr(E["d_rw2"]))
              g16(("wl", 2), w_h1[:, :, :], rr(E["d_hw1"]))
              g16(("wl", 3), w_h2[:, :, :], rr(E["d_hw2"]))
              g16(("wl", 4), w_e1[:, :, :], rr(E["d_ew1"]))
              g16(("wl", 5), w_e2[:, :, :], rr(E["d_ew2"]))
              g16(("wl", 6), w_eb2[:, :], E["d_eb2"][:, :])
              g16(("wl", 7), w_s1[:, :, :], rr(E["d_sw1"]))
              g16(("wl", 8), w_s2[:, :], E["d_sw2"][:, :])
              for t in range(NT):
                P.w(gp, ("icast", t))
                ap = d_intTs.rearrange("(c p) t -> p c t", p=128)[:, :, t * T:(t + 1) * T]
                P.e(("ist", t), "sS0", gp.dma_start(ap, intTb[:, t % 2]), 16)

            def robin(t, tag, arg):
                idx = item_idx[(t, tag, arg)]
                if idx >= 8:
                    P.w(te, ("rev",) + all_items[idx - 8])
                return probin[idx % 8]

            def rbank(t, tag, arg):
                return probin[item_idx[(t, tag, arg)] % 8]

            if part in ("all", "tensor"):
              P.w(te, ("wl", 8))
              P.w(te, ("bl", 8))
              P.w(te, ("vmem",))
              def emit_r1(tt):
                  P.w(te, ("cast0", tt))
                  pb = robin(tt, "r1", None)
                  for m in range(2):
                      last = None
                      for c in range(8):
                          last = te.matmul(pb[:, m, :], w_r1[:, c, m * 128:(m + 1) * 128],
                                           intTb[:, tt % 2, c, :], start=(c == 0), stop=(c == 7))
                      if m == 1:
                          P.e(("mm", tt, "r1", None), "sMM", last)

              def emit_lg(tt):
                  P.w(te, ("hrev", tt))
                  pb = robin(tt, "lgmm", None)
                  last = None
                  for cc in range(2):
                      last = te.matmul(pb[:, 0, :], w_r2[:, cc, :], hr[:, cc, :],
                                       start=(cc == 0), stop=(cc == 1))
                  P.e(("mm", tt, "lgmm", None), "sMM", last)
                  P.w(te, ("lgev", tt))
                  pb = robin(tt, "lgtr", None)
                  for j in range(2):
                      last = te.transpose(pb[:, j, 0:128], lg[:, j * 128:(j + 1) * 128],
                                          c_id[:, :])
                  P.e(("mm", tt, "lgtr", None), "sMM", last)

              emit_r1(0)
              emit_lg(0)
              for t in range(NT):
                  tb = t % 2
                  def s1chunks(it):
                      if it == 0:
                          return [(cp, [2 * cp, 2 * cp + 1]) for cp in range(4)]
                      return [(cp, [2 * cp, 2 * cp + 1, 8 + 2 * cp, 9 + 2 * cp])
                              for cp in range(4)]

                  def stage1(it):
                      for half in range(2):
                          pb2 = robin(t, f"he{half}", it)
                          last = None
                          for m in (2 * half, 2 * half + 1):
                              seq = []
                              for cp, cl in s1chunks(it):
                                  for c in cl:
                                      seq.append((cp, c))
                              for k, (cp, c) in enumerate(seq):
                                  if it == 1 and m == 2 * half and half == 0 and c % 2 == 0 and c < 8:
                                      P.w(te, ("cast1p", t, cp))
                                  rhs = intTb[:, t % 2, c, :] if c < 8 else vTb[:, c - 8, :]
                                  last = te.matmul(pb2[:, m % 2, :],
                                                   w_e1[:, c, m * 128:(m + 1) * 128],
                                                   rhs, start=(k == 0),
                                                   stop=(k == len(seq) - 1))
                          P.e(("mm", t, f"he{half}", it), "sMM", last)
                      pb2 = robin(t, "hsh", it)
                      seq = [(cp, c) for cp, cl in s1chunks(it) for c in cl]
                      last = None
                      for k, (cp, c) in enumerate(seq):
                          rhs = intTb[:, t % 2, c, :] if c < 8 else vTb[:, c - 8, :]
                          last = te.matmul(pb2[:, 0, :], w_s1[:, c, :], rhs,
                                           start=(k == 0), stop=(k == len(seq) - 1))
                      P.e(("mm", t, "hsh", it), "sMM", last)
                      pb2 = robin(t, "hh", it)
                      last = None
                      for m in range(2):
                          for c in range(8):
                              last = te.matmul(pb2[:, m, :], w_h1[:, c, m * 128:(m + 1) * 128],
                                               intTb[:, t % 2, c, :], start=(c == 0), stop=(c == 7))
                      P.e(("mm", t, "hh", it), "sMM", last)

                  stage1(0)
                  P.w(te, ("topk", t))
                  pb = robin(t, "seltr", None)
                  for j in range(2):
                      last = te.transpose(pb[:, 0, 128 * j:128 * j + 128],
                                          sel1f[:, j, :], c_id[:, :])
                  for j in range(2):
                      last = te.transpose(pb[:, 1, 128 * j:128 * j + 128],
                                          sel2f[:, j, :], c_id[:, :])
                  P.e(("mm", t, "seltr", None), "sMM", last)
                  for p in range(2):
                      for pp in range(4):
                          pb = robin(t, "pair", (p, pp))
                          if p == 0 and pp == 0:
                              for j in range(2):
                                  te.transpose(pb[:, 1, 128 * j:128 * j + 128],
                                               sttm[:, j, :], c_id[:, :])
                          P.w(te, ("selev", t))
                          selT = selT1 if p == 0 else selT2
                          last = te.matmul(pb[:, 0, :], c_pair[:, pp, :], selT[:, :],
                                           start=True, stop=True)
                          P.e(("mm", t, "pair", (p, pp)), "sMM", last)
                  for it in range(2):
                      if it == 1:
                          stage1(1)
                          if t + 1 < NT:
                              emit_r1(t + 1)
                              emit_lg(t + 1)
                      for cp in range(4):
                          for path in range(3):
                              pb = robin(t, f"b{it}", (cp, path))
                              if path < 2:
                                  P.w(te, ("mask", t, it, path))
                              else:
                                  P.w(te, ("hshev", t, it))
                              last = None
                              for cj in range(2):
                                  c = 2 * cp + cj
                                  fcol = 8 + c
                                  if path < 2:
                                      hmx = hm1 if path == 0 else hEb
                                      selT = selT1 if path == 0 else selT2
                                      for hc in range(4):
                                          last = te.matmul(pb[:, cj, :],
                                                           w_e2[:, hc, 128 * fcol:128 * fcol + 128],
                                                           hmx[:, hc, :],
                                                           start=(hc == 0), stop=False)
                                      last = te.matmul(pb[:, cj, :],
                                                       w_eb2[:, 128 * fcol:128 * fcol + 128],
                                                       selT[:, :], start=False, stop=True)
                                  else:
                                      last = te.matmul(pb[:, cj, :],
                                                       w_s2[:, 128 * fcol:128 * fcol + 128],
                                                       hshb[:, :], start=True, stop=True)
                              P.e(("mm", t, f"b{it}", (cp, path)), "sMM", last)
                      pb = robin(t, f"hw{it}", None)
                      P.w(te, ("hhev", t, it))
                      last = None
                      for cc in range(2):
                          last = te.matmul(pb[:, 0, :], w_h2[:, cc, :], hHb[:, cc, :],
                                           start=(cc == 0), stop=(cc == 1))
                      P.e(("mm", t, f"hw{it}", None), "sMM", last)
                      P.w(te, ("hsg", t, it))
                      for k in range(2):
                          pb = robin(t, f"hlbc{it}", k)
                          lhs = c_or32 if k == 0 else c_ordt32
                          last = te.matmul(pb[:, 0, :], lhs[:, :], hsrc[:, :],
                                           start=True, stop=True)
                          P.e(("mm", t, f"hlbc{it}", k), "sMM", last)
                      if it == 0:
                          P.w(te, ("wrow", t))
                          for k in range(2):
                              pb = robin(t, "wbc", k)
                              wsx = wsrc if k == 0 else wsrc2
                              last = te.matmul(pb[:, 0, :], c_or32[:, :], wsx[:, :],
                                               start=True, stop=True)
                              P.e(("mm", t, "wbc", k), "sMM", last)
                      for cp in range(4):
                          for rng in ("a", "g"):
                              for path in range(3):
                                  pb = robin(t, f"s{it}", (cp, rng, path))
                                  last = None
                                  for cj in range(2):
                                      c = 2 * cp + cj
                                      fcol = c if rng == "a" else 16 + c
                                      if path < 2:
                                          hmx = hm1 if path == 0 else hEb
                                          selT = selT1 if path == 0 else selT2
                                          for hc in range(4):
                                              last = te.matmul(pb[:, cj, :],
                                                               w_e2[:, hc, 128 * fcol:128 * fcol + 128],
                                                               hmx[:, hc, :],
                                                               start=(hc == 0), stop=False)
                                          last = te.matmul(pb[:, cj, :],
                                                           w_eb2[:, 128 * fcol:128 * fcol + 128],
                                                           selT[:, :], start=False, stop=True)
                                      else:
                                          last = te.matmul(pb[:, cj, :],
                                                           w_s2[:, 128 * fcol:128 * fcol + 128],
                                                           hshb[:, :], start=True, stop=True)
                                  P.e(("mm", t, f"s{it}", (cp, rng, path)), "sMM", last)

            if part in ("all", "scalar"):
              P.w(sc, ("bl", 8))
              ins = sc.activation(sw0[:, :], c_shw[:, :], AF.Sigmoid)
              P.e(("sw0",), "sSC", ins)

              def sc_router(tt):
                  P.w(sc, ("mm", tt, "r1", None))
                  pb = probin[item_idx[(tt, "r1", None)] % 8]
                  ins = None
                  for m in range(2):
                      ins = sc.activation(hr[:, m, :], pb[:, m, :],
                                          AF.Gelu, bias=b_r1[:, m:m + 1])
                  P.e(("hrev", tt), "sSC", ins)
                  P.e(("rev", tt, "r1", None), "sSC", ins, 0)
                  P.w(sc, ("mm", tt, "lgmm", None))
                  pb = probin[item_idx[(tt, "lgmm", None)] % 8]
                  ins = sc.activation(lg[:, :], pb[:, 0, :], AF.Identity,
                                      bias=b_r2[:, 0:1])
                  P.e(("lgev", tt), "sSC", ins)
                  P.e(("rev", tt, "lgmm", None), "sSC", ins, 0)
                  P.w(sc, ("mm", tt, "lgtr", None))
                  pb = probin[item_idx[(tt, "lgtr", None)] % 8]
                  ins = None
                  for j in range(2):
                      ins = sc.activation(lgt[:, j, :], pb[:, j, 0:128], AF.Copy)
                  P.e(("lgtev", tt), "sSC", ins)
                  P.e(("rev", tt, "lgtr", None), "sSC", ins, 0)

              sc_router(0)
              for t in range(NT):
                  for it in range(2):
                      for half in range(2):
                          P.w(sc, ("mm", t, f"he{half}", it))
                          pb2 = probin[item_idx[(t, f"he{half}", it)] % 8]
                          ins = None
                          for m in range(2):
                              ins = sc.activation(hEb[:, 2 * half + m, :], pb2[:, m, :],
                                                  AF.Gelu, bias=b_e1[:, 2 * half + m:2 * half + m + 1])
                          P.e(("rev", t, f"he{half}", it), "sSC", ins)
                      P.e(("heev", t, it), "sSC", ins, 0)
                      P.w(sc, ("mm", t, "hsh", it))
                      pb2 = probin[item_idx[(t, "hsh", it)] % 8]
                      ins = sc.activation(hshb[0:64, :], pb2[0:64, 0, :], AF.Gelu,
                                          bias=b_s1[0:64, 0:1])
                      P.e(("rev", t, "hsh", it), "sSC", ins)
                      P.e(("hshev", t, it), "sSC", ins, 0)
                      P.w(sc, ("mm", t, "hh", it))
                      pb2 = probin[item_idx[(t, "hh", it)] % 8]
                      ins = None
                      for m in range(2):
                          ins = sc.activation(hHb[:, m, :], pb2[:, m, :], AF.Gelu,
                                              bias=b_h1[:, m:m + 1])
                      P.e(("rev", t, "hh", it), "sSC", ins)
                      P.e(("hhev", t, it), "sSC", ins, 0)
                      if it == 1 and t + 1 < NT:
                          sc_router(t + 1)
                      if it == 0:
                          P.w(sc, ("mm", t, "seltr", None))
                          pb = probin[item_idx[(t, "seltr", None)] % 8]
                          ins = None
                          for j in range(2):
                              ins = sc.activation(selT1[:, 128 * j:128 * j + 128],
                                                  pb[:, 0, 128 * j:128 * j + 128], AF.Copy)
                              ins = sc.activation(selT2[:, 128 * j:128 * j + 128],
                                                  pb[:, 1, 128 * j:128 * j + 128], AF.Copy)
                          P.e(("selev", t), "sSC", ins)
                          P.e(("rev", t, "seltr", None), "sSC", ins, 0)
                          P.w(sc, ("mm", t, "pair", (0, 0)))
                          pb = probin[item_idx[(t, "pair", (0, 0))] % 8]
                          ins = None
                          for j in range(2):
                              ins = sc.activation(dstat[:, 128 * j:128 * j + 128],
                                                  pb[:, 1, 128 * j:128 * j + 128], AF.Copy)
                          P.e(("dstev", t), "sSC", ins)
                          ins = None
                          for p in range(2):
                              for pp in range(4):
                                  P.w(sc, ("mm", t, "pair", (p, pp)))
                                  pb = probin[item_idx[(t, "pair", (p, pp))] % 8]
                                  dst = (selbc1 if p == 0 else selbc2)[:, pp, :]
                                  ins = sc.activation(dst, pb[:, 0, :], AF.Copy)
                                  P.e(("rev", t, "pair", (p, pp)), "sSC", ins)
                          P.e(("selbcev", t), "sSC", ins, 0)
                      if it == 1:
                          P.w(sc, ("el", t, 0))
                      P.w(sc, ("sw1",))
                      ins = None
                      for c in range(8):
                          src = x1[:, t % 2, c, :] if it == 0 else intT[:, c, :]
                          ins = sc.activation(xm[:, c, :], src, AF.Identity,
                                              bias=b_mun[:, c:c + 1])
                          ins = sc.activation(xms[:, c, :], src, AF.Identity,
                                              scale=sw0[:, 0:1], bias=b_muns[:, c:c + 1])
                      P.e(("xmv", t, it), "sSC", ins)
                      for cp in range(4):
                          for path in range(3):
                              P.w(sc, ("mm", t, f"b{it}", (cp, path)))
                              pb = probin[item_idx[(t, f"b{it}", (cp, path))] % 8]
                              cs_ = slice(2 * cp, 2 * cp + 2)
                              sc.activation(etmp[:, :, :], pb[:, :, :], AF.Exp)
                              ins = sc.activation(spq[:, path, cs_, :], etmp[:, :, :],
                                                  AF.Ln, bias=1.0)
                              P.e(("rev", t, f"b{it}", (cp, path)), "sSC", ins)
                      P.w(sc, ("mm", t, f"hw{it}", None))
                      pb = probin[item_idx[(t, f"hw{it}", None)] % 8]
                      ins = sc.activation(hsrc[0:1, :], pb[0:1, 0, :], AF.Sigmoid,
                                          bias=b_h2[0:1, :])
                      P.e(("hsg", t, it), "sSC", ins)
                      P.e(("rev", t, f"hw{it}", None), "sSC", ins, 0)
                      if it == 0:
                          P.w(sc, ("dstev", t))
                          ins = sc.activation(wsrc[0:1, :], dstat[0:1, :], AF.Sigmoid)
                          P.e(("wsg", t), "sSC", ins)
                      for cp in range(4):
                          for rng in ("a", "g"):
                              for path in range(3):
                                  P.w(sc, ("mm", t, f"s{it}", (cp, rng, path)))
                                  pb = probin[item_idx[(t, f"s{it}", (cp, rng, path))] % 8]
                                  sgx = sgA if rng == "a" else sgG
                                  g = (2 * t + it) * 4 + cp
                                  if g >= 2:
                                      t3, r3 = divmod(g - 2, 8)
                                      it3, cp3 = divmod(r3, 4)
                                      P.w(sc, ("ew", t3, it3, cp3, rng))
                                  dst = sgx[:, ((2 * t + it) * 4 + cp) % 2,
                                            2 * path:2 * path + 2, :]
                                  ins = sc.activation(dst, pb[:, :, :], AF.Sigmoid)
                                  P.e(("rev", t, f"s{it}", (cp, rng, path)), "sSC", ins)
                              P.e(("sgev", t, it, cp, rng), "sSC", ins, 0)

            if part in ("all", "vector"):
              ins = ve.memset(sel1f[:, :, :], 0.0)
              ins = ve.memset(sel2f[:, :, :], 0.0)
              ins = ve.memset(sttm[:, :, :], 0.0)
              ins = ve.memset(wsrc[:, :], 0.0)
              ins = ve.memset(wsrc2[:, :], 0.0)
              ve.memset(hsrc[:, :], 0.0)
              ve.memset(hshb[:, :], 0.0)
              ins = ve.memset(hshb[64:65, :], 1.0)
              P.e(("vmem",), "sVE", ins)
              if P.emitting:
                  ve.wait_ge(E["sAux"].h, 80)
              P.w(ve, ("sw0",))
              ins = ve.tensor_scalar(sw1c[:, :], sw0[:, :], -1.0, 1.0, ALU.mult, ALU.add)
              ins = ve.tensor_scalar(b_muns[:, :], b_mun[:, :], sw0[:, 0:1], None,
                                     ALU.mult, ALU.bypass)
              P.e(("sw1",), "sVE", ins)
              def cast0(t):
                  P.w(ve, ("x1ld", t))
                  if t >= 2:
                      P.w(ve, ("ist", t - 2))
                  ins = ve.tensor_copy(intTb[:, t % 2], x1[:, t % 2])
                  P.e(("cast0", t), "sVE", ins)
              cast0(0)
              def topk(t):
                  P.w(ve, ("lgtev", t))
                  if t >= 1:
                      P.w(ve, ("mm", t - 1, "seltr", None))
                      P.w(ve, ("mm", t - 1, "pair", (0, 0)))
                  for j in range(2):
                      sv = lambda key, ins_: P.e(key, "sVX", ins_)
                      sv(("tk", t, j, 0), ve.max(mx[:, j, :], lgt[:, j, 0:8]))
                      P.w(ve, ("tk", t, j, 0))
                      sv(("tk", t, j, 1), ve.max_index(ixu[:, j, :], mx[:, j, :],
                                                       lgt[:, j, 0:8]))
                      P.w(ve, ("tk", t, j, 1))
                      sv(("tk", t, j, 2), ve.tensor_copy(iif[:, 0:1], ixu[:, j, 0:1]))
                      P.w(ve, ("tk", t, j, 2))
                      ve.tensor_scalar(sel1f[:, j, 0:8], c_iota[:, :], iif[:, 0:1], None,
                                       ALU.is_equal, ALU.bypass)
                      sv(("tk", t, j, 3),
                         ve.scalar_tensor_tensor(l2t[:, j, :], sel1f[:, j, 0:8], -1e30,
                                                 lgt[:, j, 0:8], ALU.mult, ALU.add))
                      P.w(ve, ("tk", t, j, 3))
                      sv(("tk", t, j, 4), ve.max(mx2[:, j, :], l2t[:, j, :]))
                      P.w(ve, ("tk", t, j, 4))
                      sv(("tk", t, j, 5), ve.max_index(ix2u[:, j, :], mx2[:, j, :],
                                                       l2t[:, j, :]))
                      P.w(ve, ("tk", t, j, 5))
                      sv(("tk", t, j, 6), ve.tensor_copy(iif[:, 1:2], ix2u[:, j, 0:1]))
                      P.w(ve, ("tk", t, j, 6))
                      ve.tensor_scalar(sel2f[:, j, 0:8], c_iota[:, :], iif[:, 1:2], None,
                                       ALU.is_equal, ALU.bypass)
                      ins = ve.tensor_tensor(sttm[:, j, 0:1], mx[:, j, 0:1],
                                             mx2[:, j, 0:1], ALU.subtract)
                  P.e(("topk", t), "sVE", ins)

              topk(0)
              for t in range(NT):
                  for it in range(2):
                      P.w(ve, ("heev", t, it))
                      P.w(ve, ("selbcev", t))
                      ve.tensor_tensor(hm1[:, :, :], hEb[:, :, :], selbc1[:, :, :], ALU.mult)
                      ins = ve.tensor_tensor(hEb[:, :, :], hEb[:, :, :], selbc2[:, :, :],
                                             ALU.mult)
                      P.e(("mask", t, it, 0), "sVE", ins, 1)
                      P.e(("mask", t, it, 1), "sVE", ins, 0)
                      P.w(ve, ("xmv", t, it))
                      ins = None
                      for path in range(3):
                          P.w(ve, ("rev", t, f"b{it}", (3, path)))
                          xmx = xm if path < 2 else xms
                          ins = ve.tensor_tensor(spq[:, path, :, :], spq[:, path, :, :],
                                                 xmx[:, :, :], ALU.mult)
                      P.e(("qrdy", t, it), "sVE", ins)
                      for k in range(2):
                          P.w(ve, ("mm", t, f"hlbc{it}", k))
                          pb = probin[item_idx[(t, f"hlbc{it}", k)] % 8]
                          dst = hlb if k == 0 else hlDTb
                          ins = ve.tensor_copy(dst[:, :], pb[:, 0, :])
                          P.e(("rev", t, f"hlbc{it}", k), "sVE", ins)
                      if it == 0:
                          P.w(ve, ("wsg", t))
                          ve.tensor_scalar(wsrc2[0:1, :], wsrc[0:1, :], -1.0, 1.0,
                                           ALU.mult, ALU.add)
                          ve.tensor_scalar(wsrc2[0:1, :], wsrc2[0:1, :], sw1c[0:1, 0:1],
                                           None, ALU.mult, ALU.bypass)
                          ins = ve.tensor_scalar(wsrc[0:1, :], wsrc[0:1, :], sw1c[0:1, 0:1],
                                                 None, ALU.mult, ALU.bypass)
                          P.e(("wrow", t), "sVE", ins)
                          for k in range(2):
                              P.w(ve, ("mm", t, "wbc", k))
                              pb = probin[item_idx[(t, "wbc", k)] % 8]
                              dst = wAb if k == 0 else wBb
                              ins = ve.tensor_copy(dst[:, :], pb[:, 0, :])
                              P.e(("rev", t, "wbc", k), "sVE", ins)
                      if it == 1 and t + 1 < NT:
                          topk(t + 1)
                      for cp in range(4):
                          cs = slice(2 * cp, 2 * cp + 2)
                          db = ((2 * t + it) * 4 + cp) % 2
                          P.w(ve, ("sgev", t, it, cp, "a"))
                          P.w(ve, ("sgev", t, it, cp, "g"))
                          P.w(ve, ("qrdy", t, it))
                          wA2, wB2 = bcast2(wAb[:, :], 2), bcast2(wBb[:, :], 2)
                          hl2, hlDT2 = bcast2(hlb[:, :], 2), bcast2(hlDTb[:, :], 2)
                          vnv = [spq[:, p, cs, :] for p in range(3)]
                          if it == 1:
                              for p in range(2):
                                  tgt = sgA[:, db, 2 * p:2 * p + 2, :]
                                  ve.tensor_tensor(tgt, tgt, vT[:, cs, :], ALU.mult)
                                  ve.tensor_tensor(vnv[p], tgt, vnv[p], ALU.subtract)
                              tgt = sgA[:, db, 4:6, :]
                              ve.scalar_tensor_tensor(tgt, tgt, sw0[:, 0:1], vT[:, cs, :],
                                                      ALU.mult, ALU.mult)
                              ve.tensor_tensor(vnv[2], tgt, vnv[2], ALU.subtract)
                          for p in range(3):
                              tgt = sgG[:, db, 2 * p:2 * p + 2, :]
                              ve.tensor_tensor(tgt, tgt, vnv[p], ALU.mult)
                          u1 = sgG[:, db, 0:2, :]
                          u2 = sgG[:, db, 2:4, :]
                          us = sgG[:, db, 4:6, :]
                          ve.tensor_tensor(tmpA[:, :, :], wA2, u1, ALU.mult)
                          ve.tensor_tensor(tmpB[:, :, :], wB2, u2, ALU.mult)
                          ve.tensor_tensor(tmpA[:, :, :], tmpA[:, :, :], tmpB[:, :, :],
                                           ALU.add)
                          ve.tensor_tensor(tmpA[:, :, :], tmpA[:, :, :], us, ALU.add)
                          ve.tensor_tensor(tmpA[:, :, :], hlDT2, tmpA[:, :, :], ALU.mult)
                          xs = x1[:, t % 2, cs, :] if it == 0 else intT[:, cs, :]
                          ve.tensor_tensor(intT[:, cs, :], xs, tmpA[:, :, :],
                                           ALU.subtract if it == 0 else ALU.add)
                          ve.tensor_tensor(tmpB[:, :, :], wA2, vnv[0], ALU.mult)
                          ve.tensor_tensor(tmpA[:, :, :], wB2, vnv[1], ALU.mult)
                          ve.tensor_tensor(tmpB[:, :, :], tmpB[:, :, :], tmpA[:, :, :],
                                           ALU.add)
                          ve.tensor_tensor(tmpB[:, :, :], tmpB[:, :, :], vnv[2], ALU.add)
                          if it == 1:
                              ve.tensor_tensor(tmpB[:, :, :], tmpB[:, :, :], vT[:, cs, :],
                                               ALU.subtract)
                              ve.tensor_tensor(tmpB[:, :, :], hl2, tmpB[:, :, :], ALU.mult)
                              ins = ve.tensor_tensor(vT[:, cs, :], vT[:, cs, :],
                                                     tmpB[:, :, :], ALU.add)
                          else:
                              ve.tensor_tensor(tmpB[:, :, :], hl2, tmpB[:, :, :], ALU.mult)
                              ins = ve.tensor_scalar(vT[:, cs, :], tmpB[:, :, :], -1.0,
                                                     None, ALU.mult, ALU.bypass)
                          P.e(("ew", t, it, cp, "a"), "sVE", ins, 1)
                          P.e(("ew", t, it, cp, "g"), "sVE", ins, 0)
                          if it == 0:
                              ve.tensor_copy(intTb[:, t % 2, cs, :], intT[:, cs, :])
                              ins = ve.tensor_copy(vTb[:, cs, :], vT[:, cs, :])
                              P.e(("cast1p", t, cp), "sVE", ins)
                      P.e(("el", t, it), "sVE", ins, 0)
                      if it == 0:
                          if t + 1 < NT:
                              cast0(t + 1)
                      else:
                          ins = ve.tensor_copy(intTb[:, t % 2], intT[:, :, :])
                          P.e(("icast", t), "sVE", ins)

        null = {k: _Null() for k in ("sync", "gpsimd", "tensor", "scalar", "vector")}
        run_streams(null, "all")
        P.start_emit()

        def mk(name):
            def body(engine):
                eng = dict(null)
                eng[name] = engine
                run_streams(eng, name)
            return body

        with nc.Block("F1", no_gpsimd_drain=True) as blk:
            blk.sync(mk("sync"))
            blk.gpsimd(mk("gpsimd"))
            blk.tensor(mk("tensor"))
            blk.scalar(mk("scalar"))
            blk.vector(mk("vector"))

            @blk.gpsimd
            def _(gp):
                gp.wait_ge(E["sS0"].h, P.ev[("ist", NT - 1)][1])


def emit_F2(nc, E):
    sems = {k: E[k] for k in ("sLD", "sWL", "sST", "sMM", "sSC", "sVE", "sAux",
                              "sP2", "sP3", "sP4", "sP5", "sS1")}
    sST = E["sST"]
    c_oc, c_or32 = E["c_oc"], E["c_or32"]
    d_x1Ts, d_intTs, d_outT = E["d_x1Ts"], E["d_intTs"], E["d_outT"]

    NT = NQ // F2T
    T = F2T

    with contextlib.ExitStack() as es:
        ec = es.enter_context
        sb = lambda nm, shp, dt: ec(nc.sbuf_tensor(nm, shp, dt))
        w_f1 = sb("f2_wf1", [128, 8, 2 * D], BF16)
        w_f2 = sb("f2_wf2", [128, 16, D], BF16)
        b_f1 = sb("f2_bf1", [128, 16], F32)
        b_f2 = sb("f2_bf2", [128, 8], F32)
        b_iw = sb("f2_biw", [128, 8], F32)
        ib = sb("f2_ib", [128, 2, 8, T], BF16)
        x1f = sb("f2_x1", [128, 2, 8, T], F32)
        rh = sb("f2_rh", [128, 16, T], BF16)
        reft = sb("f2_reft", [128, 2, T], F32)
        pre2 = sb("f2_pre2", [128, 8, T], F32)
        p2b = sb("f2_p2b", [128, 8, T], BF16)
        p2sq = sb("f2_p2sq", [128, 8, T], BF16)
        st2 = sb("f2_st2", [128, 4, T], F32)
        tmp2 = sb("f2_tmp2", [128, T], F32)
        x2 = sb("f2_x2", [128, 8, T], F32)
        probin = [ec(nc.psum_tensor(f"f2_pr{i}", [128, T], F32)) for i in range(4)]

        P = Plan(sems)

        def tile_items(t):
            return ([("rh", m) for m in range(16)] + [("rf", m) for m in range(8)]
                    + [("s1", None), ("s2", None), ("nmb", None), ("rsb", None)])

        all_items = [(t, tag, arg) for t in range(NT) for tag, arg in tile_items(t)]
        item_idx = {k: i for i, k in enumerate(all_items)}

        def run_streams(eng, part="all"):
            sy, gp, te, sc, ve = (eng[k] for k in ("sync", "gpsimd", "tensor",
                                                   "scalar", "vector"))

            if part in ("all", "sync"):
                s16 = lambda key, dst, src_: P.e(key, "sAux", sy.dma_start(dst, src_), 16)
                s16(("b2l", 0), b_f1[:, :], E["d_fb1"][:, :])
                s16(("b2l", 1), b_f2[:, :], E["d_fb2"][:, :])
                s16(("b2l", 2), b_iw[:, :], E["d_iw"][:, :])
                for t in range(NT):
                    if t >= 2:
                        P.w(sy, ("fin2", t - 2))
                    ap = d_intTs.rearrange("(c p) t -> p c t", p=128)[:, :, t * T:(t + 1) * T]
                    P.e(("ibld", t), f"sP{2 + t % 2}", sy.dma_start(ib[:, t % 2], ap), 16)
                    ap = d_x1Ts.rearrange("(c p) t -> p c t", p=128)[:, :, t * T:(t + 1) * T]
                    P.e(("x1ld2", t), f"sP{4 + t % 2}", sy.dma_start(x1f[:, t % 2], ap), 16)

            if part in ("all", "gpsimd"):
                rr = lambda d: d.rearrange("(c p) n -> p c n", p=128)
                P.e(("w2l", 0), "sWL", gp.dma_start(w_f1[:, :, :], rr(E["d_fw1"])), 16)
                P.e(("w2l", 1), "sWL", gp.dma_start(w_f2[:, :, :], rr(E["d_fw2"])), 16)
                for t in range(NT):
                    P.w(gp, ("x2dn", t))
                    ap = d_outT.rearrange("(c p) t -> p c t", p=128)[:, :, t * T:(t + 1) * T]
                    P.e(("ost", t), "sS1", gp.dma_start(ap, x2[:, :, :]), 16)

            def robin(t, tag, arg):
                idx = item_idx[(t, tag, arg)]
                if idx >= 4:
                    P.w(te, ("rev2",) + all_items[idx - 4])
                return probin[idx % 4]

            if part in ("all", "tensor"):
                P.w(te, ("w2l", 1))
                P.w(te, ("b2l", 2))
                for t in range(NT):
                    P.w(te, ("ibld", t))
                    for m in range(16):
                        pb = robin(t, "rh", m)
                        last = None
                        for c in range(8):
                            last = te.matmul(pb[:, :], w_f1[:, c, 128 * m:128 * m + 128],
                                             ib[:, t % 2, c, :], start=(c == 0), stop=(c == 7))
                        P.e(("mm2", t, "rh", m), "sMM", last)
                    for m in range(8):
                        pb = robin(t, "rf", m)
                        P.w(te, ("rhev", t, 15) if m == 0 else ("rhev", t, 15))
                        last = None
                        for hc in range(16):
                            last = te.matmul(pb[:, :], w_f2[:, hc, 128 * m:128 * m + 128],
                                             rh[:, hc, :], start=(hc == 0), stop=(hc == 15))
                        P.e(("mm2", t, "rf", m), "sMM", last)
                    pb = robin(t, "s1", None)
                    P.w(te, ("castb", t))
                    last = None
                    for c in range(8):
                        last = te.matmul(pb[:, :], c_oc[:, :], p2b[:, c, :],
                                         start=(c == 0), stop=(c == 7))
                    P.e(("mm2", t, "s1", None), "sMM", last)
                    pb = robin(t, "s2", None)
                    P.w(te, ("sq2", t))
                    last = None
                    for c in range(8):
                        last = te.matmul(pb[:, :], c_oc[:, :], p2sq[:, c, :],
                                         start=(c == 0), stop=(c == 7))
                    P.e(("mm2", t, "s2", None), "sMM", last)
                    pb = robin(t, "nmb", None)
                    P.w(te, ("st2v", t))
                    ins = te.matmul(pb[:, :], c_or32[:, :], st2[:, 3, :], start=True, stop=True)
                    P.e(("mm2", t, "nmb", None), "sMM", ins)
                    pb = robin(t, "rsb", None)
                    P.w(te, ("rstd2", t))
                    ins = te.matmul(pb[:, :], c_or32[:, :], st2[:, 2, :], start=True, stop=True)
                    P.e(("mm2", t, "rsb", None), "sMM", ins)

            if part in ("all", "scalar"):
                P.w(sc, ("b2l", 2))
                for t in range(NT):
                    for m in range(16):
                        P.w(sc, ("mm2", t, "rh", m))
                        pb = probin[item_idx[(t, "rh", m)] % 4]
                        ins = sc.activation(rh[:, m, :], pb[:, :], AF.Gelu,
                                            bias=b_f1[:, m:m + 1])
                        P.e(("rev2", t, "rh", m), "sSC", ins)
                        if m == 15:
                            P.e(("rhev", t, 15), "sSC", ins, 0)
                    for m in range(8):
                        P.w(sc, ("mm2", t, "rf", m))
                        if m > 1:
                            P.w(sc, ("pr2", t, m - 2))
                        elif t > 0:
                            P.w(sc, ("pr2", t - 1, 6 + m))
                        pb = probin[item_idx[(t, "rf", m)] % 4]
                        ins = sc.activation(reft[:, m % 2, :], pb[:, :], AF.Identity,
                                            bias=b_f2[:, m:m + 1])
                        P.e(("rev2", t, "rf", m), "sSC", ins)
                        P.e(("rfev", t, m), "sSC", ins, 0)
                    P.w(sc, ("cast2", t))
                    ins = sc.activation(p2sq[:, :, :], pre2[:, :, :], AF.Square)
                    P.e(("sq2", t), "sSC", ins)
                    P.w(sc, ("st2v", t))
                    ins = sc.activation(tmp2[0:1, :], st2[0:1, 1, :], AF.Ln, bias=LN_EPS)
                    P.e(("ln2", t), "sSC", ins)
                    ins = sc.activation(st2[0:1, 2, :], tmp2[0:1, :], AF.Exp, scale=-0.5)
                    P.e(("rstd2", t), "sSC", ins)

            if part in ("all", "vector"):
                ins = ve.memset(st2[:, :, :], 0.0)
                P.e(("vm2",), "sVE", ins)
                for t in range(NT):
                    P.w(ve, ("x1ld2", t))
                    ins = None
                    for m in range(8):
                        P.w(ve, ("rfev", t, m))
                        ve.scalar_tensor_tensor(pre2[:, m, :], reft[:, m % 2, :],
                                                b_iw[:, m:m + 1],
                                                x1f[:, t % 2, m, :], ALU.mult, ALU.add)
                        ins = ve.tensor_tensor(pre2[:, m, :], pre2[:, m, :],
                                               x1f[:, t % 2, m, :], ALU.add)
                        P.e(("pr2", t, m), "sVE", ins)
                    P.e(("cast2", t), "sVE", ins, 0)
                    ins = ve.tensor_copy(p2b[:, :, :], pre2[:, :, :])
                    P.e(("castb", t), "sVE", ins)
                    P.w(ve, ("mm2", t, "s1", None))
                    P.w(ve, ("mm2", t, "s2", None))
                    pb1 = probin[item_idx[(t, "s1", None)] % 4]
                    pb2 = probin[item_idx[(t, "s2", None)] % 4]
                    ve.tensor_scalar(st2[0:1, 3, :], pb1[0:1, :], -1.0 / D, None,
                                     ALU.mult, ALU.bypass)
                    ve.tensor_scalar(st2[0:1, 1, :], pb1[0:1, :], 1.0 / D, None,
                                     ALU.mult, ALU.bypass)
                    ve.tensor_tensor(tmp2[0:1, :], st2[0:1, 1, :], st2[0:1, 1, :], ALU.mult)
                    ve.tensor_scalar(st2[0:1, 1, :], pb2[0:1, :], 1.0 / D, None,
                                     ALU.mult, ALU.bypass)
                    ins = ve.tensor_tensor(st2[0:1, 1, :], st2[0:1, 1, :], tmp2[0:1, :],
                                           ALU.subtract)
                    P.e(("st2v", t), "sVE", ins)
                    P.e(("rev2", t, "s1", None), "sVE", ins, 0)
                    P.e(("rev2", t, "s2", None), "sVE", ins, 0)
                    P.w(ve, ("mm2", t, "nmb", None))
                    P.w(ve, ("mm2", t, "rsb", None))
                    if t >= 1:
                        P.w(ve, ("ost", t - 1))
                    pbn = probin[item_idx[(t, "nmb", None)] % 4]
                    pbr = probin[item_idx[(t, "rsb", None)] % 4]
                    ins = None
                    for m in range(8):
                        ve.tensor_tensor(tmp2[:, :], pre2[:, m, :], pbn[:, :], ALU.add)
                        ins = ve.tensor_tensor(x2[:, m, :], tmp2[:, :], pbr[:, :], ALU.mult)
                    P.e(("x2dn", t), "sVE", ins)
                    P.e(("rev2", t, "nmb", None), "sVE", ins, 0)
                    P.e(("rev2", t, "rsb", None), "sVE", ins, 0)
                    P.e(("fin2", t), "sVE", ins, 0)

        null = {k: _Null() for k in ("sync", "gpsimd", "tensor", "scalar", "vector")}
        run_streams(null, "all")
        P.start_emit()

        def mk(name):
            def body(engine):
                eng = dict(null)
                eng[name] = engine
                run_streams(eng, name)
            return body

        with nc.Block("F2", no_gpsimd_drain=True) as blk:
            blk.sync(mk("sync"))
            blk.gpsimd(mk("gpsimd"))
            blk.tensor(mk("tensor"))
            blk.scalar(mk("scalar"))
            blk.vector(mk("vector"))

            @blk.gpsimd
            def _(gp):
                gp.wait_ge(E["sS1"].h, P.ev[("ost", NT - 1)][1])



def build_program(upto="F2", dbg=False, phases=None):
    nc = bass.Bass("TRN2", target_bir_lowering=False)
    scratch_kind = "ExternalOutput" if dbg else "Internal"
    order = ["A1", "A2", "F1", "F2"]
    if phases is None:
        phases = order[:order.index(upto) + 1]

    E = {}
    E["d_xT"] = nc.dram_tensor("xT", [D, NSEQ], F32, kind="ExternalInput").ap()
    E["d_xqT"] = nc.dram_tensor("xqT", [D, NQ], F32, kind="ExternalInput").ap()
    E["d_wqkv"] = nc.dram_tensor("wqkv", [D, 3 * D], F32, kind="ExternalInput").ap()
    for nm, shp, dt in [
        ("identf", [128, 128], F32), ("onescol", [128, 128], F32),
        ("onesrow", [128, 128], F32), ("onesrowdt", [128, 128], F32),
        ("pairpat", [4, 128, 128], F32), ("iota8", [128, 8], F32),
        ("cmask", [8, 128, QB], F32),
        ("rw1", [D, 256], F32), ("rb1", [128, 2], F32),
        ("rw2", [256, 128], F32), ("rb2", [128, 1], F32),
        ("hw1", [D, 256], F32), ("hb1", [128, 2], F32),
        ("hw2", [256, 128], F32), ("hb2", [1, 1], F32),
        ("ew1", [2 * D, 512], F32), ("eb1", [128, 4], F32),
        ("ew2", [512, 3 * D], F32), ("eb2", [128, 3 * D], F32),
        ("sw1", [2 * D, 128], F32), ("sb1", [128, 1], F32),
        ("sw2", [128, 3 * D], F32), ("sb2", [128, 24], F32),
        ("shw", [128, 1], F32),
        ("fw1", [D, 2 * D], F32), ("fb1", [128, 16], F32),
        ("fw2", [2 * D, D], F32), ("fb2", [128, 8], F32),
        ("mun", [128, 8], F32), ("iw", [128, 8], F32),
        ("misc", [128, 4], F32),
    ]:
        E["d_" + nm] = nc.dram_tensor(nm, shp, dt, kind="ExternalInput").ap()

    kind_of = lambda prod: scratch_kind if prod in phases else "ExternalInput"
    E["d_kTs"] = nc.dram_tensor("kTs", [D, NSEQ], BF16, kind=kind_of("A1")).ap()
    E["d_Vs"] = nc.dram_tensor("Vs", [NSEQ, D], BF16, kind=kind_of("A1")).ap()
    E["d_qTs"] = nc.dram_tensor("qTs", [D, NQ], BF16, kind=kind_of("A1")).ap()
    E["d_x1Ts"] = nc.dram_tensor("x1Ts", [D, NQ], F32, kind=kind_of("A2")).ap()
    E["d_intTs"] = nc.dram_tensor("intTs", [D, NQ], BF16, kind=kind_of("F1")).ap()
    E["d_outT"] = nc.dram_tensor("outT", [D, NQ], F32, kind="ExternalOutput").ap()

    with contextlib.ExitStack() as es:
        ec = es.enter_context
        for nm in ("sLD", "sWL", "sST", "sMM", "sSC", "sVE", "sVX", "sAux",
                   "sP0", "sP1", "sP2", "sP3", "sP4", "sP5", "sP6", "sP7",
                   "sS0", "sS1"):
            E[nm] = Sem(ec(nc.semaphore(nm)))

        E["c_id"] = ec(nc.sbuf_tensor("c_id", [128, 128], F32))
        E["c_oc"] = ec(nc.sbuf_tensor("c_oc", [128, 128], BF16))
        E["c_or32"] = ec(nc.sbuf_tensor("c_or32", [128, 128], F32))
        E["c_ordt32"] = ec(nc.sbuf_tensor("c_ordt32", [128, 128], F32))
        E["c_pair"] = ec(nc.sbuf_tensor("c_pair", [128, 4, 128], BF16))
        E["c_iota"] = ec(nc.sbuf_tensor("c_iota", [128, 8], F32))
        E["c_misc"] = ec(nc.sbuf_tensor("c_misc", [128, 4], F32))
        nc.const_aps.register(F32, LN_EPS, E["c_misc"][:, 0:1])
        nc.const_aps.register(F32, 1.0, E["c_misc"][:, 1:2])
        nc.const_aps.register(F32, 0.0, E["c_misc"][:, 2:3])

        sLD, sWL = E["sLD"], E["sWL"]
        with nc.Block("INIT", no_gpsimd_drain=True) as blk:
            @blk.sync
            def _(sync):
                sAux = E["sAux"]
                sAux.inc(sync.dma_start(E["c_id"][:, :], E["d_identf"][:, :]), 16)
                sAux.inc(sync.dma_start(E["c_iota"][:, :], E["d_iota8"][:, :]), 16)
                sAux.inc(sync.dma_start(E["c_misc"][:, :], E["d_misc"][:, :]), 16)
                sAux.inc(sync.dma_start(E["c_or32"][:, :], E["d_onesrow"][:, :]), 16)
                sAux.inc(sync.dma_start(E["c_ordt32"][:, :], E["d_onesrowdt"][:, :]), 16)

            @blk.gpsimd
            def _(g):
                sWL.inc(g.dma_start(E["c_oc"][:, :], E["d_onescol"][:, :]), 16)
                sWL.inc(g.dma_start(E["c_pair"][:, :, :],
                                    E["d_pairpat"].rearrange("k p f -> p k f")), 16)

        if "A1" in phases:
            with nc.named_scope("A1"):
                emit_A1(nc, E)
        if "A2" in phases:
            with nc.named_scope("A2"):
                emit_A2(nc, E)
        if "F1" in phases:
            with nc.named_scope("F1"):
                emit_F1(nc, E)
        if "F2" in phases:
            with nc.named_scope("F2"):
                emit_F2(nc, E)

    return nc



def stage_inputs(inputs):
    x = np.asarray(inputs["L0_attn_x"])
    wqkv = np.ascontiguousarray(inputs["L0_attn_W_qkv"], dtype=np.float32)

    ident = np.eye(128, dtype=np.float32)
    onescol = np.zeros((128, 128), np.float32); onescol[:, 0] = 1.0
    onesrow = np.zeros((128, 128), np.float32); onesrow[0, :] = 1.0
    onesrowdt = np.zeros((128, 128), np.float32); onesrowdt[0, :] = DT_STEP
    pairpat = np.zeros((4, 128, 128), np.float32)
    for P in range(4):
        pairpat[P, 2 * P, 0:64] = 1.0
        pairpat[P, 2 * P + 1, 64:128] = 1.0
    iota8 = np.tile(np.arange(8, dtype=np.float32)[None, :], (128, 1))

    p_idx = np.arange(128)[:, None]
    j_idx = np.arange(QB)[None, :]
    cmask = {}
    for h in (0, 1):
        m = np.zeros((8, 128, QB), np.float32)
        for s in range(8):
            if h == 0:
                if s < 4:
                    m[s] = (128 * s + p_idx <= j_idx)
            else:
                if s < 4:
                    m[s] = 1.0
                else:
                    m[s] = (128 * (s - 4) + p_idx <= j_idx)
        cmask[h] = m

    f32 = lambda a: np.ascontiguousarray(np.asarray(a), dtype=np.float32)
    rw1 = f32(inputs["L0_ffn_router_w1"])
    rb1 = f32(inputs["L0_ffn_router_b1"]).reshape(2, 128).T.copy()
    rw2 = np.zeros((256, 128), np.float32); rw2[:, :8] = f32(inputs["L0_ffn_router_w2"])
    rb2 = np.zeros((128, 1), np.float32); rb2[:8, 0] = f32(inputs["L0_ffn_router_b2"])
    hw1 = f32(inputs["L0_ffn_halt_w1"])
    hb1 = f32(inputs["L0_ffn_halt_b1"]).reshape(2, 128).T.copy()
    hw2 = np.zeros((256, 128), np.float32); hw2[:, 0:1] = f32(inputs["L0_ffn_halt_w2"])
    hb2 = f32(inputs["L0_ffn_halt_b2"]).reshape(1, 1)
    ew1 = np.ascontiguousarray(
        np.transpose(f32(inputs["L0_ffn_expert_w1"]), (1, 0, 2)).reshape(2 * D, 512))
    eb1 = f32(inputs["L0_ffn_expert_b1"]).reshape(512).reshape(4, 128).T.copy()
    ew2 = f32(inputs["L0_ffn_expert_w2"]).reshape(512, 3 * D)
    eb2 = np.zeros((128, 3 * D), np.float32); eb2[:8] = f32(inputs["L0_ffn_expert_b2"])
    sw1 = np.zeros((2 * D, 128), np.float32); sw1[:, :HID] = f32(inputs["L0_ffn_shared_w1"])
    sb1 = np.zeros((128, 1), np.float32); sb1[:HID, 0] = f32(inputs["L0_ffn_shared_b1"])
    sw2 = np.zeros((128, 3 * D), np.float32); sw2[:HID] = f32(inputs["L0_ffn_shared_w2"])
    sw2[HID] = f32(inputs["L0_ffn_shared_b2"])
    sb2 = f32(inputs["L0_ffn_shared_b2"]).reshape(24, 128).T.copy()
    shw = np.full((128, 1), np.float32(inputs["L0_ffn_shared_weight"]), np.float32)
    fw1 = f32(inputs["L0_ffn_refine_w1"])
    fb1 = f32(inputs["L0_ffn_refine_b1"]).reshape(16, 128).T.copy()
    fw2 = f32(inputs["L0_ffn_refine_w2"])
    fb2 = f32(inputs["L0_ffn_refine_b2"]).reshape(8, 128).T.copy()
    mun = (-f32(inputs["L0_ffn_mu"])).reshape(8, 128).T.copy()
    iw = f32(inputs["L0_ffn_integration_weight"]).reshape(8, 128).T.copy()

    misc = np.tile(np.array([[LN_EPS, 1.0, 0.0, 0.0]], np.float32), (128, 1))
    shared = dict(misc=misc, identf=ident, onescol=onescol, onesrow=onesrow, onesrowdt=onesrowdt,
                  pairpat=pairpat, iota8=iota8, wqkv=wqkv,
                  rw1=rw1, rb1=rb1, rw2=rw2, rb2=rb2, hw1=hw1, hb1=hb1, hw2=hw2,
                  hb2=hb2, ew1=ew1, eb1=eb1, ew2=ew2, eb2=eb2, sw1=sw1, sb1=sb1,
                  sw2=sw2, sb2=sb2, shw=shw, fw1=fw1, fb1=fb1, fw2=fw2, fb2=fb2,
                  mun=mun, iw=iw)

    xT = [np.ascontiguousarray(x[b].T, dtype=np.float32) for b in range(NB)]
    in_maps = []
    for c in range(NCORE):
        b, h = c // 2, c % 2
        cols = np.concatenate([np.arange(QB * (2 * u + h), QB * (2 * u + h) + QB)
                               for u in range(NBLK)])
        m = dict(shared)
        m["xT"] = xT[b]
        m["xqT"] = np.ascontiguousarray(xT[b][:, cols])
        m["cmask"] = cmask[h]
        in_maps.append(m)
    return in_maps


def assemble_output(results):
    y = np.empty((NB, NSEQ, D), np.float32)
    for c in range(NCORE):
        b, h = c // 2, c % 2
        outT = results[c]["outT"]
        for u in range(NBLK):
            y[b, QB * (2 * u + h):QB * (2 * u + h) + QB, :] = \
                outT[:, QB * u:QB * u + QB].T
    return y


def kernel(**inputs):
    nc = build_program(upto="F2")
    in_maps = stage_inputs(inputs)
    res = run_bass_kernel_spmd(nc, in_maps, core_ids=list(range(NCORE)))
    return assemble_output(res.results)
